# version 02
# baseline (speedup 1.0000x reference)
import math
import numpy as np
import ml_dtypes
import concourse.bass as bass
import concourse.mybir as mybir
from concourse.bass_utils import run_bass_kernel_spmd

F32 = mybir.dt.float32
BF16 = mybir.dt.bfloat16
AF = mybir.ActivationFunctionType
ALU = mybir.AluOpType
AX = mybir.AxisListType
ds = bass.ds

D = 2048
SEQ = 4096
NB = 2
FF = 5632
P = 128
NCORE = 8
TOK = NB * SEQ
OWN = TOK // NCORE
NDC = D // P
NFC = FF // P
NG = SEQ // 512
LAM_INIT = float(0.8 - 0.6 * math.exp(np.float32(-0.3 * 12)))
EPS_LN = 1e-5
EPS_RMS = 1e-6
VS = 132

bf = ml_dtypes.bfloat16
KB = 1024


def _prep(inp):
    x = np.ascontiguousarray(np.asarray(inp["L0_attn_x"], np.float32).reshape(TOK, D))
    xT = np.ascontiguousarray(x.T)
    Wqkv = np.asarray(inp["L0_attn_W_qkv"], np.float32)
    Wo = np.asarray(inp["L0_attn_W_o"], np.float32)
    woT = np.ascontiguousarray(Wo.T)
    woT_cb = np.ascontiguousarray(woT.reshape(NDC, P, 4, 512).transpose(2, 0, 1, 3))
    wvf = np.asarray(inp["L0_ffn_wv"], np.float32)
    wvf_st = np.ascontiguousarray(wvf.reshape(NDC, P, NDC, P).transpose(2, 0, 1, 3))
    wof = np.asarray(inp["L0_ffn_wo"], np.float32)
    wof_cb = np.ascontiguousarray(wof.reshape(NDC, P, 4, 512).transpose(2, 0, 1, 3))
    gate = np.asarray(inp["L0_ffn_w_gate"], np.float32)
    gate_st = np.ascontiguousarray(gate.reshape(NDC, P, NFC, P).transpose(2, 0, 1, 3))
    up = np.asarray(inp["L0_ffn_w_up"], np.float32)
    up_st = np.ascontiguousarray(up.reshape(NDC, P, NFC, P).transpose(2, 0, 1, 3))
    down = np.asarray(inp["L0_ffn_w_down"], np.float32)
    down_cb = np.ascontiguousarray(down.reshape(NFC, P, 8, 256).transpose(2, 0, 1, 3))

    tile = lambda v: np.ascontiguousarray(
        np.tile(np.asarray(v, np.float32).reshape(1, -1), (P, 1)))
    shared = dict(
        xT=xT, woT_cb=woT_cb, wvf_st=wvf_st, wof_cb=wof_cb,
        gate_st=gate_st, up_st=up_st, down_cb=down_cb,
        lq1=tile(inp["L0_attn_lambda_q1"]), lk1=tile(inp["L0_attn_lambda_k1"]),
        lq2=tile(inp["L0_attn_lambda_q2"]), lk2=tile(inp["L0_attn_lambda_k2"]),
        n1w=np.ascontiguousarray(
            np.asarray(inp["L0_ffn_norm1_w"], np.float32).reshape(NDC, P).T),
        n2w=np.ascontiguousarray(
            np.asarray(inp["L0_ffn_norm2_w"], np.float32).reshape(NDC, P).T),
        bvv=np.ascontiguousarray(
            np.asarray(inp["L0_ffn_bv"], np.float32).reshape(NDC, P).T),
        wrt=np.ascontiguousarray(tile(inp["L0_ffn_w_router"]).astype(bf)),
        brt=np.ascontiguousarray(
            np.full((P, 1), np.asarray(inp["L0_ffn_b_router"], np.float32), np.float32)),
        tri=np.ascontiguousarray(np.triu(np.ones((P, P), dtype=bf))),
        trin=np.ascontiguousarray(
            np.where(np.tril(np.ones((P, P), bool)), 0.0, -1e30).astype(np.float32)),
        ident=np.ascontiguousarray(np.eye(P, dtype=bf)),
    )
    gng = np.asarray(inp["L0_attn_gn_gamma"], np.float32)
    gnb = np.asarray(inp["L0_attn_gn_beta"], np.float32)
    per_core = []
    for c in range(NCORE):
        h0 = 2 * c
        wq_t = np.stack([np.ascontiguousarray(
            Wqkv[P * (h0 + hl):P * (h0 + hl + 1), :].T).reshape(NDC, P, P)
            for hl in range(2)])
        wk_t = np.stack([np.ascontiguousarray(
            Wqkv[2048 + P * (h0 + hl):2048 + P * (h0 + hl + 1), :].T).reshape(NDC, P, P)
            for hl in range(2)])
        wv_t = np.ascontiguousarray(
            Wqkv[4096 + P * h0:4096 + P * (h0 + 2), :].T).reshape(NDC, P, 256)
        gam = np.stack([np.tile(gng[h0 + hl][None, :], (P, 1)) for hl in range(2)])
        bet = np.stack([np.tile(gnb[h0 + hl][None, :], (P, 1)) for hl in range(2)])
        agidx = np.zeros((P, 64), np.uint32)
        for tt in range(8):
            for r in range(NCORE):
                agidx[:, tt * 8 + r] = c * OWN + r * TOK + tt * P + np.arange(P)
        per_core.append(dict(
            wq_t=np.ascontiguousarray(wq_t), wk_t=np.ascontiguousarray(wk_t),
            wv_t=wv_t, gam=np.ascontiguousarray(gam),
            bet=np.ascontiguousarray(bet),
            agidx=np.ascontiguousarray(agidx),
            x_own=np.ascontiguousarray(x[OWN * c:OWN * (c + 1)])))
    return shared, per_core


class Ctr:
    __slots__ = ("sem", "n")

    def __init__(self, sem):
        self.sem = sem
        self.n = 0


def build(debug=False):
    nc = bass.Bass("TRN2", target_bir_lowering=False)

    DT = {}

    def din(name, shape, dtype=F32):
        DT[name] = nc.dram_tensor(name, list(shape), dtype, kind="ExternalInput").ap()
        return DT[name]

    d_xT = din("xT", [D, TOK])
    d_wq = din("wq_t", [2, NDC, P, P])
    d_wk = din("wk_t", [2, NDC, P, P])
    d_wv = din("wv_t", [NDC, P, 256])
    d_woT = din("woT_cb", [4, NDC, P, 512])
    d_wvf = din("wvf_st", [NDC, NDC, P, P])
    d_wof = din("wof_cb", [4, NDC, P, 512])
    d_gate = din("gate_st", [NFC, NDC, P, P])
    d_up = din("up_st", [NFC, NDC, P, P])
    d_down = din("down_cb", [8, NFC, P, 256])
    d_lq1 = din("lq1", [P, P])
    d_lk1 = din("lk1", [P, P])
    d_lq2 = din("lq2", [P, P])
    d_lk2 = din("lk2", [P, P])
    d_gam = din("gam", [2, P, P])
    d_bet = din("bet", [2, P, P])
    d_n1w = din("n1w", [P, NDC])
    d_n2w = din("n2w", [P, NDC])
    d_bvv = din("bvv", [P, NDC])
    d_wrt = din("wrt", [P, D], BF16)
    d_brt = din("brt", [P, 1])
    d_tri = din("tri", [P, P], BF16)
    d_trin = din("trin", [P, P])
    d_idn = din("ident", [P, P], BF16)
    d_xow = din("x_own", [OWN, D])
    d_agix = din("agidx", [P, 64], mybir.dt.uint32)
    d_y = nc.dram_tensor("y", [OWN, D], F32, kind="ExternalOutput").ap()

    dk = "ExternalOutput" if debug else "Internal"
    d_agin = nc.dram_tensor("ag_in", [TOK, 256], BF16, kind="Internal").ap()
    d_dbga = (nc.dram_tensor("dbg_a", [TOK, 256], BF16, kind="ExternalOutput").ap()
              if debug else None)
    if debug:
        d_dbgq = nc.dram_tensor("dbg_q", [P, SEQ], BF16, kind="ExternalOutput").ap()
        d_dbgk = nc.dram_tensor("dbg_k", [P, SEQ], BF16, kind="ExternalOutput").ap()
        d_dbgv = nc.dram_tensor("dbg_v", [P, 32 * VS], BF16, kind="ExternalOutput").ap()
        d_dbg2 = nc.dram_tensor("dbg_at2", [P, 8 * D], BF16, kind="ExternalOutput").ap()
        d_dbgp = nc.dram_tensor("dbg_ps", [P, 512], F32, kind="ExternalOutput").ap()
        d_dbge = nc.dram_tensor("dbg_ev", [P, 4 * P], F32, kind="ExternalOutput").ap()
        d_dbgs = nc.dram_tensor("dbg_st", [P, 16], F32, kind="ExternalOutput").ap()
        d_dbgp1 = nc.dram_tensor("dbg_p1", [P, 512], BF16, kind="ExternalOutput").ap()
        d_dbgh = nc.dram_tensor("dbg_hat", [P, P], BF16, kind="ExternalOutput").ap()
        d_dbgb = nc.dram_tensor("dbg_b2", [P, 2 * P], F32, kind="ExternalOutput").ap()
        d_dbgo = nc.dram_tensor("dbg_ago", [P, 512], BF16, kind="ExternalOutput").ap()
        d_dbgt = nc.dram_tensor("dbg_atm", [P, D], BF16, kind="ExternalOutput").ap()
        d_dbgf0 = nc.dram_tensor("dbg_afm0", [P, NDC * OWN], BF16,
                                 kind="ExternalOutput").ap()
        d_dbgf = nc.dram_tensor("dbg_afm", [P, NDC * OWN], BF16, kind="ExternalOutput").ap()

    d_agout = nc.dram_tensor("ag_out", [NCORE * TOK, 256], BF16, kind="Internal",
                             addr_space="Shared").ap()
    d_x1sp = nc.dram_tensor("x1sp", [OWN, D], F32, kind=dk).ap()
    d_ln2sp = nc.dram_tensor("ln2sp", [OWN, D], BF16, kind=dk).ap()
    d_hsp = nc.dram_tensor("hsp", [OWN, D], BF16, kind=dk).ap()
    d_selsp = nc.dram_tensor("selsp", [OWN, D], BF16, kind=dk).ap()

    pb = [nc.alloc_psum_tensor(f"pb{i}", [P, 512], F32) for i in range(8)]
    pbh = [b.ap() for b in pb]
    pbb = [b.ap().bitcast(BF16) for b in pb]

    def sb(name, shape, dtype):
        return nc.alloc_sbuf_tensor(name, list(shape), dtype).ap()

    s_idn = sb("s_idn", [P, P], BF16)
    s_tri = sb("s_tri", [P, P], BF16)
    s_lam = sb("s_lam", [P, P], F32)
    s_lt = sb("s_lt", [P, 2 * P], F32)
    s_g2 = sb("s_g2", [P, 2 * P], F32)
    s_b2 = sb("s_b2", [P, 2 * P], F32)
    s_n1 = sb("s_n1", [P, NDC], F32)
    s_n2 = sb("s_n2", [P, NDC], F32)
    s_bv = sb("s_bv", [P, NDC], F32)
    s_wrt = sb("s_wrt", [P, D], BF16)
    s_brt = sb("s_brt", [P, 1], F32)
    s_eln = sb("s_eln", [P, 1], F32)
    s_erm = sb("s_erm", [P, 1], F32)
    s_er1 = sb("s_er1", [1, 1], F32)
    s_oc = sb("s_oc", [P, 1], BF16)
    s_or = sb("s_or", [1, P], BF16)
    s_trin = sb("s_trin", [P, P], F32)
    s_mxa = sb("s_mxa", [P, 12], F32)
    s_mxb = sb("s_mxb", [P, 12], F32)
    s_mc = sb("s_mc", [P, 2], F32)
    s_mcb = sb("s_mcb", [P, 2], BF16)
    s_row = sb("s_row", [1, 3 * 512], F32)
    s_rwb = sb("s_rwb", [1, 1024], BF16)
    s_st1 = sb("s_st1", [P, 16], F32)
    s_st2 = sb("s_st2", [P, 8], F32)
    s_rmask = sb("s_rmask", [P, 8], F32)
    s_rstd2 = sb("s_rstd2", [P, 8], F32)
    s_ev = sb("s_ev", [P, 4 * P], F32)
    s_aix = sb("s_aix", [P, 64], mybir.dt.uint32)
    h_scr = sb("scr4", [P, D], F32)
    h_scr2 = sb("scr4b", [P, D], BF16)

    base = nc.sbuf_base
    avail = nc.sbuf_bytes_remaining - 512
    items = []

    def at(name, shape, dtype, off_kb, stages):
        per_part = int(np.prod(shape[1:])) * mybir.dt.size(dtype)
        off = off_kb * KB
        assert off + per_part <= avail, (name, off, per_part, avail)
        items.append((name, off, per_part, frozenset(stages)))
        return nc.alloc_sbuf_tensor_at(name, list(shape), dtype,
                                       offset=base + off).ap()

    P12 = ("p12",)
    h_xfm = [at(f"xfm{i}", [P, NDC * 512], BF16, 0 + 16 * i, P12) for i in range(2)]
    h_wq = at("wqs", [P, 2 * NDC * P], BF16, 32, P12)
    h_wk = at("wks", [P, 2 * NDC * P], BF16, 40, P12)
    h_wv = at("wvs", [P, NDC * 256], BF16, 48, P12)
    h_q = [at(f"qsb{i}", [P, SEQ], BF16, 56 + 8 * i, P12) for i in range(2)]
    h_k = [at(f"ksb{i}", [P, SEQ], BF16, 72 + 8 * i, P12) for i in range(2)]
    h_v = [at(f"vsb{i}", [P, 32 * VS], BF16, 88 + 9 * i, P12) for i in range(2)]
    h_q1a = at("q1a", [65, SEQ], BF16, 106, P12)
    h_q2a = at("q2a", [65, SEQ], BF16, 114, P12)
    h_k1a = at("k1a", [65, SEQ], BF16, 122, P12)
    h_k2a = at("k2a", [65, SEQ], BF16, 130, P12)
    h_p1t = [at(f"p1t{i}", [P, 512], BF16, 138 + i, P12) for i in range(2)]
    h_p2t = [at(f"p2t{i}", [P, 512], BF16, 140 + i, P12) for i in range(2)]
    h_xsq = [at(f"xsq{i}", [P, 512], BF16, 142 + i, P12) for i in range(2)]
    h_at = [at(f"atile{i}", [P, P], BF16, 144 + i, P12) for i in range(2)]

    h_atm = [at(f"a_tm{i}", [P, D], BF16, 0 + 4 * i, ("s4a",)) for i in range(2)]
    h_afm = at("afm", [P, NDC * OWN], BF16, 8, ("s4a", "s4b"))
    h_wcb = [at(f"wcb{i}", [P, NDC * 512], BF16, 40 + 16 * i, ("s4b", "s4f"))
             for i in range(2)]
    h_at2 = at("attn2", [P, 8 * D], BF16, 72, ("s4b", "s4c"))
    h_xsb = at("x_sb", [P, 8 * D], F32, 0, ("s4c",))
    h_ln2 = at("ln2", [P, 8 * D], BF16, 104, ("s4c", "s4d", "s4e", "s4f"))
    h_h1f = at("h1f", [P, NDC * OWN], BF16, 0, ("s4d", "s4e"))
    h_wstA = [at(f"wstA{i}", [P, NDC * P], BF16, 32 + 4 * i, ("s4e",))
              for i in range(2)]
    h_vfm = at("vfm", [P, NDC * OWN], BF16, 72, ("s4e", "s4f"))
    h_h = at("h_tm", [P, 8 * D], BF16, 0, ("s4f", "s4g"))
    h_wstB = [at(f"wstB{i}", [P, NDC * P], BF16, 0 + 4 * i, ("s4h",))
              for i in range(4)]
    h_gact = at("gact", [P, NFC * 512], BF16, 32, ("s4h",))
    h_dwn = [at(f"dwn{i}", [P, NFC * 256], BF16, 76 + 22 * i, ("s4h",))
             for i in range(2)]
    h_h2f = at("h2f", [P, NDC * 512], BF16, 120, ("s4h",))
    h_l2p = [at(f"l2p{i}", [P, 4 * 256], BF16, 136 + 2 * i, ("s4h",)) for i in range(2)]
    h_hp = [at(f"hp{i}", [P, 4 * 256], BF16, 140 + 2 * i, ("s4h",)) for i in range(2)]
    h_self = at("self", [P, 4 * D], BF16, 32, ("s4i",))
    h_x1f = [at(f"x1f{i}", [P, D], F32, 48 + 8 * i, ("s4i",)) for i in range(2)]

    for i, (n1, o1, s1, st1) in enumerate(items):
        for (n2, o2, s2, st2) in items[i + 1:]:
            if st1 & st2 and not (o1 + s1 <= o2 or o2 + s2 <= o1):
                raise AssertionError(f"sbuf overlap {n1} / {n2} in {sorted(st1 & st2)}")

    import contextlib
    stack = contextlib.ExitStack()
    sems = {k: stack.enter_context(nc.semaphore("sem_" + k))
            for k in ("dg", "dsy", "pe", "dve", "act", "red", "coll", "out",
                      "ga0", "ga1")}

    def emit(en, eng):
        C = {k: Ctr(v) for k, v in sems.items()}

        def dma(q, out, in_, ctr="dg", **kw):
            c = C[ctr]
            if en == q:
                eng.dma_start(out=out, in_=in_, **kw).then_inc(c.sem, 16)
            c.n += 16
            return c.n

        def wait(which, ctr, val=None):
            c = C[ctr]
            v = c.n if val is None else val
            if en == which:
                eng.wait_ge(c.sem, v)

        def inc(which, ins, ctr, k=1):
            c = C[ctr]
            if en == which and ins is not None:
                ins.then_inc(c.sem, k)
            c.n += k
            return c.n

        E = (lambda w: (lambda f: f(eng) if en == w else None))
        T, V, A, G, S = E("t"), E("v"), E("c"), E("g"), E("s")

        dma("s", s_idn[:], d_idn[:], "dsy")
        dma("s", s_tri[:], d_tri[:], "dsy")
        dma("s", s_trin[:], d_trin[:], "dsy")
        dma("s", s_lt[:, 0:P], d_lq1[:], "dsy")
        dma("s", s_lt[:, P:2 * P], d_lk1[:], "dsy")
        dma("s", s_g2[:, 0:P], d_lq2[:], "dsy")
        dma("s", s_g2[:, P:2 * P], d_lk2[:], "dsy")
        dma("s", s_n1[:], d_n1w[:], "dsy")
        dma("s", s_n2[:], d_n2w[:], "dsy")
        dma("s", s_bv[:], d_bvv[:], "dsy")
        dma("s", s_wrt[:], d_wrt[:], "dsy")
        dma("s", s_brt[:], d_brt[:], "dsy")
        dma("s", s_aix[:], d_agix[:], "dsy")
        v_p0 = dma("s", s_b2.rearrange("p (h c) -> p h c", h=2),
               d_gam.rearrange("h p c -> p h c"), "dsy")
        wait("v", "dsy", v_p0)
        V(lambda e: e.memset(s_eln[:], EPS_LN))
        V(lambda e: e.memset(s_erm[:], EPS_RMS))
        V(lambda e: e.memset(s_er1[:], EPS_LN))
        V(lambda e: e.memset(s_oc[:], 1.0))
        V(lambda e: e.memset(s_or[:], 1.0))
        V(lambda e: e.memset(h_k1a[64:65, :], -1.0))
        V(lambda e: e.memset(h_k2a[64:65, :], -1.0))
        V(lambda e: e.memset(h_v[0][:], 1.0))
        V(lambda e: e.memset(h_v[1][:], 1.0))
        V(lambda e: e.tensor_mul(s_lam[:], s_lt[:, 0:P], s_lt[:, P:2 * P]))
        n_d = inc("v", V(lambda e: e.tensor_mul(
            s_lt[:, 0:P], s_g2[:, 0:P], s_g2[:, P:2 * P])), "dve")
        wait("c", "dve", n_d)
        A(lambda e: e.activation(s_lam[:], s_lam[:], AF.Exp))
        n_a = inc("c", A(lambda e: e.activation(s_lt[:, 0:P], s_lt[:, 0:P],
                                                AF.Exp)), "act")
        wait("v", "act", n_a)
        V(lambda e: e.tensor_sub(s_lam[:], s_lam[:], s_lt[:, 0:P]))
        V(lambda e: e.tensor_scalar(s_lam[:], s_lam[:], LAM_INIT, None, ALU.add))
        n_g2 = inc("v", V(lambda e: e.tensor_scalar(
            s_g2[:], s_b2[:], 1.0 - LAM_INIT, None, ALU.mult)), "dve")
        wait("s", "dve", n_g2)
        v_p0b = dma("s", s_b2.rearrange("p (h c) -> p h c", h=2),
                d_bet.rearrange("h p c -> p h c"), "dsy")
        wait("v", "dsy", v_p0b)
        n_const = inc("v", V(lambda e: e.tensor_scalar(
            s_b2[:], s_b2[:], 1.0 - LAM_INIT, None, ALU.mult)), "dve")

        dma("g", h_wq.rearrange("p (h c f) -> p h c f", h=2, c=NDC),
            d_wq.rearrange("h c p f -> p h c f"))
        dma("g", h_wk.rearrange("p (h c f) -> p h c f", h=2, c=NDC),
            d_wk.rearrange("h c p f -> p h c f"))
        v_wqkv = dma("g", h_wv.rearrange("p (c f) -> p c f", c=NDC),
                 d_wv.rearrange("c p f -> p c f"))

        xfm_free = [("dve", 0), ("dve", 0)]
        xsq_free = [("pe", 0), ("pe", 0)]
        pt_free = [[("act", 0)], [("act", 0)]]
        at_free = [("dsy", 0), ("dsy", 0)]
        wait("t", "dve", n_const)
        wait("t", "dsy", v_p0b)

        for b in range(NB):
            for tc in range(8):
                sl = tc % 2
                t0 = b * SEQ + tc * 512
                wait("t", "act", C["act"].n)
                wait("t", "dve", C["dve"].n)
                wait("t", "red", C["red"].n)
                kind, val = xfm_free[sl]
                wait("g", kind, val)
                v_x = dma("g", h_xfm[sl].rearrange("p (c t) -> p c t", c=NDC),
                          d_xT[:, t0:t0 + 512].rearrange("(c p) t -> p c t", p=P))
                wait("v", "dg", v_x)
                wait("t", "dg", v_x)
                for dc in range(NDC):
                    par = dc % 2
                    xs = h_xfm[sl][:, dc * 512:(dc + 1) * 512]
                    kind, val = xsq_free[par]
                    wait("v", kind, val)
                    n_sq = inc("v", V(lambda e, xs=xs, par=par: e.tensor_mul(
                        h_xsq[par][:], xs, xs)), "dve")
                    T(lambda e, xs=xs, dc=dc: e.matmul(
                        pbh[6][0:1, 0:512], s_oc[:], xs,
                        start=(dc == 0), stop=(dc == NDC - 1)))
                    wait("t", "dve", n_sq)
                    ins = T(lambda e, par=par, dc=dc: e.matmul(
                        pbh[7][0:1, 0:512], s_oc[:], h_xsq[par][:],
                        start=(dc == 0), stop=(dc == NDC - 1)))
                    n_pe = inc("t", ins, "pe")
                    xsq_free[par] = ("pe", n_pe)
                n_stats = n_pe
                wait("v", "pe", n_stats)
                V(lambda e: e.tensor_scalar(s_row[0:1, 0:512], pbh[6][0:1, 0:512],
                                            1.0 / D, None, ALU.mult))
                V(lambda e: e.tensor_scalar(s_row[0:1, 512:1024], pbh[7][0:1, 0:512],
                                            1.0 / D, None, ALU.mult))
                V(lambda e: e.tensor_mul(s_row[0:1, 1024:1536], s_row[0:1, 0:512],
                                         s_row[0:1, 0:512]))
                n_var = inc("v", V(lambda e: e.tensor_sub(
                    s_row[0:1, 512:1024], s_row[0:1, 512:1024],
                    s_row[0:1, 1024:1536])), "dve")
                wait("c", "dve", n_var)
                A(lambda e: e.activation(s_row[0:1, 1024:1536], s_row[0:1, 512:1024],
                                         AF.Ln, bias=s_er1[:]))
                n_rs = inc("c", A(lambda e: e.activation(
                    s_row[0:1, 512:1024], s_row[0:1, 1024:1536], AF.Exp,
                    scale=-0.5)), "act")
                wait("v", "act", n_rs)
                V(lambda e: e.tensor_copy(s_rwb[0:1, 0:512], s_row[0:1, 512:1024]))
                V(lambda e: e.tensor_mul(s_row[0:1, 1024:1536], s_row[0:1, 0:512],
                                         s_row[0:1, 512:1024]))
                n_rows = inc("v", V(lambda e: e.tensor_scalar(
                    s_rwb[0:1, 512:1024], s_row[0:1, 1024:1536], -1.0, None,
                    ALU.mult)), "dve")
                wait("t", "dve", n_rows)
                T(lambda e: e.matmul(pbh[6][:, 0:512], s_or[:], s_rwb[0:1, 0:512],
                                     start=True, stop=True))
                n_bc = inc("t", T(lambda e: e.matmul(
                    pbh[7][:, 0:512], s_or[:], s_rwb[0:1, 512:1024],
                    start=True, stop=True)), "pe")
                wait("v", "pe", n_bc)
                for dc in range(NDC):
                    xs = h_xfm[sl][:, dc * 512:(dc + 1) * 512]
                    V(lambda e, xs=xs: e.tensor_tensor(xs, xs, pbh[6][:, 0:512],
                                                       ALU.mult))
                    ins = V(lambda e, xs=xs: e.tensor_tensor(
                        xs, xs, pbh[7][:, 0:512], ALU.add))
                    n_ln1 = inc("v", ins, "dve")
                wait("t", "dve", n_ln1)
                if b == 0 and tc == 0:
                    wait("t", "dg", v_wqkv)
                for hl in range(2):
                    for dc in range(NDC):
                        T(lambda e, hl=hl, dc=dc, sl=sl: e.matmul(
                            pbh[hl][:, 0:512],
                            h_wq[:, (hl * NDC + dc) * P:(hl * NDC + dc + 1) * P],
                            h_xfm[sl][:, dc * 512:(dc + 1) * 512],
                            start=(dc == 0), stop=(dc == NDC - 1)))
                    for dc in range(NDC):
                        ins = T(lambda e, hl=hl, dc=dc, sl=sl: e.matmul(
                            pbh[2 + hl][:, 0:512],
                            h_wk[:, (hl * NDC + dc) * P:(hl * NDC + dc + 1) * P],
                            h_xfm[sl][:, dc * 512:(dc + 1) * 512],
                            start=(dc == 0), stop=(dc == NDC - 1)))
                    n_qk = inc("t", ins, "pe")
                for tt in range(4):
                    bank = 4 + tt % 2
                    reg = (tt // 2) * 256
                    for dc in range(NDC):
                        ins = T(lambda e, tt=tt, dc=dc, sl=sl, bank=bank, reg=reg:
                                e.matmul(
                                    pbh[bank][:, reg:reg + 256],
                                    h_xfm[sl][:, dc * 512 + tt * P:dc * 512 + (tt + 1) * P],
                                    h_wv[:, dc * 256:(dc + 1) * 256],
                                    start=(dc == 0 and reg == 0),
                                    stop=(dc == NDC - 1 and reg == 256),
                                    skip_group_check=True))
                    n_v = inc("t", ins, "pe")
                xfm_free[sl] = ("pe", n_v)
                wait("c", "pe", n_qk)
                for hl in range(2):
                    A(lambda e, hl=hl, tc=tc: e.activation(
                        h_q[hl][:, tc * 512:tc * 512 + 512], pbh[hl][:, 0:512],
                        AF.Copy, scale=0.125))
                    n_ev = inc("c", A(lambda e, hl=hl, tc=tc: e.activation(
                        h_k[hl][:, tc * 512:tc * 512 + 512], pbh[2 + hl][:, 0:512],
                        AF.Copy)), "act")
                wait("v", "pe", n_v)
                for tt in range(4):
                    bank = 4 + tt % 2
                    reg = (tt // 2) * 256
                    gt = tc * 4 + tt
                    for hl in range(2):
                        ins = V(lambda e, bank=bank, reg=reg, hl=hl, gt=gt:
                                e.tensor_copy(
                                    h_v[hl][:, gt * VS:gt * VS + P],
                                    pbh[bank][:, reg + hl * P:reg + (hl + 1) * P]))
                        n_vev = inc("v", ins, "dve")

            if debug and b == 0:
                wait("s", "act", C["act"].n)
                wait("s", "dve", C["dve"].n)
                dma("s", d_dbgq[:], h_q[0][:], "dsy")
                dma("s", d_dbgk[:], h_k[0][:], "dsy")
                dma("s", d_dbgv[:], h_v[0][:], "dsy")
            for hl in range(2):
                wait("s", "act", C["act"].n)
                wait("s", "pe", C["pe"].n)
                dma("s", h_q1a[0:64, :], h_q[hl][0:64, :], "dsy")
                dma("s", h_q2a[0:64, :], h_q[hl][64:128, :], "dsy")
                dma("s", h_k1a[0:64, :], h_k[hl][0:64, :], "dsy")
                v_aug = dma("s", h_k2a[0:64, :], h_k[hl][64:128, :], "dsy")
                wait("t", "dsy", v_aug)
                wait("t", "act", C["act"].n)
                wait("t", "dve", C["dve"].n)
                for g in range(NG):
                    for j in range(4):
                        qt = 4 * g + j
                        for kc4 in range(g + 1):
                            N = 512 if kc4 < g else 128 * (j + 1)
                            par = kc4 % 2
                            pa, pb_ = 4 + par, 6 + par
                            for (kk, vv) in pt_free[par]:
                                wait("t", kk, vv)
                            T(lambda e, hl=hl, qt=qt, kc4=kc4, N=N, pa=pa: e.matmul(
                                pbh[pa][:, 0:N],
                                h_q[hl][0:64, qt * P:(qt + 1) * P],
                                h_k[hl][0:64, kc4 * 512:kc4 * 512 + N],
                                start=True, stop=True))
                            n_s = inc("t", T(lambda e, hl=hl, qt=qt, kc4=kc4, N=N,
                                             pb_=pb_: e.matmul(
                                pbh[pb_][:, 0:N],
                                h_q[hl][64:128, qt * P:(qt + 1) * P],
                                h_k[hl][64:128, kc4 * 512:kc4 * 512 + N],
                                start=True, stop=True)), "pe")
                            wait("v", "pe", n_s)
                            if kc4 == g:
                                V(lambda e, pa=pa, j=j: e.tensor_add(
                                    s_ev[:, 0:P], pbh[pa][:, j * P:(j + 1) * P],
                                    s_trin[:]))
                                V(lambda e, pb_=pb_, j=j: e.tensor_add(
                                    s_ev[:, P:2 * P], pbh[pb_][:, j * P:(j + 1) * P],
                                    s_trin[:]))
                                inc("v", V(lambda e, kc4=kc4: e.reduce_max(
                                    s_mxa[:, kc4:kc4 + 1], s_ev[:, 0:P], AX.X)),
                                    "red")
                                n_red = inc("v", V(lambda e, kc4=kc4: e.reduce_max(
                                    s_mxb[:, kc4:kc4 + 1], s_ev[:, P:2 * P], AX.X)),
                                    "red")
                                if j > 0:
                                    inc("v", V(lambda e, pa=pa, j=j, kc4=kc4:
                                               e.reduce_max(
                                        s_mxa[:, kc4 + 1:kc4 + 2],
                                        pbh[pa][:, 0:j * P], AX.X)), "red")
                                    n_red = inc("v", V(lambda e, pb_=pb_, j=j,
                                                       kc4=kc4: e.reduce_max(
                                        s_mxb[:, kc4 + 1:kc4 + 2],
                                        pbh[pb_][:, 0:j * P], AX.X)), "red")
                            else:
                                inc("v", V(lambda e, pa=pa, N=N, kc4=kc4:
                                           e.reduce_max(
                                    s_mxa[:, kc4:kc4 + 1], pbh[pa][:, 0:N], AX.X)),
                                    "red")
                                n_red = inc("v", V(lambda e, pb_=pb_, N=N, kc4=kc4:
                                                   e.reduce_max(
                                    s_mxb[:, kc4:kc4 + 1], pbh[pb_][:, 0:N], AX.X)),
                                    "red")
                            pt_free[par] = [("red", n_red)]
                        tot = (g + 1) + (1 if j > 0 else 0)
                        wait("v", "red", C["red"].n)
                        inc("v", V(lambda e, tot=tot: e.reduce_max(
                            s_mc[:, 0:1], s_mxa[:, 0:tot], AX.X)), "red")
                        n_red = inc("v", V(lambda e, tot=tot: e.reduce_max(
                            s_mc[:, 1:2], s_mxb[:, 0:tot], AX.X)), "red")
                        wait("v", "red", n_red)
                        n_cast = inc("v", V(lambda e: e.tensor_copy(
                            s_mcb[:], s_mc[:])), "dve")
                        wait("t", "dve", n_cast)
                        wait("t", "red", C["red"].n)
                        for (kk, vv) in pt_free[0] + pt_free[1]:
                            wait("t", kk, vv)
                        T(lambda e: e.transpose(pbb[6][0:1, 0:P], s_mcb[:, 0:1],
                                                s_idn[:]))
                        n_tp = inc("t", T(lambda e: e.transpose(
                            pbb[7][0:1, 0:P], s_mcb[:, 1:2], s_idn[:])), "pe")
                        wait("v", "pe", n_tp)
                        V(lambda e, qt=qt: e.tensor_copy(
                            h_q1a[64:65, qt * P:(qt + 1) * P], pbb[6][0:1, 0:P]))
                        n_mrow = inc("v", V(lambda e, qt=qt: e.tensor_copy(
                            h_q2a[64:65, qt * P:(qt + 1) * P], pbb[7][0:1, 0:P])),
                            "dve")
                        pt_free[0] = [("dve", n_mrow)]
                        pt_free[1] = [("dve", n_mrow)]
                    kmax = 4 * g + 4
                    for kc in range(kmax):
                        par = kc % 2
                        sb1, sb2 = 4 + par, 6 + par
                        for (kk, vv) in pt_free[par]:
                            wait("t", kk, vv)
                        if kc == 0:
                            wait("t", "dve", n_mrow)
                        T(lambda e, kc=kc, sb1=sb1, g=g: e.matmul(
                            pbh[sb1][:, 0:512], h_k1a[:, kc * P:(kc + 1) * P],
                            h_q1a[:, g * 512:(g + 1) * 512], start=True, stop=True))
                        n_s = inc("t", T(lambda e, kc=kc, sb2=sb2, g=g: e.matmul(
                            pbh[sb2][:, 0:512], h_k2a[:, kc * P:(kc + 1) * P],
                            h_q2a[:, g * 512:(g + 1) * 512], start=True, stop=True)),
                            "pe")
                        wait("c", "pe", n_s)
                        A(lambda e, par=par, sb1=sb1: e.activation(
                            h_p1t[par][:], pbh[sb1][:, 0:512], AF.Exp))
                        n_e = inc("c", A(lambda e, par=par, sb2=sb2: e.activation(
                            h_p2t[par][:], pbh[sb2][:, 0:512], AF.Exp)), "act")
                        pt_free[par] = [("act", n_e)]
                        ready = ("act", n_e)
                        if kc >= 4 * g:
                            o = kc - 4 * g
                            wait("v", "act", n_e)
                            V(lambda e, par=par, o=o: e.tensor_mul(
                                h_p1t[par][:, o * P:(o + 1) * P],
                                h_p1t[par][:, o * P:(o + 1) * P], s_tri[:]))
                            n_mk = inc("v", V(lambda e, par=par, o=o: e.tensor_mul(
                                h_p2t[par][:, o * P:(o + 1) * P],
                                h_p2t[par][:, o * P:(o + 1) * P], s_tri[:])), "dve")
                            ready = ("dve", n_mk)
                        wait("t", ready[0], ready[1])
                        last_pv = None
                        for j in range(max(0, kc - 4 * g), 4):
                            for sd in range(2):
                                pt = h_p1t[par] if sd == 0 else h_p2t[par]
                                last_pv = T(lambda e, j=j, sd=sd, pt=pt, hl=hl,
                                            kc=kc, kmax=kmax: e.matmul(
                                    pbh[j][:, sd * 256:sd * 256 + 129],
                                    pt[:, j * P:(j + 1) * P],
                                    h_v[hl][:, kc * VS:kc * VS + 129],
                                    start=(kc == 0 and sd == 0),
                                    stop=(kc == kmax - 1 and sd == 1),
                                    skip_group_check=True))
                        n_pv = inc("t", last_pv, "pe")
                    wait("v", "pe", n_pv)
                    if debug and b == 0 and hl == 0 and g == 0:
                        n_dbg = inc("v", V(lambda e: e.tensor_copy(
                            s_ev[:, 0:4 * P], pbh[0][:, 0:4 * P])), "dve")
                        wait("s", "dve", n_dbg)
                        dma("s", d_dbgp[:], s_ev[:, 0:4 * P].bitcast(F32)[:, 0:4 * P]
                            if False else s_ev[:, 0:4 * P], "dsy")
                        dma("s", d_dbgp1[:], h_p1t[1][:], "dsy")
                        wait("v", "dsy", C["dsy"].n)
                    for j in range(4):
                        V(lambda e, j=j: e.reciprocal(s_st1[:, 0:1],
                                                      pbh[j][:, 128:129]))
                        n_rc = inc("v", V(lambda e, j=j: e.reciprocal(
                            s_st1[:, 1:2], pbh[j][:, 384:385])), "red")
                        wait("v", "red", n_rc)
                        V(lambda e, j=j: e.tensor_scalar(
                            s_ev[:, 0:P], pbh[j][:, 0:P], s_st1[:, 0:1], None,
                            ALU.mult))
                        V(lambda e, j=j: e.tensor_scalar(
                            s_ev[:, P:2 * P], pbh[j][:, 256:256 + P], s_st1[:, 1:2],
                            None, ALU.mult))
                        V(lambda e: e.tensor_mul(s_ev[:, 2 * P:3 * P],
                                                 s_ev[:, P:2 * P], s_lam[:]))
                        V(lambda e: e.tensor_sub(s_ev[:, 0:P], s_ev[:, 0:P],
                                                 s_ev[:, 2 * P:3 * P]))
                        n_r1 = inc("v", V(lambda e: e.tensor_reduce(
                            s_st1[:, 2:3], s_ev[:, 0:P], AX.X, ALU.add)), "red")
                        wait("v", "red", n_r1)
                        n_mu = inc("v", V(lambda e: e.tensor_scalar(
                            s_st1[:, 2:3], s_st1[:, 2:3], 1.0 / P, None, ALU.mult)),
                            "red")
                        wait("v", "red", n_mu)
                        V(lambda e: e.tensor_scalar(s_ev[:, P:2 * P], s_ev[:, 0:P],
                                                    s_st1[:, 2:3], None,
                                                    ALU.subtract))
                        n_r2 = inc("v", V(lambda e: e.scalar_tensor_tensor(
                            s_ev[:, 2 * P:3 * P], s_ev[:, P:2 * P], 1.0,
                            s_ev[:, P:2 * P], ALU.bypass, ALU.mult,
                            accum_out=s_st1[:, 3:4])), "red")
                        wait("c", "red", n_r2)
                        A(lambda e: e.activation(s_st1[:, 4:5], s_st1[:, 3:4],
                                                 AF.Ln, bias=s_eln[:],
                                                 scale=1.0 / P))
                        n_gr = inc("c", A(lambda e: e.activation(
                            s_st1[:, 5:6], s_st1[:, 4:5], AF.Exp, scale=-0.5)),
                            "act")
                        wait("v", "act", n_gr)
                        kk, vv = at_free[j % 2]
                        wait("v", kk, vv)
                        V(lambda e: e.tensor_scalar(
                            s_ev[:, 3 * P:4 * P], s_ev[:, P:2 * P], s_st1[:, 5:6],
                            None, ALU.mult))
                        V(lambda e, hl=hl: e.tensor_mul(
                            s_ev[:, 3 * P:4 * P], s_ev[:, 3 * P:4 * P],
                            s_g2[:, hl * P:(hl + 1) * P]))
                        ins = V(lambda e, hl=hl, j=j: e.tensor_tensor(
                            h_at[j % 2][:], s_ev[:, 3 * P:4 * P],
                            s_b2[:, hl * P:(hl + 1) * P], ALU.add))
                        n_gn = inc("v", ins, "dve")
                        wait("s", "dve", n_gn)
                        if debug and b == 0 and hl == 0 and g == 0 and j == 0:
                            dma("s", d_dbge[:], s_ev[:], "dsy")
                            dma("s", d_dbgs[:], s_st1[:], "dsy")
                            dma("s", d_dbgh[:], h_at[0][:], "dsy")
                            dma("s", d_dbgb[:], s_b2[:], "dsy")
                        row0 = b * SEQ + g * 512 + j * P
                        v_at = dma("s", d_agin[row0:row0 + P,
                                               hl * P:(hl + 1) * P],
                                   h_at[j % 2][:], "dsy")
                        if debug:
                            v_at = dma("s", d_dbga[row0:row0 + P,
                                                   hl * P:(hl + 1) * P],
                                       h_at[j % 2][:], "dsy")
                        at_free[j % 2] = ("dsy", v_at)
                    wait("t", "dve", n_gn)

        wait("g", "dsy", C["dsy"].n)
        n_coll = inc("g", G(lambda e: e.collective_compute(
            "AllGather", ALU.bypass, replica_groups=[list(range(NCORE))],
            ins=[d_agin.opt()], outs=[d_agout.opt()])), "coll")


        for w_ in ("g", "v", "t", "c", "s"):
            wait(w_, "coll", n_coll)

        if debug:
            dma("g", h_atm[0][:, 0:256], d_agout[6 * TOK:6 * TOK + P, :])
            c_ = C["ga0"]
            if en == "g":
                eng.indirect_dma_start(
                    h_atm[0][:, 256:512], None, d_agout[:],
                    bass.IndirectOffsetOnAxis(s_aix[:, 6:7], 0)).then_inc(c_.sem, 16)
            c_.n += 16
            wait("s", "dg", C["dg"].n)
            wait("s", "ga0", C["ga0"].n)
            dma("s", d_dbgo[:], h_atm[0][:, 0:512], "dsy")
            wait("g", "dsy", C["dsy"].n)
        atm_free = [("dve", 0), ("dve", 0)]
        for tt in range(8):
            sl = tt % 2
            kind, val = atm_free[sl]
            wait("g", kind, val)
            for r in range(NCORE):
                i_col = tt * 8 + r
                c_ = C["ga0" if sl == 0 else "ga1"]
                if en == "g":
                    eng.indirect_dma_start(
                        h_atm[sl][:, r * 256:(r + 1) * 256], None,
                        d_agout[:], bass.IndirectOffsetOnAxis(
                            s_aix[:, i_col:i_col + 1], 0),
                    ).then_inc(c_.sem, 16)
                c_.n += 16
                v_a = c_.n
            wait("t", "ga0" if sl == 0 else "ga1", v_a)
            if debug and tt == 0:
                wait("s", "ga0", v_a)
                dma("s", d_dbgt[:], h_atm[0][:], "dsy")
                wait("t", "dsy", C["dsy"].n)
            for r in range(NCORE):
                for half in range(2):
                    dch = 2 * r + half
                    bank = dch % 2
                    n_tp = inc("t", T(lambda e, sl=sl, r=r, half=half, bank=bank:
                                      e.transpose(
                                          pbb[4 + bank][:, 0:P],
                                          h_atm[sl][:, r * 256 + half * P:
                                                    r * 256 + (half + 1) * P],
                                          s_idn[:])), "pe")
                    wait("v", "pe", n_tp)
                    n_cp = inc("v", V(lambda e, dch=dch, tt=tt, bank=bank:
                                      e.tensor_copy(
                                          h_afm[:, dch * OWN + tt * P:
                                                dch * OWN + (tt + 1) * P],
                                          pbb[4 + bank][:, 0:P])), "dve")
                    wait("t", "dve", n_cp)
            atm_free[sl] = ("dve", n_cp)
            if debug and tt == 0:
                wait("s", "dve", C["dve"].n)
                dma("s", d_dbgf0[:], h_afm[:], "dsy")
                wait("v", "dsy", C["dsy"].n)
        n_afm = C["dve"].n

        wait("t", "dve", n_afm)
        wcb_free = [("pe", 0), ("pe", 0)]
        for jc in range(4):
            sl = jc % 2
            kind, val = wcb_free[sl]
            wait("g", kind, val)
            v_w = dma("g", h_wcb[sl].rearrange("p (c f) -> p c f", c=NDC),
                      d_woT[jc].rearrange("c p f -> p c f"))
            wait("t", "dg", v_w)
            for tt in range(8):
                for ic in range(NDC):
                    ins = T(lambda e, sl=sl, ic=ic, tt=tt: e.matmul(
                        pbh[tt % 2][:, 0:512],
                        h_afm[:, ic * OWN + tt * P:ic * OWN + (tt + 1) * P],
                        h_wcb[sl][:, ic * 512:(ic + 1) * 512],
                        start=(ic == 0), stop=(ic == NDC - 1)))
                n_mm = inc("t", ins, "pe")
                wait("v", "pe", n_mm)
                n_ev = inc("v", V(lambda e, tt=tt, jc=jc: e.tensor_copy(
                    h_at2[:, tt * D + jc * 512:tt * D + (jc + 1) * 512],
                    pbh[tt % 2][:, 0:512])), "dve")
                wait("t", "dve", n_ev)
            wcb_free[sl] = ("pe", C["pe"].n)
        n_at2 = C["dve"].n

        if debug:
            wait("s", "dve", C["dve"].n)
            dma("s", d_dbg2[:], h_at2[:], "dsy")
            dma("s", d_dbgf[:], h_afm[:], "dsy")
        wait("s", "pe", C["pe"].n)
        wait("s", "dve", C["dve"].n)
        v_xo = dma("s", h_xsb.rearrange("p (t d) -> p t d", t=8),
                d_xow.rearrange("(t p) d -> p t d", p=P), "dsy")
        wait("v", "dsy", v_xo)
        wait("v", "dve", n_at2)
        for tt in range(8):
            at2 = h_at2[:, tt * D:(tt + 1) * D]
            xsl = h_xsb[:, tt * D:(tt + 1) * D]
            n_r = inc("v", V(lambda e, at2=at2: e.tensor_reduce(
                s_st1[:, 0:1], at2, AX.X, ALU.add)), "red")
            wait("v", "red", n_r)
            n_mu = inc("v", V(lambda e: e.tensor_scalar(
                s_st1[:, 0:1], s_st1[:, 0:1], 1.0 / D, None, ALU.mult)), "red")
            wait("v", "red", n_mu)
            V(lambda e, at2=at2: e.tensor_scalar(h_scr[:], at2, s_st1[:, 0:1],
                                                 None, ALU.subtract))
            n_r2 = inc("v", V(lambda e: e.scalar_tensor_tensor(
                h_scr[:], h_scr[:], 1.0, h_scr[:], ALU.bypass, ALU.mult,
                accum_out=s_st1[:, 1:2])), "red")
            wait("c", "red", n_r2)
            A(lambda e: e.activation(s_st1[:, 2:3], s_st1[:, 1:2], AF.Ln,
                                     bias=s_eln[:], scale=1.0 / D))
            n_rs = inc("c", A(lambda e: e.activation(
                s_st1[:, 3:4], s_st1[:, 2:3], AF.Exp, scale=-0.5)), "act")
            wait("v", "act", n_rs)
            V(lambda e, at2=at2: e.tensor_scalar(h_scr[:], at2, s_st1[:, 0:1],
                                                 s_st1[:, 3:4], ALU.subtract,
                                                 ALU.mult))
            n_x1 = inc("v", V(lambda e, xsl=xsl: e.tensor_add(xsl, xsl, h_scr[:])),
                       "dve")
            wait("s", "dve", n_x1)
            dma("s", d_x1sp[tt * P:(tt + 1) * P, :], xsl, "dsy")
            n_r = inc("v", V(lambda e, xsl=xsl: e.tensor_reduce(
                s_st1[:, 4:5], xsl, AX.X, ALU.add)), "red")
            wait("v", "red", n_r)
            n_mu = inc("v", V(lambda e: e.tensor_scalar(
                s_st1[:, 4:5], s_st1[:, 4:5], 1.0 / D, None, ALU.mult)), "red")
            wait("v", "red", n_mu)
            V(lambda e, xsl=xsl: e.tensor_scalar(h_scr[:], xsl, s_st1[:, 4:5],
                                                 None, ALU.subtract))
            n_r2 = inc("v", V(lambda e: e.scalar_tensor_tensor(
                h_scr[:], h_scr[:], 1.0, h_scr[:], ALU.bypass, ALU.mult,
                accum_out=s_st1[:, 5:6])), "red")
            wait("c", "red", n_r2)
            A(lambda e: e.activation(s_st1[:, 6:7], s_st1[:, 5:6], AF.Ln,
                                     bias=s_eln[:], scale=1.0 / D))
            n_rs = inc("c", A(lambda e: e.activation(
                s_st1[:, 7:8], s_st1[:, 6:7], AF.Exp, scale=-0.5)), "act")
            wait("v", "act", n_rs)
            ln2t = h_ln2[:, tt * D:(tt + 1) * D]
            n_l2 = inc("v", V(lambda e, xsl=xsl, ln2t=ln2t: e.tensor_scalar(
                ln2t, xsl, s_st1[:, 4:5], s_st1[:, 7:8], ALU.subtract, ALU.mult)),
                "dve")
            wait("s", "dve", n_l2)
            dma("s", d_ln2sp[tt * P:(tt + 1) * P, :], ln2t, "dsy")
            n_sc = inc("v", V(lambda e, ln2t=ln2t: e.scalar_tensor_tensor(
                h_scr[:], ln2t, 1.0, s_wrt[:], ALU.bypass, ALU.mult,
                accum_out=s_st2[:, 0:1])), "red")
            wait("v", "red", n_sc)
            n_sc2 = inc("v", V(lambda e: e.tensor_tensor(
                s_st2[:, 1:2], s_st2[:, 0:1], s_brt[:], ALU.add)), "dve")
            wait("c", "dve", n_sc2)
            n_sg = inc("c", A(lambda e: e.sign(s_st2[:, 2:3], s_st2[:, 1:2])), "act")
            wait("v", "act", n_sg)
            inc("v", V(lambda e, tt=tt: e.tensor_scalar_max(
                s_rmask[:, tt:tt + 1], s_st2[:, 2:3], 0.0)), "dve")

        for tt in range(8):
            ln2t = h_ln2[:, tt * D:(tt + 1) * D]
            n_r = inc("v", V(lambda e, ln2t=ln2t: e.scalar_tensor_tensor(
                h_scr[:], ln2t, 1.0, ln2t, ALU.bypass, ALU.mult,
                accum_out=s_st1[:, 0:1])), "red")
            wait("c", "red", n_r)
            A(lambda e: e.activation(s_st1[:, 1:2], s_st1[:, 0:1], AF.Ln,
                                     bias=s_erm[:], scale=1.0 / D))
            n_rs = inc("c", A(lambda e: e.activation(
                s_st1[:, 2:3], s_st1[:, 1:2], AF.Exp, scale=-0.5)), "act")
            wait("v", "act", n_rs)
            n_h1 = inc("v", V(lambda e, ln2t=ln2t: e.tensor_scalar(
                h_scr2[:], ln2t, s_st1[:, 2:3], None, ALU.mult)), "dve")
            wait("t", "dve", n_h1)
            for dc in range(NDC):
                bank = dc % 2
                n_tp = inc("t", T(lambda e, dc=dc, bank=bank: e.transpose(
                    pbb[4 + bank][:, 0:P], h_scr2[:, dc * P:(dc + 1) * P],
                    s_idn[:])), "pe")
                wait("v", "pe", n_tp)
                n_cp = inc("v", V(lambda e, dc=dc, tt=tt, bank=bank: e.tensor_scalar(
                    h_h1f[:, dc * OWN + tt * P:dc * OWN + (tt + 1) * P],
                    pbb[4 + bank][:, 0:P], s_n1[:, dc:dc + 1], None, ALU.mult)),
                    "dve")
                wait("t", "dve", n_cp)
        n_h1f = C["dve"].n

        wait("t", "dve", n_h1f)
        wst_free = [("dve", C["dve"].n), ("dve", C["dve"].n)]
        for ec in range(NDC):
            sl = ec % 2
            kind, val = wst_free[sl]
            wait("g", kind, val)
            v_w = dma("g", h_wstA[sl].rearrange("p (c f) -> p c f", c=NDC),
                      d_wvf[ec].rearrange("c p f -> p c f"))
            wait("t", "dg", v_w)
            for th in range(2):
                for dc in range(NDC):
                    ins = T(lambda e, sl=sl, dc=dc, th=th: e.matmul(
                        pbh[th][:, 0:512], h_wstA[sl][:, dc * P:(dc + 1) * P],
                        h_h1f[:, dc * OWN + th * 512:dc * OWN + (th + 1) * 512],
                        start=(dc == 0), stop=(dc == NDC - 1)))
                n_mm = inc("t", ins, "pe")
                wait("v", "pe", n_mm)
                n_ev = inc("v", V(lambda e, ec=ec, th=th: e.tensor_scalar(
                    h_vfm[:, ec * OWN + th * 512:ec * OWN + (th + 1) * 512],
                    pbh[th][:, 0:512], s_bv[:, ec:ec + 1], None, ALU.add)), "dve")
                wait("t", "dve", n_ev)
            wst_free[sl] = ("dve", C["dve"].n)
        n_vfm = C["dve"].n

        wait("t", "dve", n_vfm)
        wcb_free = [("pe", 0), ("pe", 0)]
        for jc in range(4):
            sl = jc % 2
            kind, val = wcb_free[sl]
            wait("g", kind, val)
            v_w = dma("g", h_wcb[sl].rearrange("p (c f) -> p c f", c=NDC),
                      d_wof[jc].rearrange("c p f -> p c f"))
            wait("t", "dg", v_w)
            for tt in range(8):
                for ec in range(NDC):
                    ins = T(lambda e, sl=sl, ec=ec, tt=tt: e.matmul(
                        pbh[tt % 2][:, 0:512],
                        h_vfm[:, ec * OWN + tt * P:ec * OWN + (tt + 1) * P],
                        h_wcb[sl][:, ec * 512:(ec + 1) * 512],
                        start=(ec == 0), stop=(ec == NDC - 1)))
                n_mm = inc("t", ins, "pe")
                wait("v", "pe", n_mm)
                n_ev = inc("v", V(lambda e, tt=tt, jc=jc: e.scalar_tensor_tensor(
                    h_h[:, tt * D + jc * 512:tt * D + (jc + 1) * 512],
                    pbh[tt % 2][:, 0:512], 1.0,
                    h_ln2[:, tt * D + jc * 512:tt * D + (jc + 1) * 512],
                    ALU.bypass, ALU.add)), "dve")
                wait("t", "dve", n_ev)
            wcb_free[sl] = ("pe", C["pe"].n)

        for tt in range(8):
            hsl = h_h[:, tt * D:(tt + 1) * D]
            n_r = inc("v", V(lambda e, hsl=hsl: e.scalar_tensor_tensor(
                h_scr[:], hsl, 1.0, hsl, ALU.bypass, ALU.mult,
                accum_out=s_st1[:, 0:1])), "red")
            wait("c", "red", n_r)
            A(lambda e: e.activation(s_st1[:, 1:2], s_st1[:, 0:1], AF.Ln,
                                     bias=s_erm[:], scale=1.0 / D))
            n_rs = inc("c", A(lambda e, tt=tt: e.activation(
                s_rstd2[:, tt:tt + 1], s_st1[:, 1:2], AF.Exp, scale=-0.5)), "act")
            wait("s", "red", n_r)
            dma("s", d_hsp[tt * P:(tt + 1) * P, :], hsl, "dsy")
        n_rstd2 = C["act"].n
        v_hsp = C["dsy"].n

        wait("v", "act", n_rstd2)
        wait("g", "dsy", v_hsp)
        wait("g", "dve", C["dve"].n)
        gu_free = [("dve", C["dve"].n), ("dve", C["dve"].n)]
        dw_free = [("pe", C["pe"].n), ("pe", C["pe"].n)]
        l2p_free = [("dve", 0), ("dve", 0)]
        hp_free = [("dve", 0), ("dve", 0)]
        for sl4 in range(2):
            for ti in range(4):
                tt = sl4 * 4 + ti
                wait("s", "dve", C["dve"].n)
                wait("s", "dsy", v_hsp)
                v_h = dma("s", h_scr2[:], d_hsp[tt * P:(tt + 1) * P, :], "dsy")
                wait("v", "dsy", v_h)
                n_h2 = inc("v", V(lambda e, tt=tt: e.tensor_scalar(
                    h_scr2[:], h_scr2[:], s_rstd2[:, tt:tt + 1],
                    None, ALU.mult)), "dve")
                wait("t", "dve", n_h2)
                for dc in range(NDC):
                    bank = dc % 2
                    n_tp = inc("t", T(lambda e, dc=dc, bank=bank: e.transpose(
                        pbb[4 + bank][:, 0:P], h_scr2[:, dc * P:(dc + 1) * P],
                        s_idn[:])), "pe")
                    wait("v", "pe", n_tp)
                    n_cp = inc("v", V(lambda e, dc=dc, ti=ti, bank=bank:
                                      e.tensor_scalar(
                                          h_h2f[:, dc * 512 + ti * P:
                                                dc * 512 + (ti + 1) * P],
                                          pbb[4 + bank][:, 0:P], s_n2[:, dc:dc + 1],
                                          None, ALU.mult)), "dve")
                    wait("t", "dve", n_cp)
            n_h2f = C["dve"].n
            wait("t", "dve", n_h2f)
            for fc in range(NFC):
                slg = fc % 2
                kind, val = gu_free[slg]
                wait("g", kind, val)
                dma("g", h_wstB[slg].rearrange("p (c f) -> p c f", c=NDC),
                    d_gate[fc].rearrange("c p f -> p c f"))
                v_w = dma("g", h_wstB[2 + slg].rearrange("p (c f) -> p c f", c=NDC),
                          d_up[fc].rearrange("c p f -> p c f"))
                wait("t", "dg", v_w)
                for dc in range(NDC):
                    T(lambda e, slg=slg, dc=dc: e.matmul(
                        pbh[4][:, 0:512], h_wstB[slg][:, dc * P:(dc + 1) * P],
                        h_h2f[:, dc * 512:(dc + 1) * 512],
                        start=(dc == 0), stop=(dc == NDC - 1)))
                for dc in range(NDC):
                    ins = T(lambda e, slg=slg, dc=dc: e.matmul(
                        pbh[5][:, 0:512], h_wstB[2 + slg][:, dc * P:(dc + 1) * P],
                        h_h2f[:, dc * 512:(dc + 1) * 512],
                        start=(dc == 0), stop=(dc == NDC - 1)))
                n_mm = inc("t", ins, "pe")
                wait("c", "pe", n_mm)
                n_si = inc("c", A(lambda e: e.activation(
                    h_scr2[:, 0:512], pbh[4][:, 0:512], AF.Silu)), "act")
                wait("v", "act", n_si)
                n_ga = inc("v", V(lambda e, fc=fc: e.tensor_tensor(
                    h_gact[:, fc * 512:(fc + 1) * 512], h_scr2[:, 0:512],
                    pbh[5][:, 0:512], ALU.mult)), "dve")
                gu_free[slg] = ("dve", n_ga)
                wait("t", "dve", n_ga)
            n_gact = C["dve"].n
            wait("t", "dve", n_gact)
            wait("s", "dsy", v_hsp)
            for jc in range(8):
                sld = jc % 2
                kind, val = dw_free[sld]
                wait("g", kind, val)
                v_w = dma("g", h_dwn[sld].rearrange("p (c f) -> p c f", c=NFC),
                          d_down[jc].rearrange("c p f -> p c f"))
                kind, val = l2p_free[sld]
                wait("s", kind, val)
                v_l2 = dma("s", h_l2p[sld].rearrange("p (t d) -> p t d", t=4),
                           d_ln2sp[sl4 * 512:sl4 * 512 + 512, jc * 256:(jc + 1) * 256]
                           .rearrange("(t p) d -> p t d", p=P), "dsy")
                kind, val = hp_free[sld]
                wait("s", kind, val)
                v_hp = dma("s", h_hp[sld].rearrange("p (t d) -> p t d", t=4),
                           d_hsp[sl4 * 512:sl4 * 512 + 512, jc * 256:(jc + 1) * 256]
                           .rearrange("(t p) d -> p t d", p=P), "dsy")
                wait("t", "dg", v_w)
                wait("v", "dsy", v_hp)
                for ti in range(4):
                    tt = sl4 * 4 + ti
                    for fc in range(NFC):
                        ins = T(lambda e, sld=sld, fc=fc, ti=ti: e.matmul(
                            pbh[ti % 2][:, 0:256],
                            h_gact[:, fc * 512 + ti * P:fc * 512 + (ti + 1) * P],
                            h_dwn[sld][:, fc * 256:(fc + 1) * 256],
                            start=(fc == 0), stop=(fc == NFC - 1)))
                    n_mm = inc("t", ins, "pe")
                    wait("v", "pe", n_mm)
                    V(lambda e, ti=ti, sld=sld: e.tensor_sub(
                        h_scr[:, 0:256], h_hp[sld][:, ti * 256:(ti + 1) * 256],
                        h_l2p[sld][:, ti * 256:(ti + 1) * 256]))
                    V(lambda e, ti=ti: e.tensor_add(
                        h_scr[:, 0:256], h_scr[:, 0:256], pbh[ti % 2][:, 0:256]))
                    n_sel = inc("v", V(lambda e, ti=ti, tt=tt, sld=sld:
                                       e.scalar_tensor_tensor(
                                           h_scr2[:, 0:256], h_scr[:, 0:256],
                                           s_rmask[:, tt:tt + 1],
                                           h_l2p[sld][:, ti * 256:(ti + 1) * 256],
                                           ALU.mult, ALU.add)), "dve")
                    wait("t", "dve", n_sel)
                    wait("s", "dve", n_sel)
                    v_ss = dma("s", d_selsp[sl4 * 512 + ti * P:
                                            sl4 * 512 + (ti + 1) * P,
                                            jc * 256:(jc + 1) * 256],
                               h_scr2[:, 0:256], "dsy")
                dw_free[sld] = ("pe", C["pe"].n)
                l2p_free[sld] = ("dve", n_sel)
                hp_free[sld] = ("dve", n_sel)

        wait("s", "dsy", C["dsy"].n)
        x1_free = [("dve", 0), ("dve", 0)]
        sel_done = 0
        for tq in range(2):
            wait("s", "dve", sel_done)
            v_sel = dma("s", h_self.rearrange("p (t d) -> p t d", t=4),
                        d_selsp[tq * 512:tq * 512 + 512, :]
                        .rearrange("(t p) d -> p t d", p=P), "dsy")
            for ti in range(4):
                tt = tq * 4 + ti
                slx = ti % 2
                kind, val = x1_free[slx]
                wait("s", kind, val)
                v_x1 = dma("s", h_x1f[slx][:], d_x1sp[tt * P:(tt + 1) * P, :], "dsy")
                selt = h_self[:, ti * D:(ti + 1) * D]
                wait("v", "dsy", v_sel)
                n_r = inc("v", V(lambda e, selt=selt: e.tensor_reduce(
                    s_st1[:, 0:1], selt, AX.X, ALU.add)), "red")
                wait("v", "red", n_r)
                n_mu = inc("v", V(lambda e: e.tensor_scalar(
                    s_st1[:, 0:1], s_st1[:, 0:1], 1.0 / D, None, ALU.mult)), "red")
                wait("v", "red", n_mu)
                V(lambda e, selt=selt: e.tensor_scalar(h_scr[:], selt,
                                                       s_st1[:, 0:1], None,
                                                       ALU.subtract))
                n_r2 = inc("v", V(lambda e: e.scalar_tensor_tensor(
                    h_scr[:], h_scr[:], 1.0, h_scr[:], ALU.bypass, ALU.mult,
                    accum_out=s_st1[:, 1:2])), "red")
                wait("c", "red", n_r2)
                A(lambda e: e.activation(s_st1[:, 2:3], s_st1[:, 1:2], AF.Ln,
                                         bias=s_eln[:], scale=1.0 / D))
                n_rs = inc("c", A(lambda e: e.activation(
                    s_st1[:, 3:4], s_st1[:, 2:3], AF.Exp, scale=-0.5)), "act")
                wait("v", "act", n_rs)
                V(lambda e, selt=selt: e.tensor_scalar(
                    h_scr[:], selt, s_st1[:, 0:1], s_st1[:, 3:4], ALU.subtract,
                    ALU.mult))
                wait("v", "dsy", v_x1)
                n_out = inc("v", V(lambda e, slx=slx: e.tensor_add(
                    h_x1f[slx][:], h_x1f[slx][:], h_scr[:])), "dve")
                x1_free[slx] = ("dve", n_out)
                sel_done = n_out
                wait("s", "dve", n_out)
                dma("s", d_y[tt * P:(tt + 1) * P, :], h_x1f[slx][:], "out")
        wait("s", "out", C["out"].n)

    with nc.Block() as block:
        block.gpsimd(lambda e: emit("g", e))
        block.sync(lambda e: emit("s", e))
        block.scalar(lambda e: emit("c", e))
        block.tensor(lambda e: emit("t", e))
        block.vector(lambda e: emit("v", e))
    stack.close()
    return nc


_NC_CACHE = None


def kernel(**inputs):
    global _NC_CACHE
    shared, per_core = _prep(inputs)
    if _NC_CACHE is None:
        _NC_CACHE = build()
    in_maps = []
    for c in range(NCORE):
        m = dict(shared)
        m.update(per_core[c])
        in_maps.append(m)
    res = run_bass_kernel_spmd(_NC_CACHE, in_maps, core_ids=list(range(NCORE)))
    y = np.concatenate([res.results[c]["y"] for c in range(NCORE)], axis=0)
    return y.reshape(NB, SEQ, D)


# version 05
# speedup vs baseline: 1.2147x; 1.2147x over previous
import math
import numpy as np
import ml_dtypes
import concourse.bass as bass
import concourse.mybir as mybir
from concourse.bass_utils import run_bass_kernel_spmd

F32 = mybir.dt.float32
BF16 = mybir.dt.bfloat16
AF = mybir.ActivationFunctionType
ALU = mybir.AluOpType
AX = mybir.AxisListType
ds = bass.ds

D = 2048
SEQ = 4096
NB = 2
FF = 5632
P = 128
NCORE = 8
TOK = NB * SEQ
OWN = TOK // NCORE
NDC = D // P
NFC = FF // P
NG = SEQ // 512
LAM_INIT = float(0.8 - 0.6 * math.exp(np.float32(-0.3 * 12)))
EPS_LN = 1e-5
EPS_RMS = 1e-6
VS = 132

bf = ml_dtypes.bfloat16
KB = 1024


def _prep(inp):
    x = np.ascontiguousarray(np.asarray(inp["L0_attn_x"], np.float32).reshape(TOK, D))
    xT = np.ascontiguousarray(x.T)
    Wqkv = np.asarray(inp["L0_attn_W_qkv"], np.float32)
    Wo = np.asarray(inp["L0_attn_W_o"], np.float32)
    woT = np.ascontiguousarray(Wo.T)
    woT_cb = np.ascontiguousarray(woT.reshape(NDC, P, 4, 512).transpose(2, 0, 1, 3))
    wvf = np.asarray(inp["L0_ffn_wv"], np.float32)
    wvf_st = np.ascontiguousarray(wvf.reshape(NDC, P, NDC, P).transpose(2, 0, 1, 3))
    wof = np.asarray(inp["L0_ffn_wo"], np.float32)
    wof_cb = np.ascontiguousarray(wof.reshape(NDC, P, 4, 512).transpose(2, 0, 1, 3))
    gate = np.asarray(inp["L0_ffn_w_gate"], np.float32)
    gate_st = np.ascontiguousarray(gate.reshape(NDC, P, NFC, P).transpose(2, 0, 1, 3))
    up = np.asarray(inp["L0_ffn_w_up"], np.float32)
    up_st = np.ascontiguousarray(up.reshape(NDC, P, NFC, P).transpose(2, 0, 1, 3))
    down = np.asarray(inp["L0_ffn_w_down"], np.float32)
    down_cb = np.ascontiguousarray(down.reshape(NFC, P, 8, 256).transpose(2, 0, 1, 3))

    tile = lambda v: np.ascontiguousarray(
        np.tile(np.asarray(v, np.float32).reshape(1, -1), (P, 1)))
    shared = dict(
        xT=xT, woT_cb=woT_cb, wvf_st=wvf_st, wof_cb=wof_cb,
        gate_st=gate_st, up_st=up_st, down_cb=down_cb,
        lq1=tile(inp["L0_attn_lambda_q1"]), lk1=tile(inp["L0_attn_lambda_k1"]),
        lq2=tile(inp["L0_attn_lambda_q2"]), lk2=tile(inp["L0_attn_lambda_k2"]),
        n1w=np.ascontiguousarray(
            np.asarray(inp["L0_ffn_norm1_w"], np.float32).reshape(NDC, P).T),
        n2w=np.ascontiguousarray(
            np.asarray(inp["L0_ffn_norm2_w"], np.float32).reshape(NDC, P).T),
        bvv=np.ascontiguousarray(
            np.asarray(inp["L0_ffn_bv"], np.float32).reshape(NDC, P).T),
        wrt=np.ascontiguousarray(tile(inp["L0_ffn_w_router"]).astype(bf)),
        brt=np.ascontiguousarray(
            np.full((P, 1), np.asarray(inp["L0_ffn_b_router"], np.float32), np.float32)),
        tri=np.ascontiguousarray(np.triu(np.ones((P, P), dtype=bf))),
        trin=np.ascontiguousarray(
            np.where(np.tril(np.ones((P, P), bool)), 0.0, -1e30).astype(np.float32)),
        ident=np.ascontiguousarray(np.eye(P, dtype=bf)),
    )
    gng = np.asarray(inp["L0_attn_gn_gamma"], np.float32)
    gnb = np.asarray(inp["L0_attn_gn_beta"], np.float32)
    per_core = []
    for c in range(NCORE):
        h0 = 2 * c
        wq_t = np.stack([np.ascontiguousarray(
            Wqkv[P * (h0 + hl):P * (h0 + hl + 1), :].T).reshape(NDC, P, P)
            for hl in range(2)])
        wk_t = np.stack([np.ascontiguousarray(
            Wqkv[2048 + P * (h0 + hl):2048 + P * (h0 + hl + 1), :].T).reshape(NDC, P, P)
            for hl in range(2)])
        wv_t = np.ascontiguousarray(
            Wqkv[4096 + P * h0:4096 + P * (h0 + 2), :].T).reshape(NDC, P, 256)
        gam = np.stack([np.tile(gng[h0 + hl][None, :], (P, 1)) for hl in range(2)])
        bet = np.stack([np.tile(gnb[h0 + hl][None, :], (P, 1)) for hl in range(2)])
        agidx = np.zeros((P, 64), np.uint32)
        for tt in range(8):
            for r in range(NCORE):
                agidx[:, tt * 8 + r] = c * OWN + r * TOK + tt * P + np.arange(P)
        per_core.append(dict(
            wq_t=np.ascontiguousarray(wq_t), wk_t=np.ascontiguousarray(wk_t),
            wv_t=wv_t, gam=np.ascontiguousarray(gam),
            bet=np.ascontiguousarray(bet),
            agidx=np.ascontiguousarray(agidx),
            x_own=np.ascontiguousarray(x[OWN * c:OWN * (c + 1)])))
    return shared, per_core


class Ctr:
    __slots__ = ("sem", "n")

    def __init__(self, sem):
        self.sem = sem
        self.n = 0


def build(debug=False):
    nc = bass.Bass("TRN2", target_bir_lowering=False)

    DT = {}

    def din(name, shape, dtype=F32):
        DT[name] = nc.dram_tensor(name, list(shape), dtype, kind="ExternalInput").ap()
        return DT[name]

    d_xT = din("xT", [D, TOK])
    d_wq = din("wq_t", [2, NDC, P, P])
    d_wk = din("wk_t", [2, NDC, P, P])
    d_wv = din("wv_t", [NDC, P, 256])
    d_woT = din("woT_cb", [4, NDC, P, 512])
    d_wvf = din("wvf_st", [NDC, NDC, P, P])
    d_wof = din("wof_cb", [4, NDC, P, 512])
    d_gate = din("gate_st", [NFC, NDC, P, P])
    d_up = din("up_st", [NFC, NDC, P, P])
    d_down = din("down_cb", [8, NFC, P, 256])
    d_lq1 = din("lq1", [P, P])
    d_lk1 = din("lk1", [P, P])
    d_lq2 = din("lq2", [P, P])
    d_lk2 = din("lk2", [P, P])
    d_gam = din("gam", [2, P, P])
    d_bet = din("bet", [2, P, P])
    d_n1w = din("n1w", [P, NDC])
    d_n2w = din("n2w", [P, NDC])
    d_bvv = din("bvv", [P, NDC])
    d_wrt = din("wrt", [P, D], BF16)
    d_brt = din("brt", [P, 1])
    d_tri = din("tri", [P, P], BF16)
    d_trin = din("trin", [P, P])
    d_idn = din("ident", [P, P], BF16)
    d_xow = din("x_own", [OWN, D])
    d_agix = din("agidx", [P, 64], mybir.dt.uint32)
    d_y = nc.dram_tensor("y", [OWN, D], F32, kind="ExternalOutput").ap()

    dk = "ExternalOutput" if debug else "Internal"
    d_agin = nc.dram_tensor("ag_in", [TOK, 256], BF16, kind="Internal").ap()
    d_dbga = (nc.dram_tensor("dbg_a", [TOK, 256], BF16, kind="ExternalOutput").ap()
              if debug else None)
    if debug:
        d_dbgq = nc.dram_tensor("dbg_q", [P, SEQ], BF16, kind="ExternalOutput").ap()
        d_dbgk = nc.dram_tensor("dbg_k", [P, SEQ], BF16, kind="ExternalOutput").ap()
        d_dbgv = nc.dram_tensor("dbg_v", [P, 32 * VS], BF16, kind="ExternalOutput").ap()
        d_dbg2 = nc.dram_tensor("dbg_at2", [P, 8 * D], BF16, kind="ExternalOutput").ap()
        d_dbgp = nc.dram_tensor("dbg_ps", [P, 512], F32, kind="ExternalOutput").ap()
        d_dbge = nc.dram_tensor("dbg_ev", [P, 4 * P], F32, kind="ExternalOutput").ap()
        d_dbgs = nc.dram_tensor("dbg_st", [P, 16], F32, kind="ExternalOutput").ap()
        d_dbgp1 = nc.dram_tensor("dbg_p1", [P, 512], BF16, kind="ExternalOutput").ap()
        d_dbgh = nc.dram_tensor("dbg_hat", [P, P], BF16, kind="ExternalOutput").ap()
        d_dbgb = nc.dram_tensor("dbg_b2", [P, 2 * P], F32, kind="ExternalOutput").ap()
        d_dbgo = nc.dram_tensor("dbg_ago", [P, 512], BF16, kind="ExternalOutput").ap()
        d_dbgt = nc.dram_tensor("dbg_atm", [P, D], BF16, kind="ExternalOutput").ap()
        d_dbgf0 = nc.dram_tensor("dbg_afm0", [P, NDC * OWN], BF16,
                                 kind="ExternalOutput").ap()
        d_dbgf = nc.dram_tensor("dbg_afm", [P, NDC * OWN], BF16, kind="ExternalOutput").ap()

    d_agout = nc.dram_tensor("ag_out", [NCORE * TOK, 256], BF16, kind="Internal",
                             addr_space="Shared").ap()
    d_x1sp = nc.dram_tensor("x1sp", [OWN, D], F32, kind=dk).ap()
    d_ln2sp = nc.dram_tensor("ln2sp", [OWN, D], BF16, kind=dk).ap()
    d_hsp = nc.dram_tensor("hsp", [OWN, D], BF16, kind=dk).ap()
    d_selsp = nc.dram_tensor("selsp", [OWN, D], BF16, kind=dk).ap()

    pb = [nc.alloc_psum_tensor(f"pb{i}", [P, 512], F32) for i in range(8)]
    pbh = [b.ap() for b in pb]
    pbb = [b.ap().bitcast(BF16) for b in pb]

    def sb(name, shape, dtype):
        return nc.alloc_sbuf_tensor(name, list(shape), dtype).ap()

    s_idn = sb("s_idn", [P, P], BF16)
    s_tri = sb("s_tri", [P, P], BF16)
    s_lam = sb("s_lam", [P, P], F32)
    s_lt = sb("s_lt", [P, 2 * P], F32)
    s_g2 = sb("s_g2", [P, 2 * P], F32)
    s_b2 = sb("s_b2", [P, 2 * P], F32)
    s_n1 = sb("s_n1", [P, NDC], F32)
    s_n2 = sb("s_n2", [P, NDC], F32)
    s_bv = sb("s_bv", [P, NDC], F32)
    s_wrt = sb("s_wrt", [P, D], BF16)
    s_brt = sb("s_brt", [P, 1], F32)
    s_eln = sb("s_eln", [P, 1], F32)
    s_erm = sb("s_erm", [P, 1], F32)
    s_er1 = sb("s_er1", [1, 1], F32)
    s_oc = sb("s_oc", [P, 1], BF16)
    s_or = sb("s_or", [1, P], BF16)
    s_trin = sb("s_trin", [P, P], F32)
    s_mxa = sb("s_mxa", [P, 12], F32)
    s_mxb = sb("s_mxb", [P, 12], F32)
    s_mc = sb("s_mc", [P, 2], F32)
    s_mcb = sb("s_mcb", [P, 2], BF16)
    s_row = sb("s_row", [1, 3 * 512], F32)
    s_rwb = sb("s_rwb", [1, 1024], BF16)
    s_st1 = sb("s_st1", [P, 16], F32)
    s_st2 = sb("s_st2", [P, 8], F32)
    s_rmask = sb("s_rmask", [P, 8], F32)
    s_rstd2 = sb("s_rstd2", [P, 8], F32)
    s_ev = sb("s_ev", [P, 4 * P], F32)
    s_aix = sb("s_aix", [P, 64], mybir.dt.uint32)
    h_scr = sb("scr4", [P, D], F32)
    h_scr2 = sb("scr4b", [P, D], BF16)

    base = nc.sbuf_base
    avail = nc.sbuf_bytes_remaining - 512
    items = []

    def at(name, shape, dtype, off_kb, stages):
        per_part = int(np.prod(shape[1:])) * mybir.dt.size(dtype)
        off = off_kb * KB
        assert off + per_part <= avail, (name, off, per_part, avail)
        items.append((name, off, per_part, frozenset(stages)))
        return nc.alloc_sbuf_tensor_at(name, list(shape), dtype,
                                       offset=base + off).ap()

    P12 = ("p12",)
    h_xfm = [at(f"xfm{i}", [P, NDC * 512], BF16, 0 + 16 * i, P12) for i in range(2)]
    h_wq = at("wqs", [P, 2 * NDC * P], BF16, 32, P12)
    h_wk = at("wks", [P, 2 * NDC * P], BF16, 40, P12)
    h_wv = at("wvs", [P, NDC * 256], BF16, 48, P12)
    h_q = [at(f"qsb{i}", [P, SEQ], BF16, 56 + 8 * i, P12) for i in range(2)]
    h_k = [at(f"ksb{i}", [P, SEQ], BF16, 72 + 8 * i, P12) for i in range(2)]
    h_v = [at(f"vsb{i}", [P, 32 * VS], BF16, 88 + 9 * i, P12) for i in range(2)]
    h_q1a = at("q1a", [65, SEQ], BF16, 106, P12)
    h_q2a = at("q2a", [65, SEQ], BF16, 114, P12)
    h_k1a = at("k1a", [65, SEQ], BF16, 122, P12)
    h_k2a = at("k2a", [65, SEQ], BF16, 130, P12)
    h_p1t = [at(f"p1t{i}", [P, 512], BF16, 138 + i, P12) for i in range(2)]
    h_p2t = [at(f"p2t{i}", [P, 512], BF16, 140 + i, P12) for i in range(2)]
    h_xsq = [at(f"xsq{i}", [P, 512], BF16, 142 + i, P12) for i in range(2)]
    h_at = [at(f"atile{i}", [P, P], BF16, 144 + i, P12) for i in range(2)]

    h_atm = [at(f"a_tm{i}", [P, D], BF16, 0 + 4 * i, ("s4a",)) for i in range(2)]
    h_afm = at("afm", [P, NDC * OWN], BF16, 8, ("s4a", "s4b"))
    h_wcb = [at(f"wcb{i}", [P, NDC * 512], BF16, 40 + 16 * i, ("s4b", "s4f"))
             for i in range(2)]
    h_at2 = at("attn2", [P, 8 * D], BF16, 72, ("s4b", "s4c"))
    h_xsb = at("x_sb", [P, 8 * D], F32, 0, ("s4c",))
    h_ln2 = at("ln2", [P, 8 * D], BF16, 104, ("s4c", "s4d", "s4e", "s4f"))
    h_h1f = at("h1f", [P, NDC * OWN], BF16, 0, ("s4d", "s4e"))
    h_wstA = [at(f"wstA{i}", [P, NDC * P], BF16, 32 + 4 * i, ("s4e",))
              for i in range(2)]
    h_vfm = at("vfm", [P, NDC * OWN], BF16, 72, ("s4e", "s4f"))
    h_h = at("h_tm", [P, 8 * D], BF16, 0, ("s4f", "s4g"))
    h_wstB = [at(f"wstB{i}", [P, NDC * P], BF16, 0 + 4 * i, ("s4h",))
              for i in range(4)]
    h_gact = at("gact", [P, NFC * 512], BF16, 32, ("s4h",))
    h_dwn = [at(f"dwn{i}", [P, NFC * 256], BF16, 76 + 22 * i, ("s4h",))
             for i in range(2)]
    h_h2f = at("h2f", [P, NDC * 512], BF16, 120, ("s4h",))
    h_l2p = [at(f"l2p{i}", [P, 4 * 256], BF16, 136 + 2 * i, ("s4h",)) for i in range(2)]
    h_hp = [at(f"hp{i}", [P, 4 * 256], BF16, 140 + 2 * i, ("s4h",)) for i in range(2)]
    h_self = at("self", [P, 4 * D], BF16, 32, ("s4i",))
    h_x1f = [at(f"x1f{i}", [P, D], F32, 48 + 8 * i, ("s4i",)) for i in range(2)]

    for i, (n1, o1, s1, st1) in enumerate(items):
        for (n2, o2, s2, st2) in items[i + 1:]:
            if st1 & st2 and not (o1 + s1 <= o2 or o2 + s2 <= o1):
                raise AssertionError(f"sbuf overlap {n1} / {n2} in {sorted(st1 & st2)}")

    import contextlib
    stack = contextlib.ExitStack()
    sems = {k: stack.enter_context(nc.semaphore("sem_" + k))
            for k in ("dg", "dsy", "pe", "dve", "act", "red", "coll", "out",
                      "ga0", "ga1")}

    def emit(en, eng):
        C = {k: Ctr(v) for k, v in sems.items()}

        def dma(q, out, in_, ctr="dg", **kw):
            c = C[ctr]
            if en == q:
                eng.dma_start(out=out, in_=in_, **kw).then_inc(c.sem, 16)
            c.n += 16
            return c.n

        def wait(which, ctr, val=None):
            c = C[ctr]
            v = c.n if val is None else val
            if en == which:
                eng.wait_ge(c.sem, v)

        def inc(which, ins, ctr, k=1):
            c = C[ctr]
            if en == which and ins is not None:
                ins.then_inc(c.sem, k)
            c.n += k
            return c.n

        E = (lambda w: (lambda f: f(eng) if en == w else None))
        T, V, A, G, S = E("t"), E("v"), E("c"), E("g"), E("s")

        dma("s", s_idn[:], d_idn[:], "dsy")
        dma("s", s_tri[:], d_tri[:], "dsy")
        dma("s", s_trin[:], d_trin[:], "dsy")
        dma("s", s_lt[:, 0:P], d_lq1[:], "dsy")
        dma("s", s_lt[:, P:2 * P], d_lk1[:], "dsy")
        dma("s", s_g2[:, 0:P], d_lq2[:], "dsy")
        dma("s", s_g2[:, P:2 * P], d_lk2[:], "dsy")
        dma("s", s_n1[:], d_n1w[:], "dsy")
        dma("s", s_n2[:], d_n2w[:], "dsy")
        dma("s", s_bv[:], d_bvv[:], "dsy")
        dma("s", s_wrt[:], d_wrt[:], "dsy")
        dma("s", s_brt[:], d_brt[:], "dsy")
        dma("s", s_aix[:], d_agix[:], "dsy")
        v_p0 = dma("s", s_b2.rearrange("p (h c) -> p h c", h=2),
               d_gam.rearrange("h p c -> p h c"), "dsy")
        wait("v", "dsy", v_p0)
        V(lambda e: e.memset(s_eln[:], EPS_LN))
        V(lambda e: e.memset(s_erm[:], EPS_RMS))
        V(lambda e: e.memset(s_er1[:], EPS_LN))
        V(lambda e: e.memset(s_oc[:], 1.0))
        V(lambda e: e.memset(s_or[:], 1.0))
        V(lambda e: e.memset(h_k1a[64:65, :], -1.0))
        V(lambda e: e.memset(h_k2a[64:65, :], -1.0))
        V(lambda e: e.memset(h_v[0][:], 1.0))
        V(lambda e: e.memset(h_v[1][:], 1.0))
        V(lambda e: e.tensor_mul(s_lam[:], s_lt[:, 0:P], s_lt[:, P:2 * P]))
        n_d = inc("v", V(lambda e: e.tensor_mul(
            s_lt[:, 0:P], s_g2[:, 0:P], s_g2[:, P:2 * P])), "dve")
        wait("c", "dve", n_d)
        A(lambda e: e.activation(s_lam[:], s_lam[:], AF.Exp))
        n_a = inc("c", A(lambda e: e.activation(s_lt[:, 0:P], s_lt[:, 0:P],
                                                AF.Exp)), "act")
        wait("v", "act", n_a)
        V(lambda e: e.tensor_sub(s_lam[:], s_lam[:], s_lt[:, 0:P]))
        V(lambda e: e.tensor_scalar(s_lam[:], s_lam[:], LAM_INIT, None, ALU.add))
        n_g2 = inc("v", V(lambda e: e.tensor_scalar(
            s_g2[:], s_b2[:], 1.0 - LAM_INIT, None, ALU.mult)), "dve")
        wait("s", "dve", n_g2)
        v_p0b = dma("s", s_b2.rearrange("p (h c) -> p h c", h=2),
                d_bet.rearrange("h p c -> p h c"), "dsy")
        wait("v", "dsy", v_p0b)
        n_const = inc("v", V(lambda e: e.tensor_scalar(
            s_b2[:], s_b2[:], 1.0 - LAM_INIT, None, ALU.mult)), "dve")

        dma("g", h_wq.rearrange("p (h c f) -> p h c f", h=2, c=NDC),
            d_wq.rearrange("h c p f -> p h c f"))
        dma("g", h_wk.rearrange("p (h c f) -> p h c f", h=2, c=NDC),
            d_wk.rearrange("h c p f -> p h c f"))
        v_wqkv = dma("g", h_wv.rearrange("p (c f) -> p c f", c=NDC),
                 d_wv.rearrange("c p f -> p c f"))

        xfm_free = [("dve", 0), ("dve", 0)]
        xsq_free = [("pe", 0), ("pe", 0)]
        pt_free = [[("act", 0)], [("act", 0)]]
        at_free = [("dsy", 0), ("dsy", 0)]
        acc_free = 0
        wait("t", "dve", n_const)
        wait("t", "dsy", v_p0b)

        for b in range(NB):
            for tc in range(8):
                sl = tc % 2
                t0 = b * SEQ + tc * 512
                wait("t", "act", C["act"].n)
                wait("t", "dve", C["dve"].n)
                wait("t", "red", C["red"].n)
                kind, val = xfm_free[sl]
                wait("g", kind, val)
                v_x = dma("g", h_xfm[sl].rearrange("p (c t) -> p c t", c=NDC),
                          d_xT[:, t0:t0 + 512].rearrange("(c p) t -> p c t", p=P))
                wait("v", "dg", v_x)
                wait("t", "dg", v_x)
                for dc in range(NDC):
                    par = dc % 2
                    xs = h_xfm[sl][:, dc * 512:(dc + 1) * 512]
                    kind, val = xsq_free[par]
                    wait("v", kind, val)
                    n_sq = inc("v", V(lambda e, xs=xs, par=par: e.tensor_mul(
                        h_xsq[par][:], xs, xs)), "dve")
                    T(lambda e, xs=xs, dc=dc: e.matmul(
                        pbh[6][0:1, 0:512], s_oc[:], xs,
                        start=(dc == 0), stop=(dc == NDC - 1)))
                    wait("t", "dve", n_sq)
                    ins = T(lambda e, par=par, dc=dc: e.matmul(
                        pbh[7][0:1, 0:512], s_oc[:], h_xsq[par][:],
                        start=(dc == 0), stop=(dc == NDC - 1)))
                    n_pe = inc("t", ins, "pe")
                    xsq_free[par] = ("pe", n_pe)
                n_stats = n_pe
                wait("v", "pe", n_stats)
                V(lambda e: e.tensor_scalar(s_row[0:1, 0:512], pbh[6][0:1, 0:512],
                                            1.0 / D, None, ALU.mult))
                V(lambda e: e.tensor_scalar(s_row[0:1, 512:1024], pbh[7][0:1, 0:512],
                                            1.0 / D, None, ALU.mult))
                V(lambda e: e.tensor_mul(s_row[0:1, 1024:1536], s_row[0:1, 0:512],
                                         s_row[0:1, 0:512]))
                n_var = inc("v", V(lambda e: e.tensor_sub(
                    s_row[0:1, 512:1024], s_row[0:1, 512:1024],
                    s_row[0:1, 1024:1536])), "dve")
                wait("c", "dve", n_var)
                A(lambda e: e.activation(s_row[0:1, 1024:1536], s_row[0:1, 512:1024],
                                         AF.Ln, bias=s_er1[:]))
                n_rs = inc("c", A(lambda e: e.activation(
                    s_row[0:1, 512:1024], s_row[0:1, 1024:1536], AF.Exp,
                    scale=-0.5)), "act")
                wait("v", "act", n_rs)
                V(lambda e: e.tensor_copy(s_rwb[0:1, 0:512], s_row[0:1, 512:1024]))
                V(lambda e: e.tensor_mul(s_row[0:1, 1024:1536], s_row[0:1, 0:512],
                                         s_row[0:1, 512:1024]))
                n_rows = inc("v", V(lambda e: e.tensor_scalar(
                    s_rwb[0:1, 512:1024], s_row[0:1, 1024:1536], -1.0, None,
                    ALU.mult)), "dve")
                wait("t", "dve", n_rows)
                T(lambda e: e.matmul(pbh[6][:, 0:512], s_or[:], s_rwb[0:1, 0:512],
                                     start=True, stop=True))
                n_bc = inc("t", T(lambda e: e.matmul(
                    pbh[7][:, 0:512], s_or[:], s_rwb[0:1, 512:1024],
                    start=True, stop=True)), "pe")
                wait("v", "pe", n_bc)
                for dc in range(NDC):
                    xs = h_xfm[sl][:, dc * 512:(dc + 1) * 512]
                    V(lambda e, xs=xs: e.tensor_tensor(xs, xs, pbh[6][:, 0:512],
                                                       ALU.mult))
                    ins = V(lambda e, xs=xs: e.tensor_tensor(
                        xs, xs, pbh[7][:, 0:512], ALU.add))
                    n_ln1 = inc("v", ins, "dve")
                wait("t", "dve", n_ln1)
                if b == 0 and tc == 0:
                    wait("t", "dg", v_wqkv)
                for hl in range(2):
                    for dc in range(NDC):
                        T(lambda e, hl=hl, dc=dc, sl=sl: e.matmul(
                            pbh[hl][:, 0:512],
                            h_wq[:, (hl * NDC + dc) * P:(hl * NDC + dc + 1) * P],
                            h_xfm[sl][:, dc * 512:(dc + 1) * 512],
                            start=(dc == 0), stop=(dc == NDC - 1)))
                    for dc in range(NDC):
                        ins = T(lambda e, hl=hl, dc=dc, sl=sl: e.matmul(
                            pbh[2 + hl][:, 0:512],
                            h_wk[:, (hl * NDC + dc) * P:(hl * NDC + dc + 1) * P],
                            h_xfm[sl][:, dc * 512:(dc + 1) * 512],
                            start=(dc == 0), stop=(dc == NDC - 1)))
                    n_qk = inc("t", ins, "pe")
                for tt in range(4):
                    bank = 4 + tt % 2
                    reg = (tt // 2) * 256
                    for dc in range(NDC):
                        ins = T(lambda e, tt=tt, dc=dc, sl=sl, bank=bank, reg=reg:
                                e.matmul(
                                    pbh[bank][:, reg:reg + 256],
                                    h_xfm[sl][:, dc * 512 + tt * P:dc * 512 + (tt + 1) * P],
                                    h_wv[:, dc * 256:(dc + 1) * 256],
                                    start=(dc == 0 and reg == 0),
                                    stop=(dc == NDC - 1 and reg == 256),
                                    skip_group_check=True))
                    n_v = inc("t", ins, "pe")
                xfm_free[sl] = ("pe", n_v)
                wait("c", "pe", n_qk)
                for hl in range(2):
                    A(lambda e, hl=hl, tc=tc: e.activation(
                        h_q[hl][:, tc * 512:tc * 512 + 512], pbh[hl][:, 0:512],
                        AF.Copy, scale=0.125))
                    n_ev = inc("c", A(lambda e, hl=hl, tc=tc: e.activation(
                        h_k[hl][:, tc * 512:tc * 512 + 512], pbh[2 + hl][:, 0:512],
                        AF.Copy)), "act")
                wait("v", "pe", n_v)
                for tt in range(4):
                    bank = 4 + tt % 2
                    reg = (tt // 2) * 256
                    gt = tc * 4 + tt
                    for hl in range(2):
                        ins = V(lambda e, bank=bank, reg=reg, hl=hl, gt=gt:
                                e.tensor_copy(
                                    h_v[hl][:, gt * VS:gt * VS + P],
                                    pbh[bank][:, reg + hl * P:reg + (hl + 1) * P]))
                        n_vev = inc("v", ins, "dve")

            if debug and b == 0:
                wait("s", "act", C["act"].n)
                wait("s", "dve", C["dve"].n)
                dma("s", d_dbgq[:], h_q[0][:], "dsy")
                dma("s", d_dbgk[:], h_k[0][:], "dsy")
                dma("s", d_dbgv[:], h_v[0][:], "dsy")
            for hl in range(2):
                wait("s", "act", C["act"].n)
                wait("s", "pe", C["pe"].n)
                dma("s", h_q1a[0:64, :], h_q[hl][0:64, :], "dsy")
                dma("s", h_q2a[0:64, :], h_q[hl][64:128, :], "dsy")
                dma("s", h_k1a[0:64, :], h_k[hl][0:64, :], "dsy")
                v_aug = dma("s", h_k2a[0:64, :], h_k[hl][64:128, :], "dsy")
                wait("t", "dsy", v_aug)
                wait("t", "act", C["act"].n)
                wait("t", "dve", C["dve"].n)
                for g in range(NG):
                    for j in range(4):
                        qt = 4 * g + j
                        for kc4 in range(g + 1):
                            N = 512 if kc4 < g else 128 * (j + 1)
                            par = kc4 % 2
                            pa, pb_ = 4 + par, 6 + par
                            for (kk, vv) in pt_free[par]:
                                wait("t", kk, vv)
                            T(lambda e, hl=hl, qt=qt, kc4=kc4, N=N, pa=pa: e.matmul(
                                pbh[pa][:, 0:N],
                                h_q[hl][0:64, qt * P:(qt + 1) * P],
                                h_k[hl][0:64, kc4 * 512:kc4 * 512 + N],
                                start=True, stop=True))
                            n_s = inc("t", T(lambda e, hl=hl, qt=qt, kc4=kc4, N=N,
                                             pb_=pb_: e.matmul(
                                pbh[pb_][:, 0:N],
                                h_q[hl][64:128, qt * P:(qt + 1) * P],
                                h_k[hl][64:128, kc4 * 512:kc4 * 512 + N],
                                start=True, stop=True)), "pe")
                            wait("v", "pe", n_s)
                            if kc4 == g:
                                V(lambda e, pa=pa, j=j: e.tensor_add(
                                    s_ev[:, 0:P], pbh[pa][:, j * P:(j + 1) * P],
                                    s_trin[:]))
                                V(lambda e, pb_=pb_, j=j: e.tensor_add(
                                    s_ev[:, P:2 * P], pbh[pb_][:, j * P:(j + 1) * P],
                                    s_trin[:]))
                                inc("v", V(lambda e, kc4=kc4: e.reduce_max(
                                    s_mxa[:, kc4:kc4 + 1], s_ev[:, 0:P], AX.X)),
                                    "red")
                                n_red = inc("v", V(lambda e, kc4=kc4: e.reduce_max(
                                    s_mxb[:, kc4:kc4 + 1], s_ev[:, P:2 * P], AX.X)),
                                    "red")
                                if j > 0:
                                    inc("v", V(lambda e, pa=pa, j=j, kc4=kc4:
                                               e.reduce_max(
                                        s_mxa[:, kc4 + 1:kc4 + 2],
                                        pbh[pa][:, 0:j * P], AX.X)), "red")
                                    n_red = inc("v", V(lambda e, pb_=pb_, j=j,
                                                       kc4=kc4: e.reduce_max(
                                        s_mxb[:, kc4 + 1:kc4 + 2],
                                        pbh[pb_][:, 0:j * P], AX.X)), "red")
                            else:
                                inc("v", V(lambda e, pa=pa, N=N, kc4=kc4:
                                           e.reduce_max(
                                    s_mxa[:, kc4:kc4 + 1], pbh[pa][:, 0:N], AX.X)),
                                    "red")
                                n_red = inc("v", V(lambda e, pb_=pb_, N=N, kc4=kc4:
                                                   e.reduce_max(
                                    s_mxb[:, kc4:kc4 + 1], pbh[pb_][:, 0:N], AX.X)),
                                    "red")
                            pt_free[par] = [("red", n_red)]
                        tot = (g + 1) + (1 if j > 0 else 0)
                        wait("v", "red", C["red"].n)
                        inc("v", V(lambda e, tot=tot: e.reduce_max(
                            s_mc[:, 0:1], s_mxa[:, 0:tot], AX.X)), "red")
                        n_red = inc("v", V(lambda e, tot=tot: e.reduce_max(
                            s_mc[:, 1:2], s_mxb[:, 0:tot], AX.X)), "red")
                        wait("v", "red", n_red)
                        n_cast = inc("v", V(lambda e: e.tensor_copy(
                            s_mcb[:], s_mc[:])), "dve")
                        wait("t", "dve", n_cast)
                        wait("t", "red", C["red"].n)
                        for (kk, vv) in pt_free[0] + pt_free[1]:
                            wait("t", kk, vv)
                        T(lambda e: e.transpose(pbb[6][0:1, 0:P], s_mcb[:, 0:1],
                                                s_idn[:]))
                        n_tp = inc("t", T(lambda e: e.transpose(
                            pbb[7][0:1, 0:P], s_mcb[:, 1:2], s_idn[:])), "pe")
                        wait("v", "pe", n_tp)
                        V(lambda e, qt=qt: e.tensor_copy(
                            h_q1a[64:65, qt * P:(qt + 1) * P], pbb[6][0:1, 0:P]))
                        n_mrow = inc("v", V(lambda e, qt=qt: e.tensor_copy(
                            h_q2a[64:65, qt * P:(qt + 1) * P], pbb[7][0:1, 0:P])),
                            "dve")
                        pt_free[0] = [("dve", n_mrow)]
                        pt_free[1] = [("dve", n_mrow)]
                    kmax = 4 * g + 4
                    ready_l = {}

                    def emit_spexp(kc, g=g):
                        par = kc % 2
                        sb1, sb2 = 4 + par, 6 + par
                        for (kk, vv) in pt_free[par]:
                            wait("t", kk, vv)
                        if kc == 0:
                            wait("t", "dve", n_mrow)
                        T(lambda e, kc=kc, sb1=sb1, g=g: e.matmul(
                            pbh[sb1][:, 0:512], h_k1a[:, kc * P:(kc + 1) * P],
                            h_q1a[:, g * 512:(g + 1) * 512], start=True, stop=True))
                        n_s = inc("t", T(lambda e, kc=kc, sb2=sb2, g=g: e.matmul(
                            pbh[sb2][:, 0:512], h_k2a[:, kc * P:(kc + 1) * P],
                            h_q2a[:, g * 512:(g + 1) * 512], start=True, stop=True)),
                            "pe")
                        wait("c", "pe", n_s)
                        A(lambda e, par=par, sb1=sb1: e.activation(
                            h_p1t[par][:], pbh[sb1][:, 0:512], AF.Exp))
                        n_e = inc("c", A(lambda e, par=par, sb2=sb2: e.activation(
                            h_p2t[par][:], pbh[sb2][:, 0:512], AF.Exp)), "act")
                        pt_free[par] = [("act", n_e)]
                        ready = ("act", n_e)
                        if kc >= 4 * g:
                            o = kc - 4 * g
                            wait("v", "act", n_e)
                            V(lambda e, par=par, o=o: e.tensor_mul(
                                h_p1t[par][:, o * P:(o + 1) * P],
                                h_p1t[par][:, o * P:(o + 1) * P], s_tri[:]))
                            n_mk = inc("v", V(lambda e, par=par, o=o: e.tensor_mul(
                                h_p2t[par][:, o * P:(o + 1) * P],
                                h_p2t[par][:, o * P:(o + 1) * P], s_tri[:])), "dve")
                            ready = ("dve", n_mk)
                        ready_l[kc] = ready

                    def emit_pv(kc, g=g, hl=hl, kmax=kmax):
                        par = kc % 2
                        ready = ready_l[kc]
                        wait("t", ready[0], ready[1])
                        if kc == 0:
                            wait("t", "dve", acc_free)
                        last_pv = None
                        for j in range(max(0, kc - 4 * g), 4):
                            for sd in range(2):
                                pt = h_p1t[par] if sd == 0 else h_p2t[par]
                                last_pv = T(lambda e, j=j, sd=sd, pt=pt, hl=hl,
                                            kc=kc, kmax=kmax: e.matmul(
                                    pbh[j][:, sd * 256:sd * 256 + 129],
                                    pt[:, j * P:(j + 1) * P],
                                    h_v[hl][:, kc * VS:kc * VS + 129],
                                    start=(kc == 0 and sd == 0),
                                    stop=(kc == kmax - 1 and sd == 1),
                                    skip_group_check=True))
                        return inc("t", last_pv, "pe")

                    emit_spexp(0)
                    for kc in range(kmax):
                        if kc + 1 < kmax:
                            emit_spexp(kc + 1)
                        n_pv = emit_pv(kc)
                    wait("v", "pe", n_pv)
                    if debug and b == 0 and hl == 0 and g == 0:
                        n_dbg = inc("v", V(lambda e: e.tensor_copy(
                            s_ev[:, 0:4 * P], pbh[0][:, 0:4 * P])), "dve")
                        wait("s", "dve", n_dbg)
                        dma("s", d_dbgp[:], s_ev[:, 0:4 * P].bitcast(F32)[:, 0:4 * P]
                            if False else s_ev[:, 0:4 * P], "dsy")
                        dma("s", d_dbgp1[:], h_p1t[1][:], "dsy")
                        wait("v", "dsy", C["dsy"].n)
                    for j in range(4):
                        V(lambda e, j=j: e.reciprocal(s_st1[:, 0:1],
                                                      pbh[j][:, 128:129]))
                        n_rc = inc("v", V(lambda e, j=j: e.reciprocal(
                            s_st1[:, 1:2], pbh[j][:, 384:385])), "red")
                        wait("v", "red", n_rc)
                        V(lambda e, j=j: e.tensor_scalar(
                            s_ev[:, 0:P], pbh[j][:, 0:P], s_st1[:, 0:1], None,
                            ALU.mult))
                        V(lambda e, j=j: e.tensor_scalar(
                            s_ev[:, P:2 * P], pbh[j][:, 256:256 + P], s_st1[:, 1:2],
                            None, ALU.mult))
                        V(lambda e: e.tensor_mul(s_ev[:, 2 * P:3 * P],
                                                 s_ev[:, P:2 * P], s_lam[:]))
                        V(lambda e: e.tensor_sub(s_ev[:, 0:P], s_ev[:, 0:P],
                                                 s_ev[:, 2 * P:3 * P]))
                        n_r1 = inc("v", V(lambda e: e.tensor_reduce(
                            s_st1[:, 2:3], s_ev[:, 0:P], AX.X, ALU.add)), "red")
                        wait("v", "red", n_r1)
                        n_mu = inc("v", V(lambda e: e.tensor_scalar(
                            s_st1[:, 2:3], s_st1[:, 2:3], 1.0 / P, None, ALU.mult)),
                            "red")
                        wait("v", "red", n_mu)
                        V(lambda e: e.tensor_scalar(s_ev[:, P:2 * P], s_ev[:, 0:P],
                                                    s_st1[:, 2:3], None,
                                                    ALU.subtract))
                        n_r2 = inc("v", V(lambda e: e.scalar_tensor_tensor(
                            s_ev[:, 2 * P:3 * P], s_ev[:, P:2 * P], 1.0,
                            s_ev[:, P:2 * P], ALU.bypass, ALU.mult,
                            accum_out=s_st1[:, 3:4])), "red")
                        wait("c", "red", n_r2)
                        A(lambda e: e.activation(s_st1[:, 4:5], s_st1[:, 3:4],
                                                 AF.Ln, bias=s_eln[:],
                                                 scale=1.0 / P))
                        n_gr = inc("c", A(lambda e: e.activation(
                            s_st1[:, 5:6], s_st1[:, 4:5], AF.Exp, scale=-0.5)),
                            "act")
                        wait("v", "act", n_gr)
                        kk, vv = at_free[j % 2]
                        wait("v", kk, vv)
                        V(lambda e: e.tensor_scalar(
                            s_ev[:, 3 * P:4 * P], s_ev[:, P:2 * P], s_st1[:, 5:6],
                            None, ALU.mult))
                        V(lambda e, hl=hl: e.tensor_mul(
                            s_ev[:, 3 * P:4 * P], s_ev[:, 3 * P:4 * P],
                            s_g2[:, hl * P:(hl + 1) * P]))
                        ins = V(lambda e, hl=hl, j=j: e.tensor_tensor(
                            h_at[j % 2][:], s_ev[:, 3 * P:4 * P],
                            s_b2[:, hl * P:(hl + 1) * P], ALU.add))
                        n_gn = inc("v", ins, "dve")
                        acc_free = n_gn
                        wait("s", "dve", n_gn)
                        if debug and b == 0 and hl == 0 and g == 0 and j == 0:
                            dma("s", d_dbge[:], s_ev[:], "dsy")
                            dma("s", d_dbgs[:], s_st1[:], "dsy")
                            dma("s", d_dbgh[:], h_at[0][:], "dsy")
                            dma("s", d_dbgb[:], s_b2[:], "dsy")
                        row0 = b * SEQ + g * 512 + j * P
                        v_at = dma("s", d_agin[row0:row0 + P,
                                               hl * P:(hl + 1) * P],
                                   h_at[j % 2][:], "dsy")
                        if debug:
                            v_at = dma("s", d_dbga[row0:row0 + P,
                                                   hl * P:(hl + 1) * P],
                                       h_at[j % 2][:], "dsy")
                        at_free[j % 2] = ("dsy", v_at)

        wait("g", "dsy", C["dsy"].n)
        n_coll = inc("g", G(lambda e: e.collective_compute(
            "AllGather", ALU.bypass, replica_groups=[list(range(NCORE))],
            ins=[d_agin.opt()], outs=[d_agout.opt()])), "coll")


        for w_ in ("g", "v", "t", "c", "s"):
            wait(w_, "coll", n_coll)

        if debug:
            dma("g", h_atm[0][:, 0:256], d_agout[6 * TOK:6 * TOK + P, :])
            c_ = C["ga0"]
            if en == "g":
                eng.indirect_dma_start(
                    h_atm[0][:, 256:512], None, d_agout[:],
                    bass.IndirectOffsetOnAxis(s_aix[:, 6:7], 0)).then_inc(c_.sem, 16)
            c_.n += 16
            wait("s", "dg", C["dg"].n)
            wait("s", "ga0", C["ga0"].n)
            dma("s", d_dbgo[:], h_atm[0][:, 0:512], "dsy")
            wait("g", "dsy", C["dsy"].n)
        atm_free = [("dve", 0), ("dve", 0)]
        for tt in range(8):
            sl = tt % 2
            kind, val = atm_free[sl]
            wait("g", kind, val)
            for r in range(NCORE):
                i_col = tt * 8 + r
                c_ = C["ga0" if sl == 0 else "ga1"]
                if en == "g":
                    eng.indirect_dma_start(
                        h_atm[sl][:, r * 256:(r + 1) * 256], None,
                        d_agout[:], bass.IndirectOffsetOnAxis(
                            s_aix[:, i_col:i_col + 1], 0),
                    ).then_inc(c_.sem, 16)
                c_.n += 16
                v_a = c_.n
            wait("t", "ga0" if sl == 0 else "ga1", v_a)
            if debug and tt == 0:
                wait("s", "ga0", v_a)
                dma("s", d_dbgt[:], h_atm[0][:], "dsy")
                wait("t", "dsy", C["dsy"].n)
            for r in range(NCORE):
                for half in range(2):
                    dch = 2 * r + half
                    bank = dch % 2
                    n_tp = inc("t", T(lambda e, sl=sl, r=r, half=half, bank=bank:
                                      e.transpose(
                                          pbb[4 + bank][:, 0:P],
                                          h_atm[sl][:, r * 256 + half * P:
                                                    r * 256 + (half + 1) * P],
                                          s_idn[:])), "pe")
                    wait("v", "pe", n_tp)
                    n_cp = inc("v", V(lambda e, dch=dch, tt=tt, bank=bank:
                                      e.tensor_copy(
                                          h_afm[:, dch * OWN + tt * P:
                                                dch * OWN + (tt + 1) * P],
                                          pbb[4 + bank][:, 0:P])), "dve")
                    wait("t", "dve", n_cp)
            atm_free[sl] = ("dve", n_cp)
            if debug and tt == 0:
                wait("s", "dve", C["dve"].n)
                dma("s", d_dbgf0[:], h_afm[:], "dsy")
                wait("v", "dsy", C["dsy"].n)
        n_afm = C["dve"].n

        wait("t", "dve", n_afm)
        wcb_free = [("pe", 0), ("pe", 0)]
        bk4b = [0, 0]
        for jc in range(4):
            sl = jc % 2
            kind, val = wcb_free[sl]
            wait("g", kind, val)
            v_w = dma("g", h_wcb[sl].rearrange("p (c f) -> p c f", c=NDC),
                      d_woT[jc].rearrange("c p f -> p c f"))
            wait("t", "dg", v_w)
            for tt in range(8):
                wait("t", "dve", bk4b[tt % 2])
                for ic in range(NDC):
                    ins = T(lambda e, sl=sl, ic=ic, tt=tt: e.matmul(
                        pbh[tt % 2][:, 0:512],
                        h_afm[:, ic * OWN + tt * P:ic * OWN + (tt + 1) * P],
                        h_wcb[sl][:, ic * 512:(ic + 1) * 512],
                        start=(ic == 0), stop=(ic == NDC - 1)))
                n_mm = inc("t", ins, "pe")
                wait("v", "pe", n_mm)
                n_ev = inc("v", V(lambda e, tt=tt, jc=jc: e.tensor_copy(
                    h_at2[:, tt * D + jc * 512:tt * D + (jc + 1) * 512],
                    pbh[tt % 2][:, 0:512])), "dve")
                bk4b[tt % 2] = n_ev
            wcb_free[sl] = ("pe", C["pe"].n)
        n_at2 = C["dve"].n

        if debug:
            wait("s", "dve", C["dve"].n)
            dma("s", d_dbg2[:], h_at2[:], "dsy")
            dma("s", d_dbgf[:], h_afm[:], "dsy")
        wait("s", "pe", C["pe"].n)
        wait("s", "dve", C["dve"].n)
        v_xo = dma("s", h_xsb.rearrange("p (t d) -> p t d", t=8),
                d_xow.rearrange("(t p) d -> p t d", p=P), "dsy")
        wait("v", "dsy", v_xo)
        wait("v", "dve", n_at2)
        for tt in range(8):
            at2 = h_at2[:, tt * D:(tt + 1) * D]
            xsl = h_xsb[:, tt * D:(tt + 1) * D]
            n_r = inc("v", V(lambda e, at2=at2: e.tensor_reduce(
                s_st1[:, 0:1], at2, AX.X, ALU.add)), "red")
            wait("v", "red", n_r)
            n_mu = inc("v", V(lambda e: e.tensor_scalar(
                s_st1[:, 0:1], s_st1[:, 0:1], 1.0 / D, None, ALU.mult)), "red")
            wait("v", "red", n_mu)
            V(lambda e, at2=at2: e.tensor_scalar(h_scr[:], at2, s_st1[:, 0:1],
                                                 None, ALU.subtract))
            n_r2 = inc("v", V(lambda e: e.scalar_tensor_tensor(
                h_scr[:], h_scr[:], 1.0, h_scr[:], ALU.bypass, ALU.mult,
                accum_out=s_st1[:, 1:2])), "red")
            wait("c", "red", n_r2)
            A(lambda e: e.activation(s_st1[:, 2:3], s_st1[:, 1:2], AF.Ln,
                                     bias=s_eln[:], scale=1.0 / D))
            n_rs = inc("c", A(lambda e: e.activation(
                s_st1[:, 3:4], s_st1[:, 2:3], AF.Exp, scale=-0.5)), "act")
            wait("v", "act", n_rs)
            V(lambda e, at2=at2: e.tensor_scalar(h_scr[:], at2, s_st1[:, 0:1],
                                                 s_st1[:, 3:4], ALU.subtract,
                                                 ALU.mult))
            n_x1 = inc("v", V(lambda e, xsl=xsl: e.tensor_add(xsl, xsl, h_scr[:])),
                       "dve")
            wait("s", "dve", n_x1)
            dma("s", d_x1sp[tt * P:(tt + 1) * P, :], xsl, "dsy")
            n_r = inc("v", V(lambda e, xsl=xsl: e.tensor_reduce(
                s_st1[:, 4:5], xsl, AX.X, ALU.add)), "red")
            wait("v", "red", n_r)
            n_mu = inc("v", V(lambda e: e.tensor_scalar(
                s_st1[:, 4:5], s_st1[:, 4:5], 1.0 / D, None, ALU.mult)), "red")
            wait("v", "red", n_mu)
            V(lambda e, xsl=xsl: e.tensor_scalar(h_scr[:], xsl, s_st1[:, 4:5],
                                                 None, ALU.subtract))
            n_r2 = inc("v", V(lambda e: e.scalar_tensor_tensor(
                h_scr[:], h_scr[:], 1.0, h_scr[:], ALU.bypass, ALU.mult,
                accum_out=s_st1[:, 5:6])), "red")
            wait("c", "red", n_r2)
            A(lambda e: e.activation(s_st1[:, 6:7], s_st1[:, 5:6], AF.Ln,
                                     bias=s_eln[:], scale=1.0 / D))
            n_rs = inc("c", A(lambda e: e.activation(
                s_st1[:, 7:8], s_st1[:, 6:7], AF.Exp, scale=-0.5)), "act")
            wait("v", "act", n_rs)
            ln2t = h_ln2[:, tt * D:(tt + 1) * D]
            n_l2 = inc("v", V(lambda e, xsl=xsl, ln2t=ln2t: e.tensor_scalar(
                ln2t, xsl, s_st1[:, 4:5], s_st1[:, 7:8], ALU.subtract, ALU.mult)),
                "dve")
            wait("s", "dve", n_l2)
            dma("s", d_ln2sp[tt * P:(tt + 1) * P, :], ln2t, "dsy")
            n_sc = inc("v", V(lambda e, ln2t=ln2t: e.scalar_tensor_tensor(
                h_scr[:], ln2t, 1.0, s_wrt[:], ALU.bypass, ALU.mult,
                accum_out=s_st2[:, 0:1])), "red")
            wait("v", "red", n_sc)
            n_sc2 = inc("v", V(lambda e: e.tensor_tensor(
                s_st2[:, 1:2], s_st2[:, 0:1], s_brt[:], ALU.add)), "dve")
            wait("c", "dve", n_sc2)
            n_sg = inc("c", A(lambda e: e.sign(s_st2[:, 2:3], s_st2[:, 1:2])), "act")
            wait("v", "act", n_sg)
            inc("v", V(lambda e, tt=tt: e.tensor_scalar_max(
                s_rmask[:, tt:tt + 1], s_st2[:, 2:3], 0.0)), "dve")

        for tt in range(8):
            ln2t = h_ln2[:, tt * D:(tt + 1) * D]
            n_r = inc("v", V(lambda e, ln2t=ln2t: e.scalar_tensor_tensor(
                h_scr[:], ln2t, 1.0, ln2t, ALU.bypass, ALU.mult,
                accum_out=s_st1[:, 0:1])), "red")
            wait("c", "red", n_r)
            A(lambda e: e.activation(s_st1[:, 1:2], s_st1[:, 0:1], AF.Ln,
                                     bias=s_erm[:], scale=1.0 / D))
            n_rs = inc("c", A(lambda e: e.activation(
                s_st1[:, 2:3], s_st1[:, 1:2], AF.Exp, scale=-0.5)), "act")
            wait("v", "act", n_rs)
            n_h1 = inc("v", V(lambda e, ln2t=ln2t: e.tensor_scalar(
                h_scr2[:], ln2t, s_st1[:, 2:3], None, ALU.mult)), "dve")
            wait("t", "dve", n_h1)
            for dc in range(NDC):
                bank = dc % 2
                n_tp = inc("t", T(lambda e, dc=dc, bank=bank: e.transpose(
                    pbb[4 + bank][:, 0:P], h_scr2[:, dc * P:(dc + 1) * P],
                    s_idn[:])), "pe")
                wait("v", "pe", n_tp)
                n_cp = inc("v", V(lambda e, dc=dc, tt=tt, bank=bank: e.tensor_scalar(
                    h_h1f[:, dc * OWN + tt * P:dc * OWN + (tt + 1) * P],
                    pbb[4 + bank][:, 0:P], s_n1[:, dc:dc + 1], None, ALU.mult)),
                    "dve")
                wait("t", "dve", n_cp)
        n_h1f = C["dve"].n

        wait("t", "dve", n_h1f)
        wst_free = [("dve", C["dve"].n), ("dve", C["dve"].n)]
        bk4e = [0, 0]
        for ec in range(NDC):
            sl = ec % 2
            kind, val = wst_free[sl]
            wait("g", kind, val)
            v_w = dma("g", h_wstA[sl].rearrange("p (c f) -> p c f", c=NDC),
                      d_wvf[ec].rearrange("c p f -> p c f"))
            wait("t", "dg", v_w)
            for th in range(2):
                wait("t", "dve", bk4e[th])
                for dc in range(NDC):
                    ins = T(lambda e, sl=sl, dc=dc, th=th: e.matmul(
                        pbh[th][:, 0:512], h_wstA[sl][:, dc * P:(dc + 1) * P],
                        h_h1f[:, dc * OWN + th * 512:dc * OWN + (th + 1) * 512],
                        start=(dc == 0), stop=(dc == NDC - 1)))
                n_mm = inc("t", ins, "pe")
                wait("v", "pe", n_mm)
                n_ev = inc("v", V(lambda e, ec=ec, th=th: e.tensor_scalar(
                    h_vfm[:, ec * OWN + th * 512:ec * OWN + (th + 1) * 512],
                    pbh[th][:, 0:512], s_bv[:, ec:ec + 1], None, ALU.add)), "dve")
                bk4e[th] = n_ev
            wst_free[sl] = ("dve", C["dve"].n)
        n_vfm = C["dve"].n

        wait("t", "dve", n_vfm)
        wcb_free = [("pe", 0), ("pe", 0)]
        bk4f = [0, 0]
        for jc in range(4):
            sl = jc % 2
            kind, val = wcb_free[sl]
            wait("g", kind, val)
            v_w = dma("g", h_wcb[sl].rearrange("p (c f) -> p c f", c=NDC),
                      d_wof[jc].rearrange("c p f -> p c f"))
            wait("t", "dg", v_w)
            for tt in range(8):
                wait("t", "dve", bk4f[tt % 2])
                for ec in range(NDC):
                    ins = T(lambda e, sl=sl, ec=ec, tt=tt: e.matmul(
                        pbh[tt % 2][:, 0:512],
                        h_vfm[:, ec * OWN + tt * P:ec * OWN + (tt + 1) * P],
                        h_wcb[sl][:, ec * 512:(ec + 1) * 512],
                        start=(ec == 0), stop=(ec == NDC - 1)))
                n_mm = inc("t", ins, "pe")
                wait("v", "pe", n_mm)
                n_ev = inc("v", V(lambda e, tt=tt, jc=jc: e.scalar_tensor_tensor(
                    h_h[:, tt * D + jc * 512:tt * D + (jc + 1) * 512],
                    pbh[tt % 2][:, 0:512], 1.0,
                    h_ln2[:, tt * D + jc * 512:tt * D + (jc + 1) * 512],
                    ALU.bypass, ALU.add)), "dve")
                bk4f[tt % 2] = n_ev
            wcb_free[sl] = ("pe", C["pe"].n)

        for tt in range(8):
            hsl = h_h[:, tt * D:(tt + 1) * D]
            n_r = inc("v", V(lambda e, hsl=hsl: e.scalar_tensor_tensor(
                h_scr[:], hsl, 1.0, hsl, ALU.bypass, ALU.mult,
                accum_out=s_st1[:, 0:1])), "red")
            wait("c", "red", n_r)
            A(lambda e: e.activation(s_st1[:, 1:2], s_st1[:, 0:1], AF.Ln,
                                     bias=s_erm[:], scale=1.0 / D))
            n_rs = inc("c", A(lambda e, tt=tt: e.activation(
                s_rstd2[:, tt:tt + 1], s_st1[:, 1:2], AF.Exp, scale=-0.5)), "act")
            wait("s", "red", n_r)
            dma("s", d_hsp[tt * P:(tt + 1) * P, :], hsl, "dsy")
        n_rstd2 = C["act"].n
        v_hsp = C["dsy"].n

        wait("v", "act", n_rstd2)
        wait("g", "dsy", v_hsp)
        wait("g", "dve", C["dve"].n)
        gu_free = [("dve", C["dve"].n), ("dve", C["dve"].n)]
        dw_free = [("pe", C["pe"].n), ("pe", C["pe"].n)]
        bk4h = [0, 0]
        bkgu = [0, 0]
        ss_free = [0, 0]
        l2p_free = [("dve", 0), ("dve", 0)]
        hp_free = [("dve", 0), ("dve", 0)]
        for sl4 in range(2):
            for ti in range(4):
                tt = sl4 * 4 + ti
                wait("s", "dve", C["dve"].n)
                wait("s", "dsy", v_hsp)
                v_h = dma("s", h_scr2[:], d_hsp[tt * P:(tt + 1) * P, :], "dsy")
                wait("v", "dsy", v_h)
                n_h2 = inc("v", V(lambda e, tt=tt: e.tensor_scalar(
                    h_scr2[:], h_scr2[:], s_rstd2[:, tt:tt + 1],
                    None, ALU.mult)), "dve")
                wait("t", "dve", n_h2)
                for dc in range(NDC):
                    bank = dc % 2
                    n_tp = inc("t", T(lambda e, dc=dc, bank=bank: e.transpose(
                        pbb[4 + bank][:, 0:P], h_scr2[:, dc * P:(dc + 1) * P],
                        s_idn[:])), "pe")
                    wait("v", "pe", n_tp)
                    n_cp = inc("v", V(lambda e, dc=dc, ti=ti, bank=bank:
                                      e.tensor_scalar(
                                          h_h2f[:, dc * 512 + ti * P:
                                                dc * 512 + (ti + 1) * P],
                                          pbb[4 + bank][:, 0:P], s_n2[:, dc:dc + 1],
                                          None, ALU.mult)), "dve")
                    wait("t", "dve", n_cp)
            n_h2f = C["dve"].n
            wait("t", "dve", n_h2f)
            for fc in range(NFC):
                slg = fc % 2
                kind, val = gu_free[slg]
                wait("g", kind, val)
                dma("g", h_wstB[slg].rearrange("p (c f) -> p c f", c=NDC),
                    d_gate[fc].rearrange("c p f -> p c f"))
                v_w = dma("g", h_wstB[2 + slg].rearrange("p (c f) -> p c f", c=NDC),
                          d_up[fc].rearrange("c p f -> p c f"))
                wait("t", "dg", v_w)
                wait("t", "act", bkgu[0])
                wait("t", "dve", bkgu[1])
                for dc in range(NDC):
                    T(lambda e, slg=slg, dc=dc: e.matmul(
                        pbh[4][:, 0:512], h_wstB[slg][:, dc * P:(dc + 1) * P],
                        h_h2f[:, dc * 512:(dc + 1) * 512],
                        start=(dc == 0), stop=(dc == NDC - 1)))
                for dc in range(NDC):
                    ins = T(lambda e, slg=slg, dc=dc: e.matmul(
                        pbh[5][:, 0:512], h_wstB[2 + slg][:, dc * P:(dc + 1) * P],
                        h_h2f[:, dc * 512:(dc + 1) * 512],
                        start=(dc == 0), stop=(dc == NDC - 1)))
                n_mm = inc("t", ins, "pe")
                wait("c", "pe", n_mm)
                n_si = inc("c", A(lambda e: e.activation(
                    h_scr2[:, 0:512], pbh[4][:, 0:512], AF.Silu)), "act")
                wait("v", "act", n_si)
                n_ga = inc("v", V(lambda e, fc=fc: e.tensor_tensor(
                    h_gact[:, fc * 512:(fc + 1) * 512], h_scr2[:, 0:512],
                    pbh[5][:, 0:512], ALU.mult)), "dve")
                gu_free[slg] = ("dve", n_ga)
                bkgu[0] = n_si
                bkgu[1] = n_ga
            n_gact = C["dve"].n
            wait("t", "dve", n_gact)
            wait("s", "dsy", v_hsp)
            for jc in range(8):
                sld = jc % 2
                kind, val = dw_free[sld]
                wait("g", kind, val)
                v_w = dma("g", h_dwn[sld].rearrange("p (c f) -> p c f", c=NFC),
                          d_down[jc].rearrange("c p f -> p c f"))
                kind, val = l2p_free[sld]
                wait("s", kind, val)
                v_l2 = dma("s", h_l2p[sld].rearrange("p (t d) -> p t d", t=4),
                           d_ln2sp[sl4 * 512:sl4 * 512 + 512, jc * 256:(jc + 1) * 256]
                           .rearrange("(t p) d -> p t d", p=P), "dsy")
                kind, val = hp_free[sld]
                wait("s", kind, val)
                v_hp = dma("s", h_hp[sld].rearrange("p (t d) -> p t d", t=4),
                           d_hsp[sl4 * 512:sl4 * 512 + 512, jc * 256:(jc + 1) * 256]
                           .rearrange("(t p) d -> p t d", p=P), "dsy")
                wait("t", "dg", v_w)
                wait("v", "dsy", v_hp)
                for ti in range(4):
                    tt = sl4 * 4 + ti
                    wait("t", "dve", bk4h[ti % 2])
                    for fc in range(NFC):
                        ins = T(lambda e, sld=sld, fc=fc, ti=ti: e.matmul(
                            pbh[ti % 2][:, 0:256],
                            h_gact[:, fc * 512 + ti * P:fc * 512 + (ti + 1) * P],
                            h_dwn[sld][:, fc * 256:(fc + 1) * 256],
                            start=(fc == 0), stop=(fc == NFC - 1)))
                    n_mm = inc("t", ins, "pe")
                    wait("v", "pe", n_mm)
                    V(lambda e, ti=ti, sld=sld: e.tensor_sub(
                        h_scr[:, 0:256], h_hp[sld][:, ti * 256:(ti + 1) * 256],
                        h_l2p[sld][:, ti * 256:(ti + 1) * 256]))
                    V(lambda e, ti=ti: e.tensor_add(
                        h_scr[:, 0:256], h_scr[:, 0:256], pbh[ti % 2][:, 0:256]))
                    off = (ti % 2) * 256
                    wait("v", "dsy", ss_free[ti % 2])
                    n_sel = inc("v", V(lambda e, ti=ti, tt=tt, sld=sld, off=off:
                                       e.scalar_tensor_tensor(
                                           h_scr2[:, off:off + 256],
                                           h_scr[:, 0:256],
                                           s_rmask[:, tt:tt + 1],
                                           h_l2p[sld][:, ti * 256:(ti + 1) * 256],
                                           ALU.mult, ALU.add)), "dve")
                    bk4h[ti % 2] = n_sel
                    wait("s", "dve", n_sel)
                    v_ss = dma("s", d_selsp[sl4 * 512 + ti * P:
                                            sl4 * 512 + (ti + 1) * P,
                                            jc * 256:(jc + 1) * 256],
                               h_scr2[:, off:off + 256], "dsy")
                    ss_free[ti % 2] = v_ss
                dw_free[sld] = ("pe", C["pe"].n)
                l2p_free[sld] = ("dve", n_sel)
                hp_free[sld] = ("dve", n_sel)

        wait("s", "dsy", C["dsy"].n)
        x1_free = [("dve", 0), ("dve", 0)]
        sel_done = 0
        for tq in range(2):
            wait("s", "dve", sel_done)
            v_sel = dma("s", h_self.rearrange("p (t d) -> p t d", t=4),
                        d_selsp[tq * 512:tq * 512 + 512, :]
                        .rearrange("(t p) d -> p t d", p=P), "dsy")
            for ti in range(4):
                tt = tq * 4 + ti
                slx = ti % 2
                kind, val = x1_free[slx]
                wait("s", kind, val)
                v_x1 = dma("s", h_x1f[slx][:], d_x1sp[tt * P:(tt + 1) * P, :], "dsy")
                selt = h_self[:, ti * D:(ti + 1) * D]
                wait("v", "dsy", v_sel)
                n_r = inc("v", V(lambda e, selt=selt: e.tensor_reduce(
                    s_st1[:, 0:1], selt, AX.X, ALU.add)), "red")
                wait("v", "red", n_r)
                n_mu = inc("v", V(lambda e: e.tensor_scalar(
                    s_st1[:, 0:1], s_st1[:, 0:1], 1.0 / D, None, ALU.mult)), "red")
                wait("v", "red", n_mu)
                V(lambda e, selt=selt: e.tensor_scalar(h_scr[:], selt,
                                                       s_st1[:, 0:1], None,
                                                       ALU.subtract))
                n_r2 = inc("v", V(lambda e: e.scalar_tensor_tensor(
                    h_scr[:], h_scr[:], 1.0, h_scr[:], ALU.bypass, ALU.mult,
                    accum_out=s_st1[:, 1:2])), "red")
                wait("c", "red", n_r2)
                A(lambda e: e.activation(s_st1[:, 2:3], s_st1[:, 1:2], AF.Ln,
                                         bias=s_eln[:], scale=1.0 / D))
                n_rs = inc("c", A(lambda e: e.activation(
                    s_st1[:, 3:4], s_st1[:, 2:3], AF.Exp, scale=-0.5)), "act")
                wait("v", "act", n_rs)
                V(lambda e, selt=selt: e.tensor_scalar(
                    h_scr[:], selt, s_st1[:, 0:1], s_st1[:, 3:4], ALU.subtract,
                    ALU.mult))
                wait("v", "dsy", v_x1)
                n_out = inc("v", V(lambda e, slx=slx: e.tensor_add(
                    h_x1f[slx][:], h_x1f[slx][:], h_scr[:])), "dve")
                x1_free[slx] = ("dve", n_out)
                sel_done = n_out
                wait("s", "dve", n_out)
                dma("s", d_y[tt * P:(tt + 1) * P, :], h_x1f[slx][:], "out")
        wait("s", "out", C["out"].n)

    with nc.Block() as block:
        block.gpsimd(lambda e: emit("g", e))
        block.sync(lambda e: emit("s", e))
        block.scalar(lambda e: emit("c", e))
        block.tensor(lambda e: emit("t", e))
        block.vector(lambda e: emit("v", e))
    stack.close()
    return nc


_NC_CACHE = None


def kernel(**inputs):
    global _NC_CACHE
    shared, per_core = _prep(inputs)
    if _NC_CACHE is None:
        _NC_CACHE = build()
    in_maps = []
    for c in range(NCORE):
        m = dict(shared)
        m.update(per_core[c])
        in_maps.append(m)
    res = run_bass_kernel_spmd(_NC_CACHE, in_maps, core_ids=list(range(NCORE)))
    y = np.concatenate([res.results[c]["y"] for c in range(NCORE)], axis=0)
    return y.reshape(NB, SEQ, D)


# version 06
# speedup vs baseline: 1.2359x; 1.0174x over previous
import math
import numpy as np
import ml_dtypes
import concourse.bass as bass
import concourse.mybir as mybir
from concourse.bass_utils import run_bass_kernel_spmd

F32 = mybir.dt.float32
BF16 = mybir.dt.bfloat16
AF = mybir.ActivationFunctionType
ALU = mybir.AluOpType
AX = mybir.AxisListType
ds = bass.ds

D = 2048
SEQ = 4096
NB = 2
FF = 5632
P = 128
NCORE = 8
TOK = NB * SEQ
OWN = TOK // NCORE
NDC = D // P
NFC = FF // P
NG = SEQ // 512
LAM_INIT = float(0.8 - 0.6 * math.exp(np.float32(-0.3 * 12)))
EPS_LN = 1e-5
EPS_RMS = 1e-6
VS = 132

bf = ml_dtypes.bfloat16
KB = 1024


def _prep(inp):
    x = np.ascontiguousarray(np.asarray(inp["L0_attn_x"], np.float32).reshape(TOK, D))
    xT = np.ascontiguousarray(x.T)
    Wqkv = np.asarray(inp["L0_attn_W_qkv"], np.float32)
    Wo = np.asarray(inp["L0_attn_W_o"], np.float32)
    woT = np.ascontiguousarray(Wo.T)
    woT_cb = np.ascontiguousarray(woT.reshape(NDC, P, 4, 512).transpose(2, 0, 1, 3))
    wvf = np.asarray(inp["L0_ffn_wv"], np.float32)
    wvf_st = np.ascontiguousarray(wvf.reshape(NDC, P, NDC, P).transpose(2, 0, 1, 3))
    wof = np.asarray(inp["L0_ffn_wo"], np.float32)
    wof_cb = np.ascontiguousarray(wof.reshape(NDC, P, 4, 512).transpose(2, 0, 1, 3))
    gate = np.asarray(inp["L0_ffn_w_gate"], np.float32)
    gate_st = np.ascontiguousarray(gate.reshape(NDC, P, NFC, P).transpose(2, 0, 1, 3))
    up = np.asarray(inp["L0_ffn_w_up"], np.float32)
    up_st = np.ascontiguousarray(up.reshape(NDC, P, NFC, P).transpose(2, 0, 1, 3))
    down = np.asarray(inp["L0_ffn_w_down"], np.float32)
    down_cb = np.ascontiguousarray(down.reshape(NFC, P, 8, 256).transpose(2, 0, 1, 3))

    tile = lambda v: np.ascontiguousarray(
        np.tile(np.asarray(v, np.float32).reshape(1, -1), (P, 1)))
    shared = dict(
        xT=xT, woT_cb=woT_cb, wvf_st=wvf_st, wof_cb=wof_cb,
        gate_st=gate_st, up_st=up_st, down_cb=down_cb,
        lq1=tile(inp["L0_attn_lambda_q1"]), lk1=tile(inp["L0_attn_lambda_k1"]),
        lq2=tile(inp["L0_attn_lambda_q2"]), lk2=tile(inp["L0_attn_lambda_k2"]),
        n1w=np.ascontiguousarray(
            np.asarray(inp["L0_ffn_norm1_w"], np.float32).reshape(NDC, P).T),
        n2w=np.ascontiguousarray(
            np.asarray(inp["L0_ffn_norm2_w"], np.float32).reshape(NDC, P).T),
        bvv=np.ascontiguousarray(
            np.asarray(inp["L0_ffn_bv"], np.float32).reshape(NDC, P).T),
        wrt=np.ascontiguousarray(tile(inp["L0_ffn_w_router"]).astype(bf)),
        brt=np.ascontiguousarray(
            np.full((P, 1), np.asarray(inp["L0_ffn_b_router"], np.float32), np.float32)),
        tri=np.ascontiguousarray(np.triu(np.ones((P, P), dtype=bf))),
        trin=np.ascontiguousarray(
            np.where(np.tril(np.ones((P, P), bool)), 0.0, -1e30).astype(np.float32)),
        ident=np.ascontiguousarray(np.eye(P, dtype=bf)),
    )
    gng = np.asarray(inp["L0_attn_gn_gamma"], np.float32)
    gnb = np.asarray(inp["L0_attn_gn_beta"], np.float32)
    per_core = []
    for c in range(NCORE):
        h0 = 2 * c
        wq_t = np.stack([np.ascontiguousarray(
            Wqkv[P * (h0 + hl):P * (h0 + hl + 1), :].T).reshape(NDC, P, P)
            for hl in range(2)])
        wk_t = np.stack([np.ascontiguousarray(
            Wqkv[2048 + P * (h0 + hl):2048 + P * (h0 + hl + 1), :].T).reshape(NDC, P, P)
            for hl in range(2)])
        wv_t = np.ascontiguousarray(
            Wqkv[4096 + P * h0:4096 + P * (h0 + 2), :].T).reshape(NDC, P, 256)
        gam = np.stack([np.tile(gng[h0 + hl][None, :], (P, 1)) for hl in range(2)])
        bet = np.stack([np.tile(gnb[h0 + hl][None, :], (P, 1)) for hl in range(2)])
        agidx = np.zeros((P, 64), np.uint32)
        for tt in range(8):
            for r in range(NCORE):
                agidx[:, tt * 8 + r] = c * OWN + r * TOK + tt * P + np.arange(P)
        per_core.append(dict(
            wq_t=np.ascontiguousarray(wq_t), wk_t=np.ascontiguousarray(wk_t),
            wv_t=wv_t, gam=np.ascontiguousarray(gam),
            bet=np.ascontiguousarray(bet),
            agidx=np.ascontiguousarray(agidx),
            x_own=np.ascontiguousarray(x[OWN * c:OWN * (c + 1)])))
    return shared, per_core


class Ctr:
    __slots__ = ("sem", "n")

    def __init__(self, sem):
        self.sem = sem
        self.n = 0


def build(debug=False):
    nc = bass.Bass("TRN2", target_bir_lowering=False)

    DT = {}

    def din(name, shape, dtype=F32):
        DT[name] = nc.dram_tensor(name, list(shape), dtype, kind="ExternalInput").ap()
        return DT[name]

    d_xT = din("xT", [D, TOK])
    d_wq = din("wq_t", [2, NDC, P, P])
    d_wk = din("wk_t", [2, NDC, P, P])
    d_wv = din("wv_t", [NDC, P, 256])
    d_woT = din("woT_cb", [4, NDC, P, 512])
    d_wvf = din("wvf_st", [NDC, NDC, P, P])
    d_wof = din("wof_cb", [4, NDC, P, 512])
    d_gate = din("gate_st", [NFC, NDC, P, P])
    d_up = din("up_st", [NFC, NDC, P, P])
    d_down = din("down_cb", [8, NFC, P, 256])
    d_lq1 = din("lq1", [P, P])
    d_lk1 = din("lk1", [P, P])
    d_lq2 = din("lq2", [P, P])
    d_lk2 = din("lk2", [P, P])
    d_gam = din("gam", [2, P, P])
    d_bet = din("bet", [2, P, P])
    d_n1w = din("n1w", [P, NDC])
    d_n2w = din("n2w", [P, NDC])
    d_bvv = din("bvv", [P, NDC])
    d_wrt = din("wrt", [P, D], BF16)
    d_brt = din("brt", [P, 1])
    d_tri = din("tri", [P, P], BF16)
    d_trin = din("trin", [P, P])
    d_idn = din("ident", [P, P], BF16)
    d_xow = din("x_own", [OWN, D])
    d_agix = din("agidx", [P, 64], mybir.dt.uint32)
    d_y = nc.dram_tensor("y", [OWN, D], F32, kind="ExternalOutput").ap()

    dk = "ExternalOutput" if debug else "Internal"
    d_agin = nc.dram_tensor("ag_in", [TOK, 256], BF16, kind="Internal").ap()
    d_dbga = (nc.dram_tensor("dbg_a", [TOK, 256], BF16, kind="ExternalOutput").ap()
              if debug else None)
    if debug:
        d_dbgq = nc.dram_tensor("dbg_q", [P, SEQ], BF16, kind="ExternalOutput").ap()
        d_dbgk = nc.dram_tensor("dbg_k", [P, SEQ], BF16, kind="ExternalOutput").ap()
        d_dbgv = nc.dram_tensor("dbg_v", [P, 32 * VS], BF16, kind="ExternalOutput").ap()
        d_dbg2 = nc.dram_tensor("dbg_at2", [P, 8 * D], BF16, kind="ExternalOutput").ap()
        d_dbgp = nc.dram_tensor("dbg_ps", [P, 512], F32, kind="ExternalOutput").ap()
        d_dbge = nc.dram_tensor("dbg_ev", [P, 4 * P], F32, kind="ExternalOutput").ap()
        d_dbgs = nc.dram_tensor("dbg_st", [P, 16], F32, kind="ExternalOutput").ap()
        d_dbgp1 = nc.dram_tensor("dbg_p1", [P, 512], BF16, kind="ExternalOutput").ap()
        d_dbgh = nc.dram_tensor("dbg_hat", [P, P], BF16, kind="ExternalOutput").ap()
        d_dbgb = nc.dram_tensor("dbg_b2", [P, 2 * P], F32, kind="ExternalOutput").ap()
        d_dbgo = nc.dram_tensor("dbg_ago", [P, 512], BF16, kind="ExternalOutput").ap()
        d_dbgt = nc.dram_tensor("dbg_atm", [P, D], BF16, kind="ExternalOutput").ap()
        d_dbgf0 = nc.dram_tensor("dbg_afm0", [P, NDC * OWN], BF16,
                                 kind="ExternalOutput").ap()
        d_dbgf = nc.dram_tensor("dbg_afm", [P, NDC * OWN], BF16, kind="ExternalOutput").ap()

    d_agout = nc.dram_tensor("ag_out", [NCORE * TOK, 256], BF16, kind="Internal",
                             addr_space="Shared").ap()
    d_x1sp = nc.dram_tensor("x1sp", [OWN, D], F32, kind=dk).ap()
    d_ln2sp = nc.dram_tensor("ln2sp", [OWN, D], BF16, kind=dk).ap()
    d_hsp = nc.dram_tensor("hsp", [OWN, D], BF16, kind=dk).ap()
    d_selsp = nc.dram_tensor("selsp", [OWN, D], BF16, kind=dk).ap()

    pb = [nc.alloc_psum_tensor(f"pb{i}", [P, 512], F32) for i in range(8)]
    pbh = [b.ap() for b in pb]
    pbb = [b.ap().bitcast(BF16) for b in pb]

    def sb(name, shape, dtype):
        return nc.alloc_sbuf_tensor(name, list(shape), dtype).ap()

    s_idn = sb("s_idn", [P, P], BF16)
    s_tri = sb("s_tri", [P, P], BF16)
    s_lam = sb("s_lam", [P, P], F32)
    s_lt = sb("s_lt", [P, 2 * P], F32)
    s_g2 = sb("s_g2", [P, 2 * P], F32)
    s_b2 = sb("s_b2", [P, 2 * P], F32)
    s_n1 = sb("s_n1", [P, NDC], F32)
    s_n2 = sb("s_n2", [P, NDC], F32)
    s_bv = sb("s_bv", [P, NDC], F32)
    s_wrt = sb("s_wrt", [P, D], BF16)
    s_brt = sb("s_brt", [P, 1], F32)
    s_eln = sb("s_eln", [P, 1], F32)
    s_erm = sb("s_erm", [P, 1], F32)
    s_er1 = sb("s_er1", [1, 1], F32)
    s_oc = sb("s_oc", [P, 1], BF16)
    s_or = sb("s_or", [1, P], BF16)
    s_trin = sb("s_trin", [P, P], F32)
    s_mxa = sb("s_mxa", [P, 12], F32)
    s_mxb = sb("s_mxb", [P, 12], F32)
    s_mc = sb("s_mc", [P, 2], F32)
    s_mcb = sb("s_mcb", [P, 2], BF16)
    s_row = sb("s_row", [1, 3 * 512], F32)
    s_rwb = sb("s_rwb", [1, 1024], BF16)
    s_st1 = sb("s_st1", [P, 16], F32)
    s_st2 = sb("s_st2", [P, 8], F32)
    s_rmask = sb("s_rmask", [P, 8], F32)
    s_rstd2 = sb("s_rstd2", [P, 8], F32)
    s_ev = sb("s_ev", [P, 4 * P], F32)
    s_aix = sb("s_aix", [P, 64], mybir.dt.uint32)
    h_scr = sb("scr4", [P, D], F32)
    h_scr2 = sb("scr4b", [P, D], BF16)

    base = nc.sbuf_base
    avail = nc.sbuf_bytes_remaining - 512
    items = []

    def at(name, shape, dtype, off_kb, stages):
        per_part = int(np.prod(shape[1:])) * mybir.dt.size(dtype)
        off = off_kb * KB
        assert off + per_part <= avail, (name, off, per_part, avail)
        items.append((name, off, per_part, frozenset(stages)))
        return nc.alloc_sbuf_tensor_at(name, list(shape), dtype,
                                       offset=base + off).ap()

    P12 = ("p12",)
    h_xfm = [at(f"xfm{i}", [P, NDC * 512], BF16, 0 + 16 * i, P12) for i in range(2)]
    h_wq = at("wqs", [P, 2 * NDC * P], BF16, 32, P12)
    h_wk = at("wks", [P, 2 * NDC * P], BF16, 40, P12)
    h_wv = at("wvs", [P, NDC * 256], BF16, 48, P12)
    h_q = [at(f"qsb{i}", [P, SEQ], BF16, 56 + 8 * i, P12) for i in range(2)]
    h_k = [at(f"ksb{i}", [P, SEQ], BF16, 72 + 8 * i, P12) for i in range(2)]
    h_v = [at(f"vsb{i}", [P, 32 * VS], BF16, 88 + 9 * i, P12) for i in range(2)]
    h_q1a = at("q1a", [65, SEQ], BF16, 106, P12)
    h_q2a = at("q2a", [65, SEQ], BF16, 114, P12)
    h_k1a = at("k1a", [65, SEQ], BF16, 122, P12)
    h_k2a = at("k2a", [65, SEQ], BF16, 130, P12)
    h_p1t = [at(f"p1t{i}", [P, 512], BF16, 138 + i, P12) for i in range(2)]
    h_p2t = [at(f"p2t{i}", [P, 512], BF16, 140 + i, P12) for i in range(2)]
    h_xsq = [at(f"xsq{i}", [P, 512], BF16, 142 + i, P12) for i in range(2)]
    h_at = [at(f"atile{i}", [P, P], BF16, 144 + i, P12) for i in range(2)]

    h_atm = [at(f"a_tm{i}", [P, D], BF16, 0 + 4 * i, ("s4a",)) for i in range(2)]
    h_afm = at("afm", [P, NDC * OWN], BF16, 8, ("s4a", "s4b"))
    h_wcb = [at(f"wcb{i}", [P, NDC * 512], BF16, 40 + 16 * i, ("s4b", "s4f"))
             for i in range(2)]
    h_at2 = at("attn2", [P, 8 * D], BF16, 72, ("s4b", "s4c"))
    h_xsb = at("x_sb", [P, 8 * D], F32, 0, ("s4c",))
    h_ln2 = at("ln2", [P, 8 * D], BF16, 104, ("s4c", "s4d", "s4e", "s4f"))
    h_h1f = at("h1f", [P, NDC * OWN], BF16, 0, ("s4d", "s4e"))
    h_wstA = [at(f"wstA{i}", [P, NDC * P], BF16, 32 + 4 * i, ("s4e",))
              for i in range(2)]
    h_vfm = at("vfm", [P, NDC * OWN], BF16, 72, ("s4e", "s4f"))
    h_h = at("h_tm", [P, 8 * D], BF16, 0, ("s4f", "s4g"))
    h_wstB = [at(f"wstB{i}", [P, NDC * P], BF16, 0 + 4 * i, ("s4h",))
              for i in range(4)]
    h_gact = at("gact", [P, NFC * 512], BF16, 32, ("s4h",))
    h_dwn = [at(f"dwn{i}", [P, NFC * 256], BF16, 76 + 22 * i, ("s4h",))
             for i in range(2)]
    h_h2f = at("h2f", [P, NDC * 512], BF16, 120, ("s4h",))
    h_l2p = [at(f"l2p{i}", [P, 4 * 256], BF16, 136 + 2 * i, ("s4h",)) for i in range(2)]
    h_hp = [at(f"hp{i}", [P, 4 * 256], BF16, 140 + 2 * i, ("s4h",)) for i in range(2)]
    h_self = at("self", [P, 4 * D], BF16, 32, ("s4i",))
    h_x1f = [at(f"x1f{i}", [P, D], F32, 48 + 8 * i, ("s4i",)) for i in range(2)]

    for i, (n1, o1, s1, st1) in enumerate(items):
        for (n2, o2, s2, st2) in items[i + 1:]:
            if st1 & st2 and not (o1 + s1 <= o2 or o2 + s2 <= o1):
                raise AssertionError(f"sbuf overlap {n1} / {n2} in {sorted(st1 & st2)}")

    import contextlib
    stack = contextlib.ExitStack()
    sems = {k: stack.enter_context(nc.semaphore("sem_" + k))
            for k in ("dg", "dsy", "pe", "dve", "act", "red", "coll", "out",
                      "ga0", "ga1")}

    def emit(en, eng):
        C = {k: Ctr(v) for k, v in sems.items()}

        def dma(q, out, in_, ctr="dg", **kw):
            c = C[ctr]
            if en == q:
                eng.dma_start(out=out, in_=in_, **kw).then_inc(c.sem, 16)
            c.n += 16
            return c.n

        def wait(which, ctr, val=None):
            c = C[ctr]
            v = c.n if val is None else val
            if en == which:
                eng.wait_ge(c.sem, v)

        def inc(which, ins, ctr, k=1):
            c = C[ctr]
            if en == which and ins is not None:
                ins.then_inc(c.sem, k)
            c.n += k
            return c.n

        E = (lambda w: (lambda f: f(eng) if en == w else None))
        T, V, A, G, S = E("t"), E("v"), E("c"), E("g"), E("s")

        dma("s", s_idn[:], d_idn[:], "dsy")
        dma("s", s_tri[:], d_tri[:], "dsy")
        dma("s", s_trin[:], d_trin[:], "dsy")
        dma("s", s_lt[:, 0:P], d_lq1[:], "dsy")
        dma("s", s_lt[:, P:2 * P], d_lk1[:], "dsy")
        dma("s", s_g2[:, 0:P], d_lq2[:], "dsy")
        dma("s", s_g2[:, P:2 * P], d_lk2[:], "dsy")
        dma("s", s_n1[:], d_n1w[:], "dsy")
        dma("s", s_n2[:], d_n2w[:], "dsy")
        dma("s", s_bv[:], d_bvv[:], "dsy")
        dma("s", s_wrt[:], d_wrt[:], "dsy")
        dma("s", s_brt[:], d_brt[:], "dsy")
        dma("s", s_aix[:], d_agix[:], "dsy")
        v_p0 = dma("s", s_b2.rearrange("p (h c) -> p h c", h=2),
               d_gam.rearrange("h p c -> p h c"), "dsy")
        wait("v", "dsy", v_p0)
        V(lambda e: e.memset(s_eln[:], EPS_LN))
        V(lambda e: e.memset(s_erm[:], EPS_RMS))
        V(lambda e: e.memset(s_er1[:], EPS_LN))
        V(lambda e: e.memset(s_oc[:], 1.0))
        V(lambda e: e.memset(s_or[:], 1.0))
        V(lambda e: e.memset(h_k1a[64:65, :], -1.0))
        V(lambda e: e.memset(h_k2a[64:65, :], -1.0))
        V(lambda e: e.memset(h_v[0][:], 1.0))
        V(lambda e: e.memset(h_v[1][:], 1.0))
        V(lambda e: e.tensor_mul(s_lam[:], s_lt[:, 0:P], s_lt[:, P:2 * P]))
        n_d = inc("v", V(lambda e: e.tensor_mul(
            s_lt[:, 0:P], s_g2[:, 0:P], s_g2[:, P:2 * P])), "dve")
        wait("c", "dve", n_d)
        A(lambda e: e.activation(s_lam[:], s_lam[:], AF.Exp))
        n_a = inc("c", A(lambda e: e.activation(s_lt[:, 0:P], s_lt[:, 0:P],
                                                AF.Exp)), "act")
        wait("v", "act", n_a)
        V(lambda e: e.tensor_sub(s_lam[:], s_lam[:], s_lt[:, 0:P]))
        V(lambda e: e.tensor_scalar(s_lam[:], s_lam[:], LAM_INIT, None, ALU.add))
        n_g2 = inc("v", V(lambda e: e.tensor_scalar(
            s_g2[:], s_b2[:], 1.0 - LAM_INIT, None, ALU.mult)), "dve")
        wait("s", "dve", n_g2)
        v_p0b = dma("s", s_b2.rearrange("p (h c) -> p h c", h=2),
                d_bet.rearrange("h p c -> p h c"), "dsy")
        wait("v", "dsy", v_p0b)
        n_const = inc("v", V(lambda e: e.tensor_scalar(
            s_b2[:], s_b2[:], 1.0 - LAM_INIT, None, ALU.mult)), "dve")

        dma("g", h_wq.rearrange("p (h c f) -> p h c f", h=2, c=NDC),
            d_wq.rearrange("h c p f -> p h c f"))
        dma("g", h_wk.rearrange("p (h c f) -> p h c f", h=2, c=NDC),
            d_wk.rearrange("h c p f -> p h c f"))
        v_wqkv = dma("g", h_wv.rearrange("p (c f) -> p c f", c=NDC),
                 d_wv.rearrange("c p f -> p c f"))

        xfm_free = [("dve", 0), ("dve", 0)]
        xsq_free = [("pe", 0), ("pe", 0)]
        pt_free = [[("act", 0)], [("act", 0)]]
        at_free = [("dsy", 0), ("dsy", 0)]
        acc_free = 0
        wait("t", "dve", n_const)
        wait("t", "dsy", v_p0b)

        for b in range(NB):
            for tc in range(8):
                sl = tc % 2
                t0 = b * SEQ + tc * 512
                wait("t", "act", C["act"].n)
                wait("t", "dve", C["dve"].n)
                wait("t", "red", C["red"].n)
                kind, val = xfm_free[sl]
                wait("g", kind, val)
                v_x = dma("g", h_xfm[sl].rearrange("p (c t) -> p c t", c=NDC),
                          d_xT[:, t0:t0 + 512].rearrange("(c p) t -> p c t", p=P))
                wait("v", "dg", v_x)
                wait("t", "dg", v_x)
                for dc in range(NDC):
                    par = dc % 2
                    xs = h_xfm[sl][:, dc * 512:(dc + 1) * 512]
                    kind, val = xsq_free[par]
                    wait("v", kind, val)
                    n_sq = inc("v", V(lambda e, xs=xs, par=par: e.tensor_mul(
                        h_xsq[par][:], xs, xs)), "dve")
                    T(lambda e, xs=xs, dc=dc: e.matmul(
                        pbh[6][0:1, 0:512], s_oc[:], xs,
                        start=(dc == 0), stop=(dc == NDC - 1)))
                    wait("t", "dve", n_sq)
                    ins = T(lambda e, par=par, dc=dc: e.matmul(
                        pbh[7][0:1, 0:512], s_oc[:], h_xsq[par][:],
                        start=(dc == 0), stop=(dc == NDC - 1)))
                    n_pe = inc("t", ins, "pe")
                    xsq_free[par] = ("pe", n_pe)
                n_stats = n_pe
                wait("v", "pe", n_stats)
                V(lambda e: e.tensor_scalar(s_row[0:1, 0:512], pbh[6][0:1, 0:512],
                                            1.0 / D, None, ALU.mult))
                V(lambda e: e.tensor_scalar(s_row[0:1, 512:1024], pbh[7][0:1, 0:512],
                                            1.0 / D, None, ALU.mult))
                V(lambda e: e.tensor_mul(s_row[0:1, 1024:1536], s_row[0:1, 0:512],
                                         s_row[0:1, 0:512]))
                n_var = inc("v", V(lambda e: e.tensor_sub(
                    s_row[0:1, 512:1024], s_row[0:1, 512:1024],
                    s_row[0:1, 1024:1536])), "dve")
                wait("c", "dve", n_var)
                A(lambda e: e.activation(s_row[0:1, 1024:1536], s_row[0:1, 512:1024],
                                         AF.Ln, bias=s_er1[:]))
                n_rs = inc("c", A(lambda e: e.activation(
                    s_row[0:1, 512:1024], s_row[0:1, 1024:1536], AF.Exp,
                    scale=-0.5)), "act")
                wait("v", "act", n_rs)
                V(lambda e: e.tensor_copy(s_rwb[0:1, 0:512], s_row[0:1, 512:1024]))
                V(lambda e: e.tensor_mul(s_row[0:1, 1024:1536], s_row[0:1, 0:512],
                                         s_row[0:1, 512:1024]))
                n_rows = inc("v", V(lambda e: e.tensor_scalar(
                    s_rwb[0:1, 512:1024], s_row[0:1, 1024:1536], -1.0, None,
                    ALU.mult)), "dve")
                wait("t", "dve", n_rows)
                T(lambda e: e.matmul(pbh[6][:, 0:512], s_or[:], s_rwb[0:1, 0:512],
                                     start=True, stop=True))
                n_bc = inc("t", T(lambda e: e.matmul(
                    pbh[7][:, 0:512], s_or[:], s_rwb[0:1, 512:1024],
                    start=True, stop=True)), "pe")
                wait("v", "pe", n_bc)
                for dc in range(NDC):
                    xs = h_xfm[sl][:, dc * 512:(dc + 1) * 512]
                    V(lambda e, xs=xs: e.tensor_tensor(xs, xs, pbh[6][:, 0:512],
                                                       ALU.mult))
                    ins = V(lambda e, xs=xs: e.tensor_tensor(
                        xs, xs, pbh[7][:, 0:512], ALU.add))
                    n_ln1 = inc("v", ins, "dve")
                wait("t", "dve", n_ln1)
                if b == 0 and tc == 0:
                    wait("t", "dg", v_wqkv)
                for hl in range(2):
                    for dc in range(NDC):
                        T(lambda e, hl=hl, dc=dc, sl=sl: e.matmul(
                            pbh[hl][:, 0:512],
                            h_wq[:, (hl * NDC + dc) * P:(hl * NDC + dc + 1) * P],
                            h_xfm[sl][:, dc * 512:(dc + 1) * 512],
                            start=(dc == 0), stop=(dc == NDC - 1)))
                    for dc in range(NDC):
                        ins = T(lambda e, hl=hl, dc=dc, sl=sl: e.matmul(
                            pbh[2 + hl][:, 0:512],
                            h_wk[:, (hl * NDC + dc) * P:(hl * NDC + dc + 1) * P],
                            h_xfm[sl][:, dc * 512:(dc + 1) * 512],
                            start=(dc == 0), stop=(dc == NDC - 1)))
                    n_qk = inc("t", ins, "pe")
                for tt in range(4):
                    bank = 4 + tt % 2
                    reg = (tt // 2) * 256
                    for dc in range(NDC):
                        ins = T(lambda e, tt=tt, dc=dc, sl=sl, bank=bank, reg=reg:
                                e.matmul(
                                    pbh[bank][:, reg:reg + 256],
                                    h_xfm[sl][:, dc * 512 + tt * P:dc * 512 + (tt + 1) * P],
                                    h_wv[:, dc * 256:(dc + 1) * 256],
                                    start=(dc == 0 and reg == 0),
                                    stop=(dc == NDC - 1 and reg == 256),
                                    skip_group_check=True))
                    n_v = inc("t", ins, "pe")
                xfm_free[sl] = ("pe", n_v)
                wait("c", "pe", n_qk)
                for hl in range(2):
                    A(lambda e, hl=hl, tc=tc: e.activation(
                        h_q[hl][:, tc * 512:tc * 512 + 512], pbh[hl][:, 0:512],
                        AF.Copy, scale=0.125))
                    n_ev = inc("c", A(lambda e, hl=hl, tc=tc: e.activation(
                        h_k[hl][:, tc * 512:tc * 512 + 512], pbh[2 + hl][:, 0:512],
                        AF.Copy)), "act")
                wait("v", "pe", n_v)
                for tt in range(4):
                    bank = 4 + tt % 2
                    reg = (tt // 2) * 256
                    gt = tc * 4 + tt
                    for hl in range(2):
                        ins = V(lambda e, bank=bank, reg=reg, hl=hl, gt=gt:
                                e.tensor_copy(
                                    h_v[hl][:, gt * VS:gt * VS + P],
                                    pbh[bank][:, reg + hl * P:reg + (hl + 1) * P]))
                        n_vev = inc("v", ins, "dve")

            if debug and b == 0:
                wait("s", "act", C["act"].n)
                wait("s", "dve", C["dve"].n)
                dma("s", d_dbgq[:], h_q[0][:], "dsy")
                dma("s", d_dbgk[:], h_k[0][:], "dsy")
                dma("s", d_dbgv[:], h_v[0][:], "dsy")
            for hl in range(2):
                wait("s", "act", C["act"].n)
                wait("s", "pe", C["pe"].n)
                dma("s", h_q1a[0:64, :], h_q[hl][0:64, :], "dsy")
                dma("s", h_q2a[0:64, :], h_q[hl][64:128, :], "dsy")
                dma("s", h_k1a[0:64, :], h_k[hl][0:64, :], "dsy")
                v_aug = dma("s", h_k2a[0:64, :], h_k[hl][64:128, :], "dsy")
                wait("t", "dsy", v_aug)
                wait("t", "act", C["act"].n)
                wait("t", "dve", C["dve"].n)
                for g in range(NG):
                    for j in range(4):
                        qt = 4 * g + j
                        for kc4 in range(g + 1):
                            N = 512 if kc4 < g else 128 * (j + 1)
                            par = kc4 % 2
                            pa, pb_ = 4 + par, 6 + par
                            for (kk, vv) in pt_free[par]:
                                wait("t", kk, vv)
                            T(lambda e, hl=hl, qt=qt, kc4=kc4, N=N, pa=pa: e.matmul(
                                pbh[pa][:, 0:N],
                                h_q[hl][0:64, qt * P:(qt + 1) * P],
                                h_k[hl][0:64, kc4 * 512:kc4 * 512 + N],
                                start=True, stop=True))
                            n_s = inc("t", T(lambda e, hl=hl, qt=qt, kc4=kc4, N=N,
                                             pb_=pb_: e.matmul(
                                pbh[pb_][:, 0:N],
                                h_q[hl][64:128, qt * P:(qt + 1) * P],
                                h_k[hl][64:128, kc4 * 512:kc4 * 512 + N],
                                start=True, stop=True)), "pe")
                            wait("v", "pe", n_s)
                            if kc4 == g:
                                V(lambda e, pa=pa, j=j: e.tensor_add(
                                    s_ev[:, 0:P], pbh[pa][:, j * P:(j + 1) * P],
                                    s_trin[:]))
                                V(lambda e, pb_=pb_, j=j: e.tensor_add(
                                    s_ev[:, P:2 * P], pbh[pb_][:, j * P:(j + 1) * P],
                                    s_trin[:]))
                                inc("v", V(lambda e, kc4=kc4: e.reduce_max(
                                    s_mxa[:, kc4:kc4 + 1], s_ev[:, 0:P], AX.X)),
                                    "red")
                                n_red = inc("v", V(lambda e, kc4=kc4: e.reduce_max(
                                    s_mxb[:, kc4:kc4 + 1], s_ev[:, P:2 * P], AX.X)),
                                    "red")
                                if j > 0:
                                    inc("v", V(lambda e, pa=pa, j=j, kc4=kc4:
                                               e.reduce_max(
                                        s_mxa[:, kc4 + 1:kc4 + 2],
                                        pbh[pa][:, 0:j * P], AX.X)), "red")
                                    n_red = inc("v", V(lambda e, pb_=pb_, j=j,
                                                       kc4=kc4: e.reduce_max(
                                        s_mxb[:, kc4 + 1:kc4 + 2],
                                        pbh[pb_][:, 0:j * P], AX.X)), "red")
                            else:
                                inc("v", V(lambda e, pa=pa, N=N, kc4=kc4:
                                           e.reduce_max(
                                    s_mxa[:, kc4:kc4 + 1], pbh[pa][:, 0:N], AX.X)),
                                    "red")
                                n_red = inc("v", V(lambda e, pb_=pb_, N=N, kc4=kc4:
                                                   e.reduce_max(
                                    s_mxb[:, kc4:kc4 + 1], pbh[pb_][:, 0:N], AX.X)),
                                    "red")
                            pt_free[par] = [("red", n_red)]
                        tot = (g + 1) + (1 if j > 0 else 0)
                        wait("v", "red", C["red"].n)
                        inc("v", V(lambda e, tot=tot: e.reduce_max(
                            s_mc[:, 0:1], s_mxa[:, 0:tot], AX.X)), "red")
                        n_red = inc("v", V(lambda e, tot=tot: e.reduce_max(
                            s_mc[:, 1:2], s_mxb[:, 0:tot], AX.X)), "red")
                        wait("v", "red", n_red)
                        n_cast = inc("v", V(lambda e: e.tensor_copy(
                            s_mcb[:], s_mc[:])), "dve")
                        wait("t", "dve", n_cast)
                        wait("t", "red", C["red"].n)
                        for (kk, vv) in pt_free[0] + pt_free[1]:
                            wait("t", kk, vv)
                        T(lambda e: e.transpose(pbb[6][0:1, 0:P], s_mcb[:, 0:1],
                                                s_idn[:]))
                        n_tp = inc("t", T(lambda e: e.transpose(
                            pbb[7][0:1, 0:P], s_mcb[:, 1:2], s_idn[:])), "pe")
                        wait("v", "pe", n_tp)
                        V(lambda e, qt=qt: e.tensor_copy(
                            h_q1a[64:65, qt * P:(qt + 1) * P], pbb[6][0:1, 0:P]))
                        n_mrow = inc("v", V(lambda e, qt=qt: e.tensor_copy(
                            h_q2a[64:65, qt * P:(qt + 1) * P], pbb[7][0:1, 0:P])),
                            "dve")
                        pt_free[0] = [("dve", n_mrow)]
                        pt_free[1] = [("dve", n_mrow)]
                    kmax = 4 * g + 4
                    ready_l = {}

                    def emit_spexp(kc, g=g):
                        par = kc % 2
                        sb1, sb2 = 4 + par, 6 + par
                        for (kk, vv) in pt_free[par]:
                            wait("t", kk, vv)
                        if kc == 0:
                            wait("t", "dve", n_mrow)
                        T(lambda e, kc=kc, sb1=sb1, g=g: e.matmul(
                            pbh[sb1][:, 0:512], h_k1a[:, kc * P:(kc + 1) * P],
                            h_q1a[:, g * 512:(g + 1) * 512], start=True, stop=True))
                        n_s = inc("t", T(lambda e, kc=kc, sb2=sb2, g=g: e.matmul(
                            pbh[sb2][:, 0:512], h_k2a[:, kc * P:(kc + 1) * P],
                            h_q2a[:, g * 512:(g + 1) * 512], start=True, stop=True)),
                            "pe")
                        wait("c", "pe", n_s)
                        A(lambda e, par=par, sb1=sb1: e.activation(
                            h_p1t[par][:], pbh[sb1][:, 0:512], AF.Exp))
                        n_e = inc("c", A(lambda e, par=par, sb2=sb2: e.activation(
                            h_p2t[par][:], pbh[sb2][:, 0:512], AF.Exp)), "act")
                        pt_free[par] = [("act", n_e)]
                        ready = ("act", n_e)
                        if kc >= 4 * g:
                            o = kc - 4 * g
                            wait("v", "act", n_e)
                            V(lambda e, par=par, o=o: e.tensor_mul(
                                h_p1t[par][:, o * P:(o + 1) * P],
                                h_p1t[par][:, o * P:(o + 1) * P], s_tri[:]))
                            n_mk = inc("v", V(lambda e, par=par, o=o: e.tensor_mul(
                                h_p2t[par][:, o * P:(o + 1) * P],
                                h_p2t[par][:, o * P:(o + 1) * P], s_tri[:])), "dve")
                            ready = ("dve", n_mk)
                        ready_l[kc] = ready

                    def emit_pv(kc, g=g, hl=hl, kmax=kmax):
                        par = kc % 2
                        ready = ready_l[kc]
                        wait("t", ready[0], ready[1])
                        if kc == 0:
                            wait("t", "dve", acc_free)
                        last_pv = None
                        for j in range(max(0, kc - 4 * g), 4):
                            for sd in range(2):
                                pt = h_p1t[par] if sd == 0 else h_p2t[par]
                                last_pv = T(lambda e, j=j, sd=sd, pt=pt, hl=hl,
                                            kc=kc, kmax=kmax: e.matmul(
                                    pbh[j][:, sd * 256:sd * 256 + 129],
                                    pt[:, j * P:(j + 1) * P],
                                    h_v[hl][:, kc * VS:kc * VS + 129],
                                    start=(kc == 0 and sd == 0),
                                    stop=(kc == kmax - 1 and sd == 1),
                                    skip_group_check=True))
                        return inc("t", last_pv, "pe")

                    emit_spexp(0)
                    for kc in range(kmax):
                        if kc + 1 < kmax:
                            emit_spexp(kc + 1)
                        n_pv = emit_pv(kc)
                    wait("v", "pe", n_pv)
                    if debug and b == 0 and hl == 0 and g == 0:
                        n_dbg = inc("v", V(lambda e: e.tensor_copy(
                            s_ev[:, 0:4 * P], pbh[0][:, 0:4 * P])), "dve")
                        wait("s", "dve", n_dbg)
                        dma("s", d_dbgp[:], s_ev[:, 0:4 * P].bitcast(F32)[:, 0:4 * P]
                            if False else s_ev[:, 0:4 * P], "dsy")
                        dma("s", d_dbgp1[:], h_p1t[1][:], "dsy")
                        wait("v", "dsy", C["dsy"].n)
                    for j in range(4):
                        V(lambda e, j=j: e.reciprocal(s_st1[:, 0:1],
                                                      pbh[j][:, 128:129]))
                        n_rc = inc("v", V(lambda e, j=j: e.reciprocal(
                            s_st1[:, 1:2], pbh[j][:, 384:385])), "red")
                        wait("v", "red", n_rc)
                        V(lambda e, j=j: e.tensor_scalar(
                            s_ev[:, 0:P], pbh[j][:, 0:P], s_st1[:, 0:1], None,
                            ALU.mult))
                        V(lambda e, j=j: e.tensor_scalar(
                            s_ev[:, P:2 * P], pbh[j][:, 256:256 + P], s_st1[:, 1:2],
                            None, ALU.mult))
                        V(lambda e: e.tensor_mul(s_ev[:, 2 * P:3 * P],
                                                 s_ev[:, P:2 * P], s_lam[:]))
                        V(lambda e: e.tensor_sub(s_ev[:, 0:P], s_ev[:, 0:P],
                                                 s_ev[:, 2 * P:3 * P]))
                        n_r1 = inc("v", V(lambda e: e.tensor_reduce(
                            s_st1[:, 2:3], s_ev[:, 0:P], AX.X, ALU.add)), "red")
                        wait("v", "red", n_r1)
                        n_mu = inc("v", V(lambda e: e.tensor_scalar(
                            s_st1[:, 2:3], s_st1[:, 2:3], 1.0 / P, None, ALU.mult)),
                            "red")
                        wait("v", "red", n_mu)
                        V(lambda e: e.tensor_scalar(s_ev[:, P:2 * P], s_ev[:, 0:P],
                                                    s_st1[:, 2:3], None,
                                                    ALU.subtract))
                        n_r2 = inc("v", V(lambda e: e.scalar_tensor_tensor(
                            s_ev[:, 2 * P:3 * P], s_ev[:, P:2 * P], 1.0,
                            s_ev[:, P:2 * P], ALU.bypass, ALU.mult,
                            accum_out=s_st1[:, 3:4])), "red")
                        wait("c", "red", n_r2)
                        A(lambda e: e.activation(s_st1[:, 4:5], s_st1[:, 3:4],
                                                 AF.Ln, bias=s_eln[:],
                                                 scale=1.0 / P))
                        n_gr = inc("c", A(lambda e: e.activation(
                            s_st1[:, 5:6], s_st1[:, 4:5], AF.Exp, scale=-0.5)),
                            "act")
                        wait("v", "act", n_gr)
                        kk, vv = at_free[j % 2]
                        wait("v", kk, vv)
                        V(lambda e: e.tensor_scalar(
                            s_ev[:, 3 * P:4 * P], s_ev[:, P:2 * P], s_st1[:, 5:6],
                            None, ALU.mult))
                        V(lambda e, hl=hl: e.tensor_mul(
                            s_ev[:, 3 * P:4 * P], s_ev[:, 3 * P:4 * P],
                            s_g2[:, hl * P:(hl + 1) * P]))
                        ins = V(lambda e, hl=hl, j=j: e.tensor_tensor(
                            h_at[j % 2][:], s_ev[:, 3 * P:4 * P],
                            s_b2[:, hl * P:(hl + 1) * P], ALU.add))
                        n_gn = inc("v", ins, "dve")
                        acc_free = n_gn
                        wait("s", "dve", n_gn)
                        if debug and b == 0 and hl == 0 and g == 0 and j == 0:
                            dma("s", d_dbge[:], s_ev[:], "dsy")
                            dma("s", d_dbgs[:], s_st1[:], "dsy")
                            dma("s", d_dbgh[:], h_at[0][:], "dsy")
                            dma("s", d_dbgb[:], s_b2[:], "dsy")
                        row0 = b * SEQ + g * 512 + j * P
                        v_at = dma("s", d_agin[row0:row0 + P,
                                               hl * P:(hl + 1) * P],
                                   h_at[j % 2][:], "dsy")
                        if debug:
                            v_at = dma("s", d_dbga[row0:row0 + P,
                                                   hl * P:(hl + 1) * P],
                                       h_at[j % 2][:], "dsy")
                        at_free[j % 2] = ("dsy", v_at)

        wait("g", "dsy", C["dsy"].n)
        n_coll = inc("g", G(lambda e: e.collective_compute(
            "AllGather", ALU.bypass, replica_groups=[list(range(NCORE))],
            ins=[d_agin.opt()], outs=[d_agout.opt()])), "coll")


        for w_ in ("g", "v", "t", "c", "s"):
            wait(w_, "coll", n_coll)

        if debug:
            dma("g", h_atm[0][:, 0:256], d_agout[6 * TOK:6 * TOK + P, :])
            c_ = C["ga0"]
            if en == "g":
                eng.indirect_dma_start(
                    h_atm[0][:, 256:512], None, d_agout[:],
                    bass.IndirectOffsetOnAxis(s_aix[:, 6:7], 0)).then_inc(c_.sem, 16)
            c_.n += 16
            wait("s", "dg", C["dg"].n)
            wait("s", "ga0", C["ga0"].n)
            dma("s", d_dbgo[:], h_atm[0][:, 0:512], "dsy")
            wait("g", "dsy", C["dsy"].n)
        atm_free = [("dve", 0), ("dve", 0)]
        tplag = [0, 0]
        for tt in range(8):
            sl = tt % 2
            kind, val = atm_free[sl]
            wait("g", kind, val)
            for r in range(NCORE):
                i_col = tt * 8 + r
                c_ = C["ga0" if sl == 0 else "ga1"]
                if en == "g":
                    eng.indirect_dma_start(
                        h_atm[sl][:, r * 256:(r + 1) * 256], None,
                        d_agout[:], bass.IndirectOffsetOnAxis(
                            s_aix[:, i_col:i_col + 1], 0),
                    ).then_inc(c_.sem, 16)
                c_.n += 16
                v_a = c_.n
            wait("t", "ga0" if sl == 0 else "ga1", v_a)
            if debug and tt == 0:
                wait("s", "ga0", v_a)
                dma("s", d_dbgt[:], h_atm[0][:], "dsy")
                wait("t", "dsy", C["dsy"].n)
            for r in range(NCORE):
                for half in range(2):
                    dch = 2 * r + half
                    bank = dch % 2
                    wait("t", "dve", tplag[bank])
                    n_tp = inc("t", T(lambda e, sl=sl, r=r, half=half, bank=bank:
                                      e.transpose(
                                          pbb[4 + bank][:, 0:P],
                                          h_atm[sl][:, r * 256 + half * P:
                                                    r * 256 + (half + 1) * P],
                                          s_idn[:])), "pe")
                    wait("v", "pe", n_tp)
                    n_cp = inc("v", V(lambda e, dch=dch, tt=tt, bank=bank:
                                      e.tensor_copy(
                                          h_afm[:, dch * OWN + tt * P:
                                                dch * OWN + (tt + 1) * P],
                                          pbb[4 + bank][:, 0:P])), "dve")
                    tplag[bank] = n_cp
            atm_free[sl] = ("dve", n_cp)
            if debug and tt == 0:
                wait("s", "dve", C["dve"].n)
                dma("s", d_dbgf0[:], h_afm[:], "dsy")
                wait("v", "dsy", C["dsy"].n)
        n_afm = C["dve"].n

        wait("t", "dve", n_afm)
        wcb_free = [("pe", 0), ("pe", 0)]
        bk4b = [0, 0]
        for jc in range(4):
            sl = jc % 2
            kind, val = wcb_free[sl]
            wait("g", kind, val)
            v_w = dma("g", h_wcb[sl].rearrange("p (c f) -> p c f", c=NDC),
                      d_woT[jc].rearrange("c p f -> p c f"))
            wait("t", "dg", v_w)
            for tt in range(8):
                wait("t", "dve", bk4b[tt % 2])
                for ic in range(NDC):
                    ins = T(lambda e, sl=sl, ic=ic, tt=tt: e.matmul(
                        pbh[tt % 2][:, 0:512],
                        h_afm[:, ic * OWN + tt * P:ic * OWN + (tt + 1) * P],
                        h_wcb[sl][:, ic * 512:(ic + 1) * 512],
                        start=(ic == 0), stop=(ic == NDC - 1)))
                n_mm = inc("t", ins, "pe")
                wait("v", "pe", n_mm)
                n_ev = inc("v", V(lambda e, tt=tt, jc=jc: e.tensor_copy(
                    h_at2[:, tt * D + jc * 512:tt * D + (jc + 1) * 512],
                    pbh[tt % 2][:, 0:512])), "dve")
                bk4b[tt % 2] = n_ev
            wcb_free[sl] = ("pe", C["pe"].n)
        n_at2 = C["dve"].n

        if debug:
            wait("s", "dve", C["dve"].n)
            dma("s", d_dbg2[:], h_at2[:], "dsy")
            dma("s", d_dbgf[:], h_afm[:], "dsy")
        wait("s", "pe", C["pe"].n)
        wait("s", "dve", C["dve"].n)
        v_xo = dma("s", h_xsb.rearrange("p (t d) -> p t d", t=8),
                d_xow.rearrange("(t p) d -> p t d", p=P), "dsy")
        wait("v", "dsy", v_xo)
        wait("v", "dve", n_at2)
        for tt in range(8):
            at2 = h_at2[:, tt * D:(tt + 1) * D]
            xsl = h_xsb[:, tt * D:(tt + 1) * D]
            n_r = inc("v", V(lambda e, at2=at2: e.tensor_reduce(
                s_st1[:, 0:1], at2, AX.X, ALU.add)), "red")
            wait("v", "red", n_r)
            n_mu = inc("v", V(lambda e: e.tensor_scalar(
                s_st1[:, 0:1], s_st1[:, 0:1], 1.0 / D, None, ALU.mult)), "red")
            wait("v", "red", n_mu)
            V(lambda e, at2=at2: e.tensor_scalar(h_scr[:], at2, s_st1[:, 0:1],
                                                 None, ALU.subtract))
            n_r2 = inc("v", V(lambda e: e.scalar_tensor_tensor(
                h_scr[:], h_scr[:], 1.0, h_scr[:], ALU.bypass, ALU.mult,
                accum_out=s_st1[:, 1:2])), "red")
            wait("c", "red", n_r2)
            A(lambda e: e.activation(s_st1[:, 2:3], s_st1[:, 1:2], AF.Ln,
                                     bias=s_eln[:], scale=1.0 / D))
            n_rs = inc("c", A(lambda e: e.activation(
                s_st1[:, 3:4], s_st1[:, 2:3], AF.Exp, scale=-0.5)), "act")
            wait("v", "act", n_rs)
            V(lambda e, at2=at2: e.tensor_scalar(h_scr[:], at2, s_st1[:, 0:1],
                                                 s_st1[:, 3:4], ALU.subtract,
                                                 ALU.mult))
            n_x1 = inc("v", V(lambda e, xsl=xsl: e.tensor_add(xsl, xsl, h_scr[:])),
                       "dve")
            wait("s", "dve", n_x1)
            dma("s", d_x1sp[tt * P:(tt + 1) * P, :], xsl, "dsy")
            n_r = inc("v", V(lambda e, xsl=xsl: e.tensor_reduce(
                s_st1[:, 4:5], xsl, AX.X, ALU.add)), "red")
            wait("v", "red", n_r)
            n_mu = inc("v", V(lambda e: e.tensor_scalar(
                s_st1[:, 4:5], s_st1[:, 4:5], 1.0 / D, None, ALU.mult)), "red")
            wait("v", "red", n_mu)
            V(lambda e, xsl=xsl: e.tensor_scalar(h_scr[:], xsl, s_st1[:, 4:5],
                                                 None, ALU.subtract))
            n_r2 = inc("v", V(lambda e: e.scalar_tensor_tensor(
                h_scr[:], h_scr[:], 1.0, h_scr[:], ALU.bypass, ALU.mult,
                accum_out=s_st1[:, 5:6])), "red")
            wait("c", "red", n_r2)
            A(lambda e: e.activation(s_st1[:, 6:7], s_st1[:, 5:6], AF.Ln,
                                     bias=s_eln[:], scale=1.0 / D))
            n_rs = inc("c", A(lambda e: e.activation(
                s_st1[:, 7:8], s_st1[:, 6:7], AF.Exp, scale=-0.5)), "act")
            wait("v", "act", n_rs)
            ln2t = h_ln2[:, tt * D:(tt + 1) * D]
            n_l2 = inc("v", V(lambda e, xsl=xsl, ln2t=ln2t: e.tensor_scalar(
                ln2t, xsl, s_st1[:, 4:5], s_st1[:, 7:8], ALU.subtract, ALU.mult)),
                "dve")
            wait("s", "dve", n_l2)
            dma("s", d_ln2sp[tt * P:(tt + 1) * P, :], ln2t, "dsy")
            n_sc = inc("v", V(lambda e, ln2t=ln2t: e.scalar_tensor_tensor(
                h_scr[:], ln2t, 1.0, s_wrt[:], ALU.bypass, ALU.mult,
                accum_out=s_st2[:, 0:1])), "red")
            wait("v", "red", n_sc)
            n_sc2 = inc("v", V(lambda e: e.tensor_tensor(
                s_st2[:, 1:2], s_st2[:, 0:1], s_brt[:], ALU.add)), "dve")
            wait("c", "dve", n_sc2)
            n_sg = inc("c", A(lambda e: e.sign(s_st2[:, 2:3], s_st2[:, 1:2])), "act")
            wait("v", "act", n_sg)
            inc("v", V(lambda e, tt=tt: e.tensor_scalar_max(
                s_rmask[:, tt:tt + 1], s_st2[:, 2:3], 0.0)), "dve")

        tplag = [0, 0]
        for tt in range(8):
            ln2t = h_ln2[:, tt * D:(tt + 1) * D]
            n_r = inc("v", V(lambda e, ln2t=ln2t: e.scalar_tensor_tensor(
                h_scr[:], ln2t, 1.0, ln2t, ALU.bypass, ALU.mult,
                accum_out=s_st1[:, 0:1])), "red")
            wait("c", "red", n_r)
            A(lambda e: e.activation(s_st1[:, 1:2], s_st1[:, 0:1], AF.Ln,
                                     bias=s_erm[:], scale=1.0 / D))
            n_rs = inc("c", A(lambda e: e.activation(
                s_st1[:, 2:3], s_st1[:, 1:2], AF.Exp, scale=-0.5)), "act")
            wait("v", "act", n_rs)
            n_h1 = inc("v", V(lambda e, ln2t=ln2t: e.tensor_scalar(
                h_scr2[:], ln2t, s_st1[:, 2:3], None, ALU.mult)), "dve")
            wait("t", "dve", n_h1)
            for dc in range(NDC):
                bank = dc % 2
                wait("t", "dve", tplag[bank])
                n_tp = inc("t", T(lambda e, dc=dc, bank=bank: e.transpose(
                    pbb[4 + bank][:, 0:P], h_scr2[:, dc * P:(dc + 1) * P],
                    s_idn[:])), "pe")
                wait("v", "pe", n_tp)
                n_cp = inc("v", V(lambda e, dc=dc, tt=tt, bank=bank: e.tensor_scalar(
                    h_h1f[:, dc * OWN + tt * P:dc * OWN + (tt + 1) * P],
                    pbb[4 + bank][:, 0:P], s_n1[:, dc:dc + 1], None, ALU.mult)),
                    "dve")
                tplag[bank] = n_cp
        n_h1f = C["dve"].n

        wait("t", "dve", n_h1f)
        wst_free = [("dve", C["dve"].n), ("dve", C["dve"].n)]
        bk4e = [0, 0]
        for ec in range(NDC):
            sl = ec % 2
            kind, val = wst_free[sl]
            wait("g", kind, val)
            v_w = dma("g", h_wstA[sl].rearrange("p (c f) -> p c f", c=NDC),
                      d_wvf[ec].rearrange("c p f -> p c f"))
            wait("t", "dg", v_w)
            for th in range(2):
                wait("t", "dve", bk4e[th])
                for dc in range(NDC):
                    ins = T(lambda e, sl=sl, dc=dc, th=th: e.matmul(
                        pbh[th][:, 0:512], h_wstA[sl][:, dc * P:(dc + 1) * P],
                        h_h1f[:, dc * OWN + th * 512:dc * OWN + (th + 1) * 512],
                        start=(dc == 0), stop=(dc == NDC - 1)))
                n_mm = inc("t", ins, "pe")
                wait("v", "pe", n_mm)
                n_ev = inc("v", V(lambda e, ec=ec, th=th: e.tensor_scalar(
                    h_vfm[:, ec * OWN + th * 512:ec * OWN + (th + 1) * 512],
                    pbh[th][:, 0:512], s_bv[:, ec:ec + 1], None, ALU.add)), "dve")
                bk4e[th] = n_ev
            wst_free[sl] = ("dve", C["dve"].n)
        n_vfm = C["dve"].n

        wait("t", "dve", n_vfm)
        wcb_free = [("pe", 0), ("pe", 0)]
        bk4f = [0, 0]
        for jc in range(4):
            sl = jc % 2
            kind, val = wcb_free[sl]
            wait("g", kind, val)
            v_w = dma("g", h_wcb[sl].rearrange("p (c f) -> p c f", c=NDC),
                      d_wof[jc].rearrange("c p f -> p c f"))
            wait("t", "dg", v_w)
            for tt in range(8):
                wait("t", "dve", bk4f[tt % 2])
                for ec in range(NDC):
                    ins = T(lambda e, sl=sl, ec=ec, tt=tt: e.matmul(
                        pbh[tt % 2][:, 0:512],
                        h_vfm[:, ec * OWN + tt * P:ec * OWN + (tt + 1) * P],
                        h_wcb[sl][:, ec * 512:(ec + 1) * 512],
                        start=(ec == 0), stop=(ec == NDC - 1)))
                n_mm = inc("t", ins, "pe")
                wait("v", "pe", n_mm)
                n_ev = inc("v", V(lambda e, tt=tt, jc=jc: e.scalar_tensor_tensor(
                    h_h[:, tt * D + jc * 512:tt * D + (jc + 1) * 512],
                    pbh[tt % 2][:, 0:512], 1.0,
                    h_ln2[:, tt * D + jc * 512:tt * D + (jc + 1) * 512],
                    ALU.bypass, ALU.add)), "dve")
                bk4f[tt % 2] = n_ev
            wcb_free[sl] = ("pe", C["pe"].n)

        for tt in range(8):
            hsl = h_h[:, tt * D:(tt + 1) * D]
            n_r = inc("v", V(lambda e, hsl=hsl: e.scalar_tensor_tensor(
                h_scr[:], hsl, 1.0, hsl, ALU.bypass, ALU.mult,
                accum_out=s_st1[:, 0:1])), "red")
            wait("c", "red", n_r)
            A(lambda e: e.activation(s_st1[:, 1:2], s_st1[:, 0:1], AF.Ln,
                                     bias=s_erm[:], scale=1.0 / D))
            n_rs = inc("c", A(lambda e, tt=tt: e.activation(
                s_rstd2[:, tt:tt + 1], s_st1[:, 1:2], AF.Exp, scale=-0.5)), "act")
            wait("s", "red", n_r)
            dma("s", d_hsp[tt * P:(tt + 1) * P, :], hsl, "dsy")
        n_rstd2 = C["act"].n
        v_hsp = C["dsy"].n

        wait("v", "act", n_rstd2)
        wait("g", "dsy", v_hsp)
        wait("g", "dve", C["dve"].n)
        gu_free = [("dve", C["dve"].n), ("dve", C["dve"].n)]
        dw_free = [("pe", C["pe"].n), ("pe", C["pe"].n)]
        bk4h = [0, 0]
        bkgu = [0, 0]
        ss_free = [0, 0]
        l2p_free = [("dve", 0), ("dve", 0)]
        hp_free = [("dve", 0), ("dve", 0)]
        tplag2 = [0, 0]
        for sl4 in range(2):
            for ti in range(4):
                tt = sl4 * 4 + ti
                wait("s", "dve", C["dve"].n)
                wait("s", "dsy", v_hsp)
                v_h = dma("s", h_scr2[:], d_hsp[tt * P:(tt + 1) * P, :], "dsy")
                wait("v", "dsy", v_h)
                n_h2 = inc("v", V(lambda e, tt=tt: e.tensor_scalar(
                    h_scr2[:], h_scr2[:], s_rstd2[:, tt:tt + 1],
                    None, ALU.mult)), "dve")
                wait("t", "dve", n_h2)
                for dc in range(NDC):
                    bank = dc % 2
                    wait("t", "dve", tplag2[bank])
                    n_tp = inc("t", T(lambda e, dc=dc, bank=bank: e.transpose(
                        pbb[4 + bank][:, 0:P], h_scr2[:, dc * P:(dc + 1) * P],
                        s_idn[:])), "pe")
                    wait("v", "pe", n_tp)
                    n_cp = inc("v", V(lambda e, dc=dc, ti=ti, bank=bank:
                                      e.tensor_scalar(
                                          h_h2f[:, dc * 512 + ti * P:
                                                dc * 512 + (ti + 1) * P],
                                          pbb[4 + bank][:, 0:P], s_n2[:, dc:dc + 1],
                                          None, ALU.mult)), "dve")
                    tplag2[bank] = n_cp
            n_h2f = C["dve"].n
            wait("t", "dve", n_h2f)
            for fc in range(NFC):
                slg = fc % 2
                kind, val = gu_free[slg]
                wait("g", kind, val)
                dma("g", h_wstB[slg].rearrange("p (c f) -> p c f", c=NDC),
                    d_gate[fc].rearrange("c p f -> p c f"))
                v_w = dma("g", h_wstB[2 + slg].rearrange("p (c f) -> p c f", c=NDC),
                          d_up[fc].rearrange("c p f -> p c f"))
                wait("t", "dg", v_w)
                wait("t", "act", bkgu[0])
                wait("t", "dve", bkgu[1])
                for dc in range(NDC):
                    T(lambda e, slg=slg, dc=dc: e.matmul(
                        pbh[4][:, 0:512], h_wstB[slg][:, dc * P:(dc + 1) * P],
                        h_h2f[:, dc * 512:(dc + 1) * 512],
                        start=(dc == 0), stop=(dc == NDC - 1)))
                for dc in range(NDC):
                    ins = T(lambda e, slg=slg, dc=dc: e.matmul(
                        pbh[5][:, 0:512], h_wstB[2 + slg][:, dc * P:(dc + 1) * P],
                        h_h2f[:, dc * 512:(dc + 1) * 512],
                        start=(dc == 0), stop=(dc == NDC - 1)))
                n_mm = inc("t", ins, "pe")
                wait("c", "pe", n_mm)
                n_si = inc("c", A(lambda e: e.activation(
                    h_scr2[:, 0:512], pbh[4][:, 0:512], AF.Silu)), "act")
                wait("v", "act", n_si)
                n_ga = inc("v", V(lambda e, fc=fc: e.tensor_tensor(
                    h_gact[:, fc * 512:(fc + 1) * 512], h_scr2[:, 0:512],
                    pbh[5][:, 0:512], ALU.mult)), "dve")
                gu_free[slg] = ("dve", n_ga)
                bkgu[0] = n_si
                bkgu[1] = n_ga
            n_gact = C["dve"].n
            wait("t", "dve", n_gact)
            wait("s", "dsy", v_hsp)
            for jc in range(8):
                sld = jc % 2
                kind, val = dw_free[sld]
                wait("g", kind, val)
                v_w = dma("g", h_dwn[sld].rearrange("p (c f) -> p c f", c=NFC),
                          d_down[jc].rearrange("c p f -> p c f"))
                kind, val = l2p_free[sld]
                wait("s", kind, val)
                v_l2 = dma("s", h_l2p[sld].rearrange("p (t d) -> p t d", t=4),
                           d_ln2sp[sl4 * 512:sl4 * 512 + 512, jc * 256:(jc + 1) * 256]
                           .rearrange("(t p) d -> p t d", p=P), "dsy")
                kind, val = hp_free[sld]
                wait("s", kind, val)
                v_hp = dma("s", h_hp[sld].rearrange("p (t d) -> p t d", t=4),
                           d_hsp[sl4 * 512:sl4 * 512 + 512, jc * 256:(jc + 1) * 256]
                           .rearrange("(t p) d -> p t d", p=P), "dsy")
                wait("t", "dg", v_w)
                wait("v", "dsy", v_hp)
                for ti in range(4):
                    tt = sl4 * 4 + ti
                    wait("t", "dve", bk4h[ti % 2])
                    for fc in range(NFC):
                        ins = T(lambda e, sld=sld, fc=fc, ti=ti: e.matmul(
                            pbh[ti % 2][:, 0:256],
                            h_gact[:, fc * 512 + ti * P:fc * 512 + (ti + 1) * P],
                            h_dwn[sld][:, fc * 256:(fc + 1) * 256],
                            start=(fc == 0), stop=(fc == NFC - 1)))
                    n_mm = inc("t", ins, "pe")
                    wait("v", "pe", n_mm)
                    V(lambda e, ti=ti, sld=sld: e.tensor_sub(
                        h_scr[:, 0:256], h_hp[sld][:, ti * 256:(ti + 1) * 256],
                        h_l2p[sld][:, ti * 256:(ti + 1) * 256]))
                    V(lambda e, ti=ti: e.tensor_add(
                        h_scr[:, 0:256], h_scr[:, 0:256], pbh[ti % 2][:, 0:256]))
                    off = (ti % 2) * 256
                    wait("v", "dsy", ss_free[ti % 2])
                    n_sel = inc("v", V(lambda e, ti=ti, tt=tt, sld=sld, off=off:
                                       e.scalar_tensor_tensor(
                                           h_scr2[:, off:off + 256],
                                           h_scr[:, 0:256],
                                           s_rmask[:, tt:tt + 1],
                                           h_l2p[sld][:, ti * 256:(ti + 1) * 256],
                                           ALU.mult, ALU.add)), "dve")
                    bk4h[ti % 2] = n_sel
                    wait("s", "dve", n_sel)
                    v_ss = dma("s", d_selsp[sl4 * 512 + ti * P:
                                            sl4 * 512 + (ti + 1) * P,
                                            jc * 256:(jc + 1) * 256],
                               h_scr2[:, off:off + 256], "dsy")
                    ss_free[ti % 2] = v_ss
                dw_free[sld] = ("pe", C["pe"].n)
                l2p_free[sld] = ("dve", n_sel)
                hp_free[sld] = ("dve", n_sel)

        wait("s", "dsy", C["dsy"].n)
        x1_free = [("dve", 0), ("dve", 0)]
        sel_done = 0
        for tq in range(2):
            wait("s", "dve", sel_done)
            v_sel = dma("s", h_self.rearrange("p (t d) -> p t d", t=4),
                        d_selsp[tq * 512:tq * 512 + 512, :]
                        .rearrange("(t p) d -> p t d", p=P), "dsy")
            for ti in range(4):
                tt = tq * 4 + ti
                slx = ti % 2
                kind, val = x1_free[slx]
                wait("s", kind, val)
                v_x1 = dma("s", h_x1f[slx][:], d_x1sp[tt * P:(tt + 1) * P, :], "dsy")
                selt = h_self[:, ti * D:(ti + 1) * D]
                wait("v", "dsy", v_sel)
                n_r = inc("v", V(lambda e, selt=selt: e.tensor_reduce(
                    s_st1[:, 0:1], selt, AX.X, ALU.add)), "red")
                wait("v", "red", n_r)
                n_mu = inc("v", V(lambda e: e.tensor_scalar(
                    s_st1[:, 0:1], s_st1[:, 0:1], 1.0 / D, None, ALU.mult)), "red")
                wait("v", "red", n_mu)
                V(lambda e, selt=selt: e.tensor_scalar(h_scr[:], selt,
                                                       s_st1[:, 0:1], None,
                                                       ALU.subtract))
                n_r2 = inc("v", V(lambda e: e.scalar_tensor_tensor(
                    h_scr[:], h_scr[:], 1.0, h_scr[:], ALU.bypass, ALU.mult,
                    accum_out=s_st1[:, 1:2])), "red")
                wait("c", "red", n_r2)
                A(lambda e: e.activation(s_st1[:, 2:3], s_st1[:, 1:2], AF.Ln,
                                         bias=s_eln[:], scale=1.0 / D))
                n_rs = inc("c", A(lambda e: e.activation(
                    s_st1[:, 3:4], s_st1[:, 2:3], AF.Exp, scale=-0.5)), "act")
                wait("v", "act", n_rs)
                V(lambda e, selt=selt: e.tensor_scalar(
                    h_scr[:], selt, s_st1[:, 0:1], s_st1[:, 3:4], ALU.subtract,
                    ALU.mult))
                wait("v", "dsy", v_x1)
                n_out = inc("v", V(lambda e, slx=slx: e.tensor_add(
                    h_x1f[slx][:], h_x1f[slx][:], h_scr[:])), "dve")
                x1_free[slx] = ("dve", n_out)
                sel_done = n_out
                wait("s", "dve", n_out)
                dma("s", d_y[tt * P:(tt + 1) * P, :], h_x1f[slx][:], "out")
        wait("s", "out", C["out"].n)

    with nc.Block() as block:
        block.gpsimd(lambda e: emit("g", e))
        block.sync(lambda e: emit("s", e))
        block.scalar(lambda e: emit("c", e))
        block.tensor(lambda e: emit("t", e))
        block.vector(lambda e: emit("v", e))
    stack.close()
    return nc


_NC_CACHE = None


def kernel(**inputs):
    global _NC_CACHE
    shared, per_core = _prep(inputs)
    if _NC_CACHE is None:
        _NC_CACHE = build()
    in_maps = []
    for c in range(NCORE):
        m = dict(shared)
        m.update(per_core[c])
        in_maps.append(m)
    res = run_bass_kernel_spmd(_NC_CACHE, in_maps, core_ids=list(range(NCORE)))
    y = np.concatenate([res.results[c]["y"] for c in range(NCORE)], axis=0)
    return y.reshape(NB, SEQ, D)


# version 07 (final)
# speedup vs baseline: 1.2416x; 1.0047x over previous
import math
import numpy as np
import ml_dtypes
import concourse.bass as bass
import concourse.mybir as mybir
from concourse.bass_utils import run_bass_kernel_spmd

F32 = mybir.dt.float32
BF16 = mybir.dt.bfloat16
AF = mybir.ActivationFunctionType
ALU = mybir.AluOpType
AX = mybir.AxisListType
ds = bass.ds

D = 2048
SEQ = 4096
NB = 2
FF = 5632
P = 128
NCORE = 8
TOK = NB * SEQ
OWN = TOK // NCORE
NDC = D // P
NFC = FF // P
NG = SEQ // 512
LAM_INIT = float(0.8 - 0.6 * math.exp(np.float32(-0.3 * 12)))
EPS_LN = 1e-5
EPS_RMS = 1e-6
VS = 132

bf = ml_dtypes.bfloat16
KB = 1024


def _prep(inp):
    x = np.ascontiguousarray(np.asarray(inp["L0_attn_x"], np.float32).reshape(TOK, D))
    xT = np.ascontiguousarray(x.T)
    Wqkv = np.asarray(inp["L0_attn_W_qkv"], np.float32)
    Wo = np.asarray(inp["L0_attn_W_o"], np.float32)
    woT = np.ascontiguousarray(Wo.T)
    woT_cb = np.ascontiguousarray(woT.reshape(NDC, P, 4, 512).transpose(2, 0, 1, 3))
    wvf = np.asarray(inp["L0_ffn_wv"], np.float32)
    wvf_st = np.ascontiguousarray(wvf.reshape(NDC, P, NDC, P).transpose(2, 0, 1, 3))
    wof = np.asarray(inp["L0_ffn_wo"], np.float32)
    wof_cb = np.ascontiguousarray(wof.reshape(NDC, P, 4, 512).transpose(2, 0, 1, 3))
    gate = np.asarray(inp["L0_ffn_w_gate"], np.float32)
    gate_st = np.ascontiguousarray(gate.reshape(NDC, P, NFC, P).transpose(2, 0, 1, 3))
    up = np.asarray(inp["L0_ffn_w_up"], np.float32)
    up_st = np.ascontiguousarray(up.reshape(NDC, P, NFC, P).transpose(2, 0, 1, 3))
    down = np.asarray(inp["L0_ffn_w_down"], np.float32)
    down_cb = np.ascontiguousarray(down.reshape(NFC, P, 8, 256).transpose(2, 0, 1, 3))

    tile = lambda v: np.ascontiguousarray(
        np.tile(np.asarray(v, np.float32).reshape(1, -1), (P, 1)))
    shared = dict(
        xT=xT, woT_cb=woT_cb, wvf_st=wvf_st, wof_cb=wof_cb,
        gate_st=gate_st, up_st=up_st, down_cb=down_cb,
        lq1=tile(inp["L0_attn_lambda_q1"]), lk1=tile(inp["L0_attn_lambda_k1"]),
        lq2=tile(inp["L0_attn_lambda_q2"]), lk2=tile(inp["L0_attn_lambda_k2"]),
        n1w=np.ascontiguousarray(
            np.asarray(inp["L0_ffn_norm1_w"], np.float32).reshape(NDC, P).T),
        n2w=np.ascontiguousarray(
            np.asarray(inp["L0_ffn_norm2_w"], np.float32).reshape(NDC, P).T),
        bvv=np.ascontiguousarray(
            np.asarray(inp["L0_ffn_bv"], np.float32).reshape(NDC, P).T),
        wrt=np.ascontiguousarray(tile(inp["L0_ffn_w_router"]).astype(bf)),
        brt=np.ascontiguousarray(
            np.full((P, 1), np.asarray(inp["L0_ffn_b_router"], np.float32), np.float32)),
        tri=np.ascontiguousarray(np.triu(np.ones((P, P), dtype=bf))),
        trin=np.ascontiguousarray(
            np.where(np.tril(np.ones((P, P), bool)), 0.0, -1e30).astype(np.float32)),
        ident=np.ascontiguousarray(np.eye(P, dtype=bf)),
    )
    gng = np.asarray(inp["L0_attn_gn_gamma"], np.float32)
    gnb = np.asarray(inp["L0_attn_gn_beta"], np.float32)
    per_core = []
    for c in range(NCORE):
        h0 = 2 * c
        wq_t = np.stack([np.ascontiguousarray(
            Wqkv[P * (h0 + hl):P * (h0 + hl + 1), :].T).reshape(NDC, P, P)
            for hl in range(2)])
        wk_t = np.stack([np.ascontiguousarray(
            Wqkv[2048 + P * (h0 + hl):2048 + P * (h0 + hl + 1), :].T).reshape(NDC, P, P)
            for hl in range(2)])
        wv_t = np.ascontiguousarray(
            Wqkv[4096 + P * h0:4096 + P * (h0 + 2), :].T).reshape(NDC, P, 256)
        gam = np.stack([np.tile(gng[h0 + hl][None, :], (P, 1)) for hl in range(2)])
        bet = np.stack([np.tile(gnb[h0 + hl][None, :], (P, 1)) for hl in range(2)])
        agidx = np.zeros((P, 64), np.uint32)
        for tt in range(8):
            for r in range(NCORE):
                agidx[:, tt * 8 + r] = c * OWN + r * TOK + tt * P + np.arange(P)
        per_core.append(dict(
            wq_t=np.ascontiguousarray(wq_t), wk_t=np.ascontiguousarray(wk_t),
            wv_t=wv_t, gam=np.ascontiguousarray(gam),
            bet=np.ascontiguousarray(bet),
            agidx=np.ascontiguousarray(agidx),
            x_own=np.ascontiguousarray(x[OWN * c:OWN * (c + 1)])))
    return shared, per_core


class Ctr:
    __slots__ = ("sem", "n")

    def __init__(self, sem):
        self.sem = sem
        self.n = 0


def build(debug=False):
    nc = bass.Bass("TRN2", target_bir_lowering=False)

    DT = {}

    def din(name, shape, dtype=F32):
        DT[name] = nc.dram_tensor(name, list(shape), dtype, kind="ExternalInput").ap()
        return DT[name]

    d_xT = din("xT", [D, TOK])
    d_wq = din("wq_t", [2, NDC, P, P])
    d_wk = din("wk_t", [2, NDC, P, P])
    d_wv = din("wv_t", [NDC, P, 256])
    d_woT = din("woT_cb", [4, NDC, P, 512])
    d_wvf = din("wvf_st", [NDC, NDC, P, P])
    d_wof = din("wof_cb", [4, NDC, P, 512])
    d_gate = din("gate_st", [NFC, NDC, P, P])
    d_up = din("up_st", [NFC, NDC, P, P])
    d_down = din("down_cb", [8, NFC, P, 256])
    d_lq1 = din("lq1", [P, P])
    d_lk1 = din("lk1", [P, P])
    d_lq2 = din("lq2", [P, P])
    d_lk2 = din("lk2", [P, P])
    d_gam = din("gam", [2, P, P])
    d_bet = din("bet", [2, P, P])
    d_n1w = din("n1w", [P, NDC])
    d_n2w = din("n2w", [P, NDC])
    d_bvv = din("bvv", [P, NDC])
    d_wrt = din("wrt", [P, D], BF16)
    d_brt = din("brt", [P, 1])
    d_tri = din("tri", [P, P], BF16)
    d_trin = din("trin", [P, P])
    d_idn = din("ident", [P, P], BF16)
    d_xow = din("x_own", [OWN, D])
    d_agix = din("agidx", [P, 64], mybir.dt.uint32)
    d_y = nc.dram_tensor("y", [OWN, D], F32, kind="ExternalOutput").ap()

    dk = "ExternalOutput" if debug else "Internal"
    d_agin = nc.dram_tensor("ag_in", [TOK, 256], BF16, kind="Internal").ap()
    d_dbga = (nc.dram_tensor("dbg_a", [TOK, 256], BF16, kind="ExternalOutput").ap()
              if debug else None)
    if debug:
        d_dbgq = nc.dram_tensor("dbg_q", [P, SEQ], BF16, kind="ExternalOutput").ap()
        d_dbgk = nc.dram_tensor("dbg_k", [P, SEQ], BF16, kind="ExternalOutput").ap()
        d_dbgv = nc.dram_tensor("dbg_v", [P, 32 * VS], BF16, kind="ExternalOutput").ap()
        d_dbg2 = nc.dram_tensor("dbg_at2", [P, 8 * D], BF16, kind="ExternalOutput").ap()
        d_dbgp = nc.dram_tensor("dbg_ps", [P, 512], F32, kind="ExternalOutput").ap()
        d_dbge = nc.dram_tensor("dbg_ev", [P, 4 * P], F32, kind="ExternalOutput").ap()
        d_dbgs = nc.dram_tensor("dbg_st", [P, 16], F32, kind="ExternalOutput").ap()
        d_dbgp1 = nc.dram_tensor("dbg_p1", [P, 512], BF16, kind="ExternalOutput").ap()
        d_dbgh = nc.dram_tensor("dbg_hat", [P, P], BF16, kind="ExternalOutput").ap()
        d_dbgb = nc.dram_tensor("dbg_b2", [P, 2 * P], F32, kind="ExternalOutput").ap()
        d_dbgo = nc.dram_tensor("dbg_ago", [P, 512], BF16, kind="ExternalOutput").ap()
        d_dbgt = nc.dram_tensor("dbg_atm", [P, D], BF16, kind="ExternalOutput").ap()
        d_dbgf0 = nc.dram_tensor("dbg_afm0", [P, NDC * OWN], BF16,
                                 kind="ExternalOutput").ap()
        d_dbgf = nc.dram_tensor("dbg_afm", [P, NDC * OWN], BF16, kind="ExternalOutput").ap()

    d_agout = nc.dram_tensor("ag_out", [NCORE * TOK, 256], BF16, kind="Internal",
                             addr_space="Shared").ap()
    d_x1sp = nc.dram_tensor("x1sp", [OWN, D], F32, kind=dk).ap()
    d_ln2sp = nc.dram_tensor("ln2sp", [OWN, D], BF16, kind=dk).ap()
    d_hsp = nc.dram_tensor("hsp", [OWN, D], BF16, kind=dk).ap()
    d_selsp = nc.dram_tensor("selsp", [OWN, D], BF16, kind=dk).ap()

    pb = [nc.alloc_psum_tensor(f"pb{i}", [P, 512], F32) for i in range(8)]
    pbh = [b.ap() for b in pb]
    pbb = [b.ap().bitcast(BF16) for b in pb]

    def sb(name, shape, dtype):
        return nc.alloc_sbuf_tensor(name, list(shape), dtype).ap()

    s_idn = sb("s_idn", [P, P], BF16)
    s_tri = sb("s_tri", [P, P], BF16)
    s_lam = sb("s_lam", [P, P], F32)
    s_lt = sb("s_lt", [P, 2 * P], F32)
    s_g2 = sb("s_g2", [P, 2 * P], F32)
    s_b2 = sb("s_b2", [P, 2 * P], F32)
    s_n1 = sb("s_n1", [P, NDC], F32)
    s_n2 = sb("s_n2", [P, NDC], F32)
    s_bv = sb("s_bv", [P, NDC], F32)
    s_wrt = sb("s_wrt", [P, D], BF16)
    s_brt = sb("s_brt", [P, 1], F32)
    s_eln = sb("s_eln", [P, 1], F32)
    s_erm = sb("s_erm", [P, 1], F32)
    s_er1 = sb("s_er1", [1, 1], F32)
    s_oc = sb("s_oc", [P, 1], BF16)
    s_or = sb("s_or", [1, P], BF16)
    s_trin = sb("s_trin", [P, P], F32)
    s_mxa = sb("s_mxa", [P, 12], F32)
    s_mxb = sb("s_mxb", [P, 12], F32)
    s_mc = sb("s_mc", [P, 2], F32)
    s_mcb = sb("s_mcb", [P, 2], BF16)
    s_row = sb("s_row", [1, 3 * 512], F32)
    s_rwb = sb("s_rwb", [1, 1024], BF16)
    s_st1 = sb("s_st1", [P, 16], F32)
    s_st2 = sb("s_st2", [P, 8], F32)
    s_rmask = sb("s_rmask", [P, 8], F32)
    s_rstd2 = sb("s_rstd2", [P, 8], F32)
    s_ev = sb("s_ev", [P, 4 * P], F32)
    s_aix = sb("s_aix", [P, 64], mybir.dt.uint32)
    h_scr = sb("scr4", [P, D], F32)
    h_scr2 = sb("scr4b", [P, D], BF16)

    base = nc.sbuf_base
    avail = nc.sbuf_bytes_remaining - 512
    items = []

    def at(name, shape, dtype, off_kb, stages):
        per_part = int(np.prod(shape[1:])) * mybir.dt.size(dtype)
        off = off_kb * KB
        assert off + per_part <= avail, (name, off, per_part, avail)
        items.append((name, off, per_part, frozenset(stages)))
        return nc.alloc_sbuf_tensor_at(name, list(shape), dtype,
                                       offset=base + off).ap()

    P12 = ("p12",)
    h_xfm = [at(f"xfm{i}", [P, NDC * 512], BF16, 0 + 16 * i, P12) for i in range(2)]
    h_wq = at("wqs", [P, 2 * NDC * P], BF16, 32, P12)
    h_wk = at("wks", [P, 2 * NDC * P], BF16, 40, P12)
    h_wv = at("wvs", [P, NDC * 256], BF16, 48, P12)
    h_q = [at(f"qsb{i}", [P, SEQ], BF16, 56 + 8 * i, P12) for i in range(2)]
    h_k = [at(f"ksb{i}", [P, SEQ], BF16, 72 + 8 * i, P12) for i in range(2)]
    h_v = [at(f"vsb{i}", [P, 32 * VS], BF16, 88 + 9 * i, P12) for i in range(2)]
    h_q1a = at("q1a", [65, SEQ], BF16, 106, P12)
    h_q2a = at("q2a", [65, SEQ], BF16, 114, P12)
    h_k1a = at("k1a", [65, SEQ], BF16, 122, P12)
    h_k2a = at("k2a", [65, SEQ], BF16, 130, P12)
    h_p1t = [at(f"p1t{i}", [P, 512], BF16, 138 + i, P12) for i in range(2)]
    h_p2t = [at(f"p2t{i}", [P, 512], BF16, 140 + i, P12) for i in range(2)]
    h_xsq = [at(f"xsq{i}", [P, 512], BF16, 142 + i, P12) for i in range(2)]
    h_at = [at(f"atile{i}", [P, P], BF16, 144 + i, P12) for i in range(2)]

    h_atm = [at(f"a_tm{i}", [P, D], BF16, 0 + 4 * i, ("s4a",)) for i in range(2)]
    h_afm = at("afm", [P, NDC * OWN], BF16, 8, ("s4a", "s4b"))
    h_wcb = [at(f"wcb{i}", [P, NDC * 512], BF16, 40 + 16 * i, ("s4b", "s4f"))
             for i in range(2)]
    h_at2 = at("attn2", [P, 8 * D], BF16, 72, ("s4b", "s4c"))
    h_xsb = at("x_sb", [P, 8 * D], F32, 0, ("s4c",))
    h_ln2 = at("ln2", [P, 8 * D], BF16, 104, ("s4c", "s4d", "s4e", "s4f"))
    h_h1f = at("h1f", [P, NDC * OWN], BF16, 0, ("s4d", "s4e"))
    h_wstA = [at(f"wstA{i}", [P, NDC * P], BF16, 32 + 4 * i, ("s4e",))
              for i in range(2)]
    h_vfm = at("vfm", [P, NDC * OWN], BF16, 72, ("s4e", "s4f"))
    h_h = at("h_tm", [P, 8 * D], BF16, 0, ("s4f", "s4g"))
    h_wstB = [at(f"wstB{i}", [P, NDC * P], BF16, 0 + 4 * i, ("s4h",))
              for i in range(4)]
    h_gact = at("gact", [P, NFC * 512], BF16, 32, ("s4h",))
    h_dwn = [at(f"dwn{i}", [P, NFC * 256], BF16, 76 + 22 * i, ("s4h",))
             for i in range(2)]
    h_h2f = at("h2f", [P, NDC * 512], BF16, 120, ("s4h",))
    h_l2p = [at(f"l2p{i}", [P, 4 * 256], BF16, 136 + 2 * i, ("s4h",)) for i in range(2)]
    h_hp = [at(f"hp{i}", [P, 4 * 256], BF16, 140 + 2 * i, ("s4h",)) for i in range(2)]
    h_self = at("self", [P, 4 * D], BF16, 32, ("s4i",))
    h_x1f = [at(f"x1f{i}", [P, D], F32, 48 + 8 * i, ("s4i",)) for i in range(2)]

    for i, (n1, o1, s1, st1) in enumerate(items):
        for (n2, o2, s2, st2) in items[i + 1:]:
            if st1 & st2 and not (o1 + s1 <= o2 or o2 + s2 <= o1):
                raise AssertionError(f"sbuf overlap {n1} / {n2} in {sorted(st1 & st2)}")

    import contextlib
    stack = contextlib.ExitStack()
    sems = {k: stack.enter_context(nc.semaphore("sem_" + k))
            for k in ("dg", "dsy", "pe", "dve", "act", "red", "coll", "out",
                      "ga0", "ga1")}

    def emit(en, eng):
        C = {k: Ctr(v) for k, v in sems.items()}

        def dma(q, out, in_, ctr="dg", **kw):
            c = C[ctr]
            if en == q:
                eng.dma_start(out=out, in_=in_, **kw).then_inc(c.sem, 16)
            c.n += 16
            return c.n

        def wait(which, ctr, val=None):
            c = C[ctr]
            v = c.n if val is None else val
            if en == which:
                eng.wait_ge(c.sem, v)

        def inc(which, ins, ctr, k=1):
            c = C[ctr]
            if en == which and ins is not None:
                ins.then_inc(c.sem, k)
            c.n += k
            return c.n

        E = (lambda w: (lambda f: f(eng) if en == w else None))
        T, V, A, G, S = E("t"), E("v"), E("c"), E("g"), E("s")

        dma("s", s_idn[:], d_idn[:], "dsy")
        dma("s", s_tri[:], d_tri[:], "dsy")
        dma("s", s_trin[:], d_trin[:], "dsy")
        dma("s", s_lt[:, 0:P], d_lq1[:], "dsy")
        dma("s", s_lt[:, P:2 * P], d_lk1[:], "dsy")
        dma("s", s_g2[:, 0:P], d_lq2[:], "dsy")
        dma("s", s_g2[:, P:2 * P], d_lk2[:], "dsy")
        dma("s", s_n1[:], d_n1w[:], "dsy")
        dma("s", s_n2[:], d_n2w[:], "dsy")
        dma("s", s_bv[:], d_bvv[:], "dsy")
        dma("s", s_wrt[:], d_wrt[:], "dsy")
        dma("s", s_brt[:], d_brt[:], "dsy")
        dma("s", s_aix[:], d_agix[:], "dsy")
        v_p0 = dma("s", s_b2.rearrange("p (h c) -> p h c", h=2),
               d_gam.rearrange("h p c -> p h c"), "dsy")
        wait("v", "dsy", v_p0)
        V(lambda e: e.memset(s_eln[:], EPS_LN))
        V(lambda e: e.memset(s_erm[:], EPS_RMS))
        V(lambda e: e.memset(s_er1[:], EPS_LN))
        V(lambda e: e.memset(s_oc[:], 1.0))
        V(lambda e: e.memset(s_or[:], 1.0))
        V(lambda e: e.memset(h_k1a[64:65, :], -1.0))
        V(lambda e: e.memset(h_k2a[64:65, :], -1.0))
        V(lambda e: e.memset(h_v[0][:], 1.0))
        V(lambda e: e.memset(h_v[1][:], 1.0))
        V(lambda e: e.tensor_mul(s_lam[:], s_lt[:, 0:P], s_lt[:, P:2 * P]))
        n_d = inc("v", V(lambda e: e.tensor_mul(
            s_lt[:, 0:P], s_g2[:, 0:P], s_g2[:, P:2 * P])), "dve")
        wait("c", "dve", n_d)
        A(lambda e: e.activation(s_lam[:], s_lam[:], AF.Exp))
        n_a = inc("c", A(lambda e: e.activation(s_lt[:, 0:P], s_lt[:, 0:P],
                                                AF.Exp)), "act")
        wait("v", "act", n_a)
        V(lambda e: e.tensor_sub(s_lam[:], s_lam[:], s_lt[:, 0:P]))
        V(lambda e: e.tensor_scalar(s_lam[:], s_lam[:], LAM_INIT, None, ALU.add))
        n_g2 = inc("v", V(lambda e: e.tensor_scalar(
            s_g2[:], s_b2[:], 1.0 - LAM_INIT, None, ALU.mult)), "dve")
        wait("s", "dve", n_g2)
        v_p0b = dma("s", s_b2.rearrange("p (h c) -> p h c", h=2),
                d_bet.rearrange("h p c -> p h c"), "dsy")
        wait("v", "dsy", v_p0b)
        n_const = inc("v", V(lambda e: e.tensor_scalar(
            s_b2[:], s_b2[:], 1.0 - LAM_INIT, None, ALU.mult)), "dve")

        dma("g", h_wq.rearrange("p (h c f) -> p h c f", h=2, c=NDC),
            d_wq.rearrange("h c p f -> p h c f"))
        dma("g", h_wk.rearrange("p (h c f) -> p h c f", h=2, c=NDC),
            d_wk.rearrange("h c p f -> p h c f"))
        v_wqkv = dma("g", h_wv.rearrange("p (c f) -> p c f", c=NDC),
                 d_wv.rearrange("c p f -> p c f"))

        xfm_free = [("dve", 0), ("dve", 0)]
        xsq_free = [("pe", 0), ("pe", 0)]
        pt_free = [[("act", 0)], [("act", 0)]]
        at_free = [("dsy", 0), ("dsy", 0)]
        acc_free = 0
        wait("t", "dve", n_const)
        wait("t", "dsy", v_p0b)

        pr_ev = C["act"].n
        pr_ln1 = C["dve"].n
        pr_vev = C["dve"].n
        for b in range(NB):
            for tc in range(8):
                sl = tc % 2
                t0 = b * SEQ + tc * 512
                wait("t", "act", pr_ev)
                wait("t", "dve", pr_ln1)
                kind, val = xfm_free[sl]
                wait("g", kind, val)
                v_x = dma("g", h_xfm[sl].rearrange("p (c t) -> p c t", c=NDC),
                          d_xT[:, t0:t0 + 512].rearrange("(c p) t -> p c t", p=P))
                wait("v", "dg", v_x)
                wait("t", "dg", v_x)
                for dc in range(NDC):
                    par = dc % 2
                    xs = h_xfm[sl][:, dc * 512:(dc + 1) * 512]
                    kind, val = xsq_free[par]
                    wait("v", kind, val)
                    n_sq = inc("v", V(lambda e, xs=xs, par=par: e.tensor_mul(
                        h_xsq[par][:], xs, xs)), "dve")
                    T(lambda e, xs=xs, dc=dc: e.matmul(
                        pbh[6][0:1, 0:512], s_oc[:], xs,
                        start=(dc == 0), stop=(dc == NDC - 1)))
                    wait("t", "dve", n_sq)
                    ins = T(lambda e, par=par, dc=dc: e.matmul(
                        pbh[7][0:1, 0:512], s_oc[:], h_xsq[par][:],
                        start=(dc == 0), stop=(dc == NDC - 1)))
                    n_pe = inc("t", ins, "pe")
                    xsq_free[par] = ("pe", n_pe)
                n_stats = n_pe
                wait("v", "pe", n_stats)
                V(lambda e: e.tensor_scalar(s_row[0:1, 0:512], pbh[6][0:1, 0:512],
                                            1.0 / D, None, ALU.mult))
                V(lambda e: e.tensor_scalar(s_row[0:1, 512:1024], pbh[7][0:1, 0:512],
                                            1.0 / D, None, ALU.mult))
                V(lambda e: e.tensor_mul(s_row[0:1, 1024:1536], s_row[0:1, 0:512],
                                         s_row[0:1, 0:512]))
                n_var = inc("v", V(lambda e: e.tensor_sub(
                    s_row[0:1, 512:1024], s_row[0:1, 512:1024],
                    s_row[0:1, 1024:1536])), "dve")
                wait("c", "dve", n_var)
                A(lambda e: e.activation(s_row[0:1, 1024:1536], s_row[0:1, 512:1024],
                                         AF.Ln, bias=s_er1[:]))
                n_rs = inc("c", A(lambda e: e.activation(
                    s_row[0:1, 512:1024], s_row[0:1, 1024:1536], AF.Exp,
                    scale=-0.5)), "act")
                wait("v", "act", n_rs)
                V(lambda e: e.tensor_copy(s_rwb[0:1, 0:512], s_row[0:1, 512:1024]))
                V(lambda e: e.tensor_mul(s_row[0:1, 1024:1536], s_row[0:1, 0:512],
                                         s_row[0:1, 512:1024]))
                n_rows = inc("v", V(lambda e: e.tensor_scalar(
                    s_rwb[0:1, 512:1024], s_row[0:1, 1024:1536], -1.0, None,
                    ALU.mult)), "dve")
                wait("t", "dve", n_rows)
                T(lambda e: e.matmul(pbh[6][:, 0:512], s_or[:], s_rwb[0:1, 0:512],
                                     start=True, stop=True))
                n_bc = inc("t", T(lambda e: e.matmul(
                    pbh[7][:, 0:512], s_or[:], s_rwb[0:1, 512:1024],
                    start=True, stop=True)), "pe")
                wait("v", "pe", n_bc)
                for dc in range(NDC):
                    xs = h_xfm[sl][:, dc * 512:(dc + 1) * 512]
                    V(lambda e, xs=xs: e.tensor_tensor(xs, xs, pbh[6][:, 0:512],
                                                       ALU.mult))
                    ins = V(lambda e, xs=xs: e.tensor_tensor(
                        xs, xs, pbh[7][:, 0:512], ALU.add))
                    n_ln1 = inc("v", ins, "dve")
                wait("t", "dve", n_ln1)
                if b == 0 and tc == 0:
                    wait("t", "dg", v_wqkv)
                for hl in range(2):
                    for dc in range(NDC):
                        T(lambda e, hl=hl, dc=dc, sl=sl: e.matmul(
                            pbh[hl][:, 0:512],
                            h_wq[:, (hl * NDC + dc) * P:(hl * NDC + dc + 1) * P],
                            h_xfm[sl][:, dc * 512:(dc + 1) * 512],
                            start=(dc == 0), stop=(dc == NDC - 1)))
                    for dc in range(NDC):
                        ins = T(lambda e, hl=hl, dc=dc, sl=sl: e.matmul(
                            pbh[2 + hl][:, 0:512],
                            h_wk[:, (hl * NDC + dc) * P:(hl * NDC + dc + 1) * P],
                            h_xfm[sl][:, dc * 512:(dc + 1) * 512],
                            start=(dc == 0), stop=(dc == NDC - 1)))
                    n_qk = inc("t", ins, "pe")
                wait("t", "dve", pr_vev)
                for tt in range(4):
                    bank = 4 + tt % 2
                    reg = (tt // 2) * 256
                    for dc in range(NDC):
                        ins = T(lambda e, tt=tt, dc=dc, sl=sl, bank=bank, reg=reg:
                                e.matmul(
                                    pbh[bank][:, reg:reg + 256],
                                    h_xfm[sl][:, dc * 512 + tt * P:dc * 512 + (tt + 1) * P],
                                    h_wv[:, dc * 256:(dc + 1) * 256],
                                    start=(dc == 0 and reg == 0),
                                    stop=(dc == NDC - 1 and reg == 256),
                                    skip_group_check=True))
                    n_v = inc("t", ins, "pe")
                xfm_free[sl] = ("pe", n_v)
                wait("c", "pe", n_qk)
                for hl in range(2):
                    A(lambda e, hl=hl, tc=tc: e.activation(
                        h_q[hl][:, tc * 512:tc * 512 + 512], pbh[hl][:, 0:512],
                        AF.Copy, scale=0.125))
                    n_ev = inc("c", A(lambda e, hl=hl, tc=tc: e.activation(
                        h_k[hl][:, tc * 512:tc * 512 + 512], pbh[2 + hl][:, 0:512],
                        AF.Copy)), "act")
                wait("v", "pe", n_v)
                for tt in range(4):
                    bank = 4 + tt % 2
                    reg = (tt // 2) * 256
                    gt = tc * 4 + tt
                    for hl in range(2):
                        ins = V(lambda e, bank=bank, reg=reg, hl=hl, gt=gt:
                                e.tensor_copy(
                                    h_v[hl][:, gt * VS:gt * VS + P],
                                    pbh[bank][:, reg + hl * P:reg + (hl + 1) * P]))
                        n_vev = inc("v", ins, "dve")
                pr_ev = n_ev
                pr_ln1 = n_ln1
                pr_vev = n_vev

            if debug and b == 0:
                wait("s", "act", C["act"].n)
                wait("s", "dve", C["dve"].n)
                dma("s", d_dbgq[:], h_q[0][:], "dsy")
                dma("s", d_dbgk[:], h_k[0][:], "dsy")
                dma("s", d_dbgv[:], h_v[0][:], "dsy")
            for hl in range(2):
                wait("s", "act", C["act"].n)
                wait("s", "pe", C["pe"].n)
                dma("s", h_q1a[0:64, :], h_q[hl][0:64, :], "dsy")
                dma("s", h_q2a[0:64, :], h_q[hl][64:128, :], "dsy")
                dma("s", h_k1a[0:64, :], h_k[hl][0:64, :], "dsy")
                v_aug = dma("s", h_k2a[0:64, :], h_k[hl][64:128, :], "dsy")
                wait("t", "dsy", v_aug)
                wait("t", "act", C["act"].n)
                wait("t", "dve", C["dve"].n)
                for g in range(NG):
                    for j in range(4):
                        qt = 4 * g + j
                        for kc4 in range(g + 1):
                            N = 512 if kc4 < g else 128 * (j + 1)
                            par = kc4 % 2
                            pa, pb_ = 4 + par, 6 + par
                            for (kk, vv) in pt_free[par]:
                                wait("t", kk, vv)
                            T(lambda e, hl=hl, qt=qt, kc4=kc4, N=N, pa=pa: e.matmul(
                                pbh[pa][:, 0:N],
                                h_q[hl][0:64, qt * P:(qt + 1) * P],
                                h_k[hl][0:64, kc4 * 512:kc4 * 512 + N],
                                start=True, stop=True))
                            n_s = inc("t", T(lambda e, hl=hl, qt=qt, kc4=kc4, N=N,
                                             pb_=pb_: e.matmul(
                                pbh[pb_][:, 0:N],
                                h_q[hl][64:128, qt * P:(qt + 1) * P],
                                h_k[hl][64:128, kc4 * 512:kc4 * 512 + N],
                                start=True, stop=True)), "pe")
                            wait("v", "pe", n_s)
                            if kc4 == g:
                                V(lambda e, pa=pa, j=j: e.tensor_add(
                                    s_ev[:, 0:P], pbh[pa][:, j * P:(j + 1) * P],
                                    s_trin[:]))
                                V(lambda e, pb_=pb_, j=j: e.tensor_add(
                                    s_ev[:, P:2 * P], pbh[pb_][:, j * P:(j + 1) * P],
                                    s_trin[:]))
                                inc("v", V(lambda e, kc4=kc4: e.reduce_max(
                                    s_mxa[:, kc4:kc4 + 1], s_ev[:, 0:P], AX.X)),
                                    "red")
                                n_red = inc("v", V(lambda e, kc4=kc4: e.reduce_max(
                                    s_mxb[:, kc4:kc4 + 1], s_ev[:, P:2 * P], AX.X)),
                                    "red")
                                if j > 0:
                                    inc("v", V(lambda e, pa=pa, j=j, kc4=kc4:
                                               e.reduce_max(
                                        s_mxa[:, kc4 + 1:kc4 + 2],
                                        pbh[pa][:, 0:j * P], AX.X)), "red")
                                    n_red = inc("v", V(lambda e, pb_=pb_, j=j,
                                                       kc4=kc4: e.reduce_max(
                                        s_mxb[:, kc4 + 1:kc4 + 2],
                                        pbh[pb_][:, 0:j * P], AX.X)), "red")
                            else:
                                inc("v", V(lambda e, pa=pa, N=N, kc4=kc4:
                                           e.reduce_max(
                                    s_mxa[:, kc4:kc4 + 1], pbh[pa][:, 0:N], AX.X)),
                                    "red")
                                n_red = inc("v", V(lambda e, pb_=pb_, N=N, kc4=kc4:
                                                   e.reduce_max(
                                    s_mxb[:, kc4:kc4 + 1], pbh[pb_][:, 0:N], AX.X)),
                                    "red")
                            pt_free[par] = [("red", n_red)]
                        tot = (g + 1) + (1 if j > 0 else 0)
                        wait("v", "red", C["red"].n)
                        inc("v", V(lambda e, tot=tot: e.reduce_max(
                            s_mc[:, 0:1], s_mxa[:, 0:tot], AX.X)), "red")
                        n_red = inc("v", V(lambda e, tot=tot: e.reduce_max(
                            s_mc[:, 1:2], s_mxb[:, 0:tot], AX.X)), "red")
                        wait("v", "red", n_red)
                        n_cast = inc("v", V(lambda e: e.tensor_copy(
                            s_mcb[:], s_mc[:])), "dve")
                        wait("t", "dve", n_cast)
                        wait("t", "red", C["red"].n)
                        for (kk, vv) in pt_free[0] + pt_free[1]:
                            wait("t", kk, vv)
                        T(lambda e: e.transpose(pbb[6][0:1, 0:P], s_mcb[:, 0:1],
                                                s_idn[:]))
                        n_tp = inc("t", T(lambda e: e.transpose(
                            pbb[7][0:1, 0:P], s_mcb[:, 1:2], s_idn[:])), "pe")
                        wait("v", "pe", n_tp)
                        V(lambda e, qt=qt: e.tensor_copy(
                            h_q1a[64:65, qt * P:(qt + 1) * P], pbb[6][0:1, 0:P]))
                        n_mrow = inc("v", V(lambda e, qt=qt: e.tensor_copy(
                            h_q2a[64:65, qt * P:(qt + 1) * P], pbb[7][0:1, 0:P])),
                            "dve")
                        pt_free[0] = [("dve", n_mrow)]
                        pt_free[1] = [("dve", n_mrow)]
                    kmax = 4 * g + 4
                    ready_l = {}

                    def emit_spexp(kc, g=g):
                        par = kc % 2
                        sb1, sb2 = 4 + par, 6 + par
                        for (kk, vv) in pt_free[par]:
                            wait("t", kk, vv)
                        if kc == 0:
                            wait("t", "dve", n_mrow)
                        T(lambda e, kc=kc, sb1=sb1, g=g: e.matmul(
                            pbh[sb1][:, 0:512], h_k1a[:, kc * P:(kc + 1) * P],
                            h_q1a[:, g * 512:(g + 1) * 512], start=True, stop=True))
                        n_s = inc("t", T(lambda e, kc=kc, sb2=sb2, g=g: e.matmul(
                            pbh[sb2][:, 0:512], h_k2a[:, kc * P:(kc + 1) * P],
                            h_q2a[:, g * 512:(g + 1) * 512], start=True, stop=True)),
                            "pe")
                        wait("c", "pe", n_s)
                        A(lambda e, par=par, sb1=sb1: e.activation(
                            h_p1t[par][:], pbh[sb1][:, 0:512], AF.Exp))
                        n_e = inc("c", A(lambda e, par=par, sb2=sb2: e.activation(
                            h_p2t[par][:], pbh[sb2][:, 0:512], AF.Exp)), "act")
                        pt_free[par] = [("act", n_e)]
                        ready = ("act", n_e)
                        if kc >= 4 * g:
                            o = kc - 4 * g
                            wait("v", "act", n_e)
                            V(lambda e, par=par, o=o: e.tensor_mul(
                                h_p1t[par][:, o * P:(o + 1) * P],
                                h_p1t[par][:, o * P:(o + 1) * P], s_tri[:]))
                            n_mk = inc("v", V(lambda e, par=par, o=o: e.tensor_mul(
                                h_p2t[par][:, o * P:(o + 1) * P],
                                h_p2t[par][:, o * P:(o + 1) * P], s_tri[:])), "dve")
                            ready = ("dve", n_mk)
                        ready_l[kc] = ready

                    def emit_pv(kc, g=g, hl=hl, kmax=kmax):
                        par = kc % 2
                        ready = ready_l[kc]
                        wait("t", ready[0], ready[1])
                        if kc == 0:
                            wait("t", "dve", acc_free)
                        last_pv = None
                        for j in range(max(0, kc - 4 * g), 4):
                            for sd in range(2):
                                pt = h_p1t[par] if sd == 0 else h_p2t[par]
                                last_pv = T(lambda e, j=j, sd=sd, pt=pt, hl=hl,
                                            kc=kc, kmax=kmax: e.matmul(
                                    pbh[j][:, sd * 256:sd * 256 + 129],
                                    pt[:, j * P:(j + 1) * P],
                                    h_v[hl][:, kc * VS:kc * VS + 129],
                                    start=(kc == 0 and sd == 0),
                                    stop=(kc == kmax - 1 and sd == 1),
                                    skip_group_check=True))
                        return inc("t", last_pv, "pe")

                    emit_spexp(0)
                    for kc in range(kmax):
                        if kc + 1 < kmax:
                            emit_spexp(kc + 1)
                        n_pv = emit_pv(kc)
                    wait("v", "pe", n_pv)
                    if debug and b == 0 and hl == 0 and g == 0:
                        n_dbg = inc("v", V(lambda e: e.tensor_copy(
                            s_ev[:, 0:4 * P], pbh[0][:, 0:4 * P])), "dve")
                        wait("s", "dve", n_dbg)
                        dma("s", d_dbgp[:], s_ev[:, 0:4 * P].bitcast(F32)[:, 0:4 * P]
                            if False else s_ev[:, 0:4 * P], "dsy")
                        dma("s", d_dbgp1[:], h_p1t[1][:], "dsy")
                        wait("v", "dsy", C["dsy"].n)
                    for j in range(4):
                        V(lambda e, j=j: e.reciprocal(s_st1[:, 0:1],
                                                      pbh[j][:, 128:129]))
                        n_rc = inc("v", V(lambda e, j=j: e.reciprocal(
                            s_st1[:, 1:2], pbh[j][:, 384:385])), "red")
                        wait("v", "red", n_rc)
                        V(lambda e, j=j: e.tensor_scalar(
                            s_ev[:, 0:P], pbh[j][:, 0:P], s_st1[:, 0:1], None,
                            ALU.mult))
                        V(lambda e, j=j: e.tensor_scalar(
                            s_ev[:, P:2 * P], pbh[j][:, 256:256 + P], s_st1[:, 1:2],
                            None, ALU.mult))
                        V(lambda e: e.tensor_mul(s_ev[:, 2 * P:3 * P],
                                                 s_ev[:, P:2 * P], s_lam[:]))
                        V(lambda e: e.tensor_sub(s_ev[:, 0:P], s_ev[:, 0:P],
                                                 s_ev[:, 2 * P:3 * P]))
                        n_r1 = inc("v", V(lambda e: e.tensor_reduce(
                            s_st1[:, 2:3], s_ev[:, 0:P], AX.X, ALU.add)), "red")
                        wait("v", "red", n_r1)
                        n_mu = inc("v", V(lambda e: e.tensor_scalar(
                            s_st1[:, 2:3], s_st1[:, 2:3], 1.0 / P, None, ALU.mult)),
                            "red")
                        wait("v", "red", n_mu)
                        V(lambda e: e.tensor_scalar(s_ev[:, P:2 * P], s_ev[:, 0:P],
                                                    s_st1[:, 2:3], None,
                                                    ALU.subtract))
                        n_r2 = inc("v", V(lambda e: e.scalar_tensor_tensor(
                            s_ev[:, 2 * P:3 * P], s_ev[:, P:2 * P], 1.0,
                            s_ev[:, P:2 * P], ALU.bypass, ALU.mult,
                            accum_out=s_st1[:, 3:4])), "red")
                        wait("c", "red", n_r2)
                        A(lambda e: e.activation(s_st1[:, 4:5], s_st1[:, 3:4],
                                                 AF.Ln, bias=s_eln[:],
                                                 scale=1.0 / P))
                        n_gr = inc("c", A(lambda e: e.activation(
                            s_st1[:, 5:6], s_st1[:, 4:5], AF.Exp, scale=-0.5)),
                            "act")
                        wait("v", "act", n_gr)
                        kk, vv = at_free[j % 2]
                        wait("v", kk, vv)
                        V(lambda e: e.tensor_scalar(
                            s_ev[:, 3 * P:4 * P], s_ev[:, P:2 * P], s_st1[:, 5:6],
                            None, ALU.mult))
                        V(lambda e, hl=hl: e.tensor_mul(
                            s_ev[:, 3 * P:4 * P], s_ev[:, 3 * P:4 * P],
                            s_g2[:, hl * P:(hl + 1) * P]))
                        ins = V(lambda e, hl=hl, j=j: e.tensor_tensor(
                            h_at[j % 2][:], s_ev[:, 3 * P:4 * P],
                            s_b2[:, hl * P:(hl + 1) * P], ALU.add))
                        n_gn = inc("v", ins, "dve")
                        acc_free = n_gn
                        wait("s", "dve", n_gn)
                        if debug and b == 0 and hl == 0 and g == 0 and j == 0:
                            dma("s", d_dbge[:], s_ev[:], "dsy")
                            dma("s", d_dbgs[:], s_st1[:], "dsy")
                            dma("s", d_dbgh[:], h_at[0][:], "dsy")
                            dma("s", d_dbgb[:], s_b2[:], "dsy")
                        row0 = b * SEQ + g * 512 + j * P
                        v_at = dma("s", d_agin[row0:row0 + P,
                                               hl * P:(hl + 1) * P],
                                   h_at[j % 2][:], "dsy")
                        if debug:
                            v_at = dma("s", d_dbga[row0:row0 + P,
                                                   hl * P:(hl + 1) * P],
                                       h_at[j % 2][:], "dsy")
                        at_free[j % 2] = ("dsy", v_at)

        wait("g", "dsy", C["dsy"].n)
        n_coll = inc("g", G(lambda e: e.collective_compute(
            "AllGather", ALU.bypass, replica_groups=[list(range(NCORE))],
            ins=[d_agin.opt()], outs=[d_agout.opt()])), "coll")


        for w_ in ("g", "v", "t", "c", "s"):
            wait(w_, "coll", n_coll)

        if debug:
            dma("g", h_atm[0][:, 0:256], d_agout[6 * TOK:6 * TOK + P, :])
            c_ = C["ga0"]
            if en == "g":
                eng.indirect_dma_start(
                    h_atm[0][:, 256:512], None, d_agout[:],
                    bass.IndirectOffsetOnAxis(s_aix[:, 6:7], 0)).then_inc(c_.sem, 16)
            c_.n += 16
            wait("s", "dg", C["dg"].n)
            wait("s", "ga0", C["ga0"].n)
            dma("s", d_dbgo[:], h_atm[0][:, 0:512], "dsy")
            wait("g", "dsy", C["dsy"].n)
        atm_free = [("dve", 0), ("dve", 0)]
        tplag = [0, 0]
        for tt in range(8):
            sl = tt % 2
            kind, val = atm_free[sl]
            wait("g", kind, val)
            for r in range(NCORE):
                i_col = tt * 8 + r
                c_ = C["ga0" if sl == 0 else "ga1"]
                if en == "g":
                    eng.indirect_dma_start(
                        h_atm[sl][:, r * 256:(r + 1) * 256], None,
                        d_agout[:], bass.IndirectOffsetOnAxis(
                            s_aix[:, i_col:i_col + 1], 0),
                    ).then_inc(c_.sem, 16)
                c_.n += 16
                v_a = c_.n
            wait("t", "ga0" if sl == 0 else "ga1", v_a)
            if debug and tt == 0:
                wait("s", "ga0", v_a)
                dma("s", d_dbgt[:], h_atm[0][:], "dsy")
                wait("t", "dsy", C["dsy"].n)
            for r in range(NCORE):
                for half in range(2):
                    dch = 2 * r + half
                    bank = dch % 2
                    wait("t", "dve", tplag[bank])
                    n_tp = inc("t", T(lambda e, sl=sl, r=r, half=half, bank=bank:
                                      e.transpose(
                                          pbb[4 + bank][:, 0:P],
                                          h_atm[sl][:, r * 256 + half * P:
                                                    r * 256 + (half + 1) * P],
                                          s_idn[:])), "pe")
                    wait("v", "pe", n_tp)
                    n_cp = inc("v", V(lambda e, dch=dch, tt=tt, bank=bank:
                                      e.tensor_copy(
                                          h_afm[:, dch * OWN + tt * P:
                                                dch * OWN + (tt + 1) * P],
                                          pbb[4 + bank][:, 0:P])), "dve")
                    tplag[bank] = n_cp
            atm_free[sl] = ("dve", n_cp)
            if debug and tt == 0:
                wait("s", "dve", C["dve"].n)
                dma("s", d_dbgf0[:], h_afm[:], "dsy")
                wait("v", "dsy", C["dsy"].n)
        n_afm = C["dve"].n

        wait("t", "dve", n_afm)
        wcb_free = [("pe", 0), ("pe", 0)]
        bk4b = [0, 0]
        for jc in range(4):
            sl = jc % 2
            kind, val = wcb_free[sl]
            wait("g", kind, val)
            v_w = dma("g", h_wcb[sl].rearrange("p (c f) -> p c f", c=NDC),
                      d_woT[jc].rearrange("c p f -> p c f"))
            wait("t", "dg", v_w)
            for tt in range(8):
                wait("t", "dve", bk4b[tt % 2])
                for ic in range(NDC):
                    ins = T(lambda e, sl=sl, ic=ic, tt=tt: e.matmul(
                        pbh[tt % 2][:, 0:512],
                        h_afm[:, ic * OWN + tt * P:ic * OWN + (tt + 1) * P],
                        h_wcb[sl][:, ic * 512:(ic + 1) * 512],
                        start=(ic == 0), stop=(ic == NDC - 1)))
                n_mm = inc("t", ins, "pe")
                wait("v", "pe", n_mm)
                n_ev = inc("v", V(lambda e, tt=tt, jc=jc: e.tensor_copy(
                    h_at2[:, tt * D + jc * 512:tt * D + (jc + 1) * 512],
                    pbh[tt % 2][:, 0:512])), "dve")
                bk4b[tt % 2] = n_ev
            wcb_free[sl] = ("pe", C["pe"].n)
        n_at2 = C["dve"].n

        if debug:
            wait("s", "dve", C["dve"].n)
            dma("s", d_dbg2[:], h_at2[:], "dsy")
            dma("s", d_dbgf[:], h_afm[:], "dsy")
        wait("s", "pe", C["pe"].n)
        wait("s", "dve", C["dve"].n)
        v_xo = dma("s", h_xsb.rearrange("p (t d) -> p t d", t=8),
                d_xow.rearrange("(t p) d -> p t d", p=P), "dsy")
        wait("v", "dsy", v_xo)
        wait("v", "dve", n_at2)
        for tt in range(8):
            at2 = h_at2[:, tt * D:(tt + 1) * D]
            xsl = h_xsb[:, tt * D:(tt + 1) * D]
            n_r = inc("v", V(lambda e, at2=at2: e.tensor_reduce(
                s_st1[:, 0:1], at2, AX.X, ALU.add)), "red")
            wait("v", "red", n_r)
            n_mu = inc("v", V(lambda e: e.tensor_scalar(
                s_st1[:, 0:1], s_st1[:, 0:1], 1.0 / D, None, ALU.mult)), "red")
            wait("v", "red", n_mu)
            V(lambda e, at2=at2: e.tensor_scalar(h_scr[:], at2, s_st1[:, 0:1],
                                                 None, ALU.subtract))
            n_r2 = inc("v", V(lambda e: e.scalar_tensor_tensor(
                h_scr[:], h_scr[:], 1.0, h_scr[:], ALU.bypass, ALU.mult,
                accum_out=s_st1[:, 1:2])), "red")
            wait("c", "red", n_r2)
            A(lambda e: e.activation(s_st1[:, 2:3], s_st1[:, 1:2], AF.Ln,
                                     bias=s_eln[:], scale=1.0 / D))
            n_rs = inc("c", A(lambda e: e.activation(
                s_st1[:, 3:4], s_st1[:, 2:3], AF.Exp, scale=-0.5)), "act")
            wait("v", "act", n_rs)
            V(lambda e, at2=at2: e.tensor_scalar(h_scr[:], at2, s_st1[:, 0:1],
                                                 s_st1[:, 3:4], ALU.subtract,
                                                 ALU.mult))
            n_x1 = inc("v", V(lambda e, xsl=xsl: e.tensor_add(xsl, xsl, h_scr[:])),
                       "dve")
            wait("s", "dve", n_x1)
            dma("s", d_x1sp[tt * P:(tt + 1) * P, :], xsl, "dsy")
            n_r = inc("v", V(lambda e, xsl=xsl: e.tensor_reduce(
                s_st1[:, 4:5], xsl, AX.X, ALU.add)), "red")
            wait("v", "red", n_r)
            n_mu = inc("v", V(lambda e: e.tensor_scalar(
                s_st1[:, 4:5], s_st1[:, 4:5], 1.0 / D, None, ALU.mult)), "red")
            wait("v", "red", n_mu)
            V(lambda e, xsl=xsl: e.tensor_scalar(h_scr[:], xsl, s_st1[:, 4:5],
                                                 None, ALU.subtract))
            n_r2 = inc("v", V(lambda e: e.scalar_tensor_tensor(
                h_scr[:], h_scr[:], 1.0, h_scr[:], ALU.bypass, ALU.mult,
                accum_out=s_st1[:, 5:6])), "red")
            wait("c", "red", n_r2)
            A(lambda e: e.activation(s_st1[:, 6:7], s_st1[:, 5:6], AF.Ln,
                                     bias=s_eln[:], scale=1.0 / D))
            n_rs = inc("c", A(lambda e: e.activation(
                s_st1[:, 7:8], s_st1[:, 6:7], AF.Exp, scale=-0.5)), "act")
            wait("v", "act", n_rs)
            ln2t = h_ln2[:, tt * D:(tt + 1) * D]
            n_l2 = inc("v", V(lambda e, xsl=xsl, ln2t=ln2t: e.tensor_scalar(
                ln2t, xsl, s_st1[:, 4:5], s_st1[:, 7:8], ALU.subtract, ALU.mult)),
                "dve")
            wait("s", "dve", n_l2)
            dma("s", d_ln2sp[tt * P:(tt + 1) * P, :], ln2t, "dsy")
            n_sc = inc("v", V(lambda e, ln2t=ln2t: e.scalar_tensor_tensor(
                h_scr[:], ln2t, 1.0, s_wrt[:], ALU.bypass, ALU.mult,
                accum_out=s_st2[:, 0:1])), "red")
            wait("v", "red", n_sc)
            n_sc2 = inc("v", V(lambda e: e.tensor_tensor(
                s_st2[:, 1:2], s_st2[:, 0:1], s_brt[:], ALU.add)), "dve")
            wait("c", "dve", n_sc2)
            n_sg = inc("c", A(lambda e: e.sign(s_st2[:, 2:3], s_st2[:, 1:2])), "act")
            wait("v", "act", n_sg)
            inc("v", V(lambda e, tt=tt: e.tensor_scalar_max(
                s_rmask[:, tt:tt + 1], s_st2[:, 2:3], 0.0)), "dve")

        tplag = [0, 0]
        for tt in range(8):
            ln2t = h_ln2[:, tt * D:(tt + 1) * D]
            n_r = inc("v", V(lambda e, ln2t=ln2t: e.scalar_tensor_tensor(
                h_scr[:], ln2t, 1.0, ln2t, ALU.bypass, ALU.mult,
                accum_out=s_st1[:, 0:1])), "red")
            wait("c", "red", n_r)
            A(lambda e: e.activation(s_st1[:, 1:2], s_st1[:, 0:1], AF.Ln,
                                     bias=s_erm[:], scale=1.0 / D))
            n_rs = inc("c", A(lambda e: e.activation(
                s_st1[:, 2:3], s_st1[:, 1:2], AF.Exp, scale=-0.5)), "act")
            wait("v", "act", n_rs)
            n_h1 = inc("v", V(lambda e, ln2t=ln2t: e.tensor_scalar(
                h_scr2[:], ln2t, s_st1[:, 2:3], None, ALU.mult)), "dve")
            wait("t", "dve", n_h1)
            for dc in range(NDC):
                bank = dc % 2
                wait("t", "dve", tplag[bank])
                n_tp = inc("t", T(lambda e, dc=dc, bank=bank: e.transpose(
                    pbb[4 + bank][:, 0:P], h_scr2[:, dc * P:(dc + 1) * P],
                    s_idn[:])), "pe")
                wait("v", "pe", n_tp)
                n_cp = inc("v", V(lambda e, dc=dc, tt=tt, bank=bank: e.tensor_scalar(
                    h_h1f[:, dc * OWN + tt * P:dc * OWN + (tt + 1) * P],
                    pbb[4 + bank][:, 0:P], s_n1[:, dc:dc + 1], None, ALU.mult)),
                    "dve")
                tplag[bank] = n_cp
        n_h1f = C["dve"].n

        wait("t", "dve", n_h1f)
        wst_free = [("dve", C["dve"].n), ("dve", C["dve"].n)]
        bk4e = [0, 0]
        for ec in range(NDC):
            sl = ec % 2
            kind, val = wst_free[sl]
            wait("g", kind, val)
            v_w = dma("g", h_wstA[sl].rearrange("p (c f) -> p c f", c=NDC),
                      d_wvf[ec].rearrange("c p f -> p c f"))
            wait("t", "dg", v_w)
            for th in range(2):
                wait("t", "dve", bk4e[th])
                for dc in range(NDC):
                    ins = T(lambda e, sl=sl, dc=dc, th=th: e.matmul(
                        pbh[th][:, 0:512], h_wstA[sl][:, dc * P:(dc + 1) * P],
                        h_h1f[:, dc * OWN + th * 512:dc * OWN + (th + 1) * 512],
                        start=(dc == 0), stop=(dc == NDC - 1)))
                n_mm = inc("t", ins, "pe")
                wait("v", "pe", n_mm)
                n_ev = inc("v", V(lambda e, ec=ec, th=th: e.tensor_scalar(
                    h_vfm[:, ec * OWN + th * 512:ec * OWN + (th + 1) * 512],
                    pbh[th][:, 0:512], s_bv[:, ec:ec + 1], None, ALU.add)), "dve")
                bk4e[th] = n_ev
            wst_free[sl] = ("dve", C["dve"].n)
        n_vfm = C["dve"].n

        wait("t", "dve", n_vfm)
        wcb_free = [("pe", 0), ("pe", 0)]
        bk4f = [0, 0]
        for jc in range(4):
            sl = jc % 2
            kind, val = wcb_free[sl]
            wait("g", kind, val)
            v_w = dma("g", h_wcb[sl].rearrange("p (c f) -> p c f", c=NDC),
                      d_wof[jc].rearrange("c p f -> p c f"))
            wait("t", "dg", v_w)
            for tt in range(8):
                wait("t", "dve", bk4f[tt % 2])
                for ec in range(NDC):
                    ins = T(lambda e, sl=sl, ec=ec, tt=tt: e.matmul(
                        pbh[tt % 2][:, 0:512],
                        h_vfm[:, ec * OWN + tt * P:ec * OWN + (tt + 1) * P],
                        h_wcb[sl][:, ec * 512:(ec + 1) * 512],
                        start=(ec == 0), stop=(ec == NDC - 1)))
                n_mm = inc("t", ins, "pe")
                wait("v", "pe", n_mm)
                n_ev = inc("v", V(lambda e, tt=tt, jc=jc: e.scalar_tensor_tensor(
                    h_h[:, tt * D + jc * 512:tt * D + (jc + 1) * 512],
                    pbh[tt % 2][:, 0:512], 1.0,
                    h_ln2[:, tt * D + jc * 512:tt * D + (jc + 1) * 512],
                    ALU.bypass, ALU.add)), "dve")
                bk4f[tt % 2] = n_ev
            wcb_free[sl] = ("pe", C["pe"].n)

        for tt in range(8):
            hsl = h_h[:, tt * D:(tt + 1) * D]
            n_r = inc("v", V(lambda e, hsl=hsl: e.scalar_tensor_tensor(
                h_scr[:], hsl, 1.0, hsl, ALU.bypass, ALU.mult,
                accum_out=s_st1[:, 0:1])), "red")
            wait("c", "red", n_r)
            A(lambda e: e.activation(s_st1[:, 1:2], s_st1[:, 0:1], AF.Ln,
                                     bias=s_erm[:], scale=1.0 / D))
            n_rs = inc("c", A(lambda e, tt=tt: e.activation(
                s_rstd2[:, tt:tt + 1], s_st1[:, 1:2], AF.Exp, scale=-0.5)), "act")
            wait("s", "red", n_r)
            dma("s", d_hsp[tt * P:(tt + 1) * P, :], hsl, "dsy")
        n_rstd2 = C["act"].n
        v_hsp = C["dsy"].n

        wait("v", "act", n_rstd2)
        wait("g", "dsy", v_hsp)
        wait("g", "dve", C["dve"].n)
        gu_free = [("dve", C["dve"].n), ("dve", C["dve"].n)]
        dw_free = [("pe", C["pe"].n), ("pe", C["pe"].n)]
        bk4h = [0, 0]
        bkgu = [0, 0]
        ss_free = [0, 0]
        l2p_free = [("dve", 0), ("dve", 0)]
        hp_free = [("dve", 0), ("dve", 0)]
        tplag2 = [0, 0]
        for sl4 in range(2):
            for ti in range(4):
                tt = sl4 * 4 + ti
                wait("s", "dve", C["dve"].n)
                wait("s", "dsy", v_hsp)
                v_h = dma("s", h_scr2[:], d_hsp[tt * P:(tt + 1) * P, :], "dsy")
                wait("v", "dsy", v_h)
                n_h2 = inc("v", V(lambda e, tt=tt: e.tensor_scalar(
                    h_scr2[:], h_scr2[:], s_rstd2[:, tt:tt + 1],
                    None, ALU.mult)), "dve")
                wait("t", "dve", n_h2)
                for dc in range(NDC):
                    bank = dc % 2
                    wait("t", "dve", tplag2[bank])
                    n_tp = inc("t", T(lambda e, dc=dc, bank=bank: e.transpose(
                        pbb[4 + bank][:, 0:P], h_scr2[:, dc * P:(dc + 1) * P],
                        s_idn[:])), "pe")
                    wait("v", "pe", n_tp)
                    n_cp = inc("v", V(lambda e, dc=dc, ti=ti, bank=bank:
                                      e.tensor_scalar(
                                          h_h2f[:, dc * 512 + ti * P:
                                                dc * 512 + (ti + 1) * P],
                                          pbb[4 + bank][:, 0:P], s_n2[:, dc:dc + 1],
                                          None, ALU.mult)), "dve")
                    tplag2[bank] = n_cp
            n_h2f = C["dve"].n
            wait("t", "dve", n_h2f)
            for fc in range(NFC):
                slg = fc % 2
                kind, val = gu_free[slg]
                wait("g", kind, val)
                dma("g", h_wstB[slg].rearrange("p (c f) -> p c f", c=NDC),
                    d_gate[fc].rearrange("c p f -> p c f"))
                v_w = dma("g", h_wstB[2 + slg].rearrange("p (c f) -> p c f", c=NDC),
                          d_up[fc].rearrange("c p f -> p c f"))
                wait("t", "dg", v_w)
                wait("t", "act", bkgu[0])
                wait("t", "dve", bkgu[1])
                for dc in range(NDC):
                    T(lambda e, slg=slg, dc=dc: e.matmul(
                        pbh[4][:, 0:512], h_wstB[slg][:, dc * P:(dc + 1) * P],
                        h_h2f[:, dc * 512:(dc + 1) * 512],
                        start=(dc == 0), stop=(dc == NDC - 1)))
                for dc in range(NDC):
                    ins = T(lambda e, slg=slg, dc=dc: e.matmul(
                        pbh[5][:, 0:512], h_wstB[2 + slg][:, dc * P:(dc + 1) * P],
                        h_h2f[:, dc * 512:(dc + 1) * 512],
                        start=(dc == 0), stop=(dc == NDC - 1)))
                n_mm = inc("t", ins, "pe")
                wait("c", "pe", n_mm)
                n_si = inc("c", A(lambda e: e.activation(
                    h_scr2[:, 0:512], pbh[4][:, 0:512], AF.Silu)), "act")
                wait("v", "act", n_si)
                n_ga = inc("v", V(lambda e, fc=fc: e.tensor_tensor(
                    h_gact[:, fc * 512:(fc + 1) * 512], h_scr2[:, 0:512],
                    pbh[5][:, 0:512], ALU.mult)), "dve")
                gu_free[slg] = ("dve", n_ga)
                bkgu[0] = n_si
                bkgu[1] = n_ga
            n_gact = C["dve"].n
            wait("t", "dve", n_gact)
            wait("s", "dsy", v_hsp)
            for jc in range(8):
                sld = jc % 2
                kind, val = dw_free[sld]
                wait("g", kind, val)
                v_w = dma("g", h_dwn[sld].rearrange("p (c f) -> p c f", c=NFC),
                          d_down[jc].rearrange("c p f -> p c f"))
                kind, val = l2p_free[sld]
                wait("s", kind, val)
                v_l2 = dma("s", h_l2p[sld].rearrange("p (t d) -> p t d", t=4),
                           d_ln2sp[sl4 * 512:sl4 * 512 + 512, jc * 256:(jc + 1) * 256]
                           .rearrange("(t p) d -> p t d", p=P), "dsy")
                kind, val = hp_free[sld]
                wait("s", kind, val)
                v_hp = dma("s", h_hp[sld].rearrange("p (t d) -> p t d", t=4),
                           d_hsp[sl4 * 512:sl4 * 512 + 512, jc * 256:(jc + 1) * 256]
                           .rearrange("(t p) d -> p t d", p=P), "dsy")
                wait("t", "dg", v_w)
                wait("v", "dsy", v_hp)
                for ti in range(4):
                    tt = sl4 * 4 + ti
                    wait("t", "dve", bk4h[ti % 2])
                    for fc in range(NFC):
                        ins = T(lambda e, sld=sld, fc=fc, ti=ti: e.matmul(
                            pbh[ti % 2][:, 0:256],
                            h_gact[:, fc * 512 + ti * P:fc * 512 + (ti + 1) * P],
                            h_dwn[sld][:, fc * 256:(fc + 1) * 256],
                            start=(fc == 0), stop=(fc == NFC - 1)))
                    n_mm = inc("t", ins, "pe")
                    wait("v", "pe", n_mm)
                    V(lambda e, ti=ti, sld=sld: e.tensor_sub(
                        h_scr[:, 0:256], h_hp[sld][:, ti * 256:(ti + 1) * 256],
                        h_l2p[sld][:, ti * 256:(ti + 1) * 256]))
                    V(lambda e, ti=ti: e.tensor_add(
                        h_scr[:, 0:256], h_scr[:, 0:256], pbh[ti % 2][:, 0:256]))
                    off = (ti % 2) * 256
                    wait("v", "dsy", ss_free[ti % 2])
                    n_sel = inc("v", V(lambda e, ti=ti, tt=tt, sld=sld, off=off:
                                       e.scalar_tensor_tensor(
                                           h_scr2[:, off:off + 256],
                                           h_scr[:, 0:256],
                                           s_rmask[:, tt:tt + 1],
                                           h_l2p[sld][:, ti * 256:(ti + 1) * 256],
                                           ALU.mult, ALU.add)), "dve")
                    bk4h[ti % 2] = n_sel
                    wait("s", "dve", n_sel)
                    v_ss = dma("s", d_selsp[sl4 * 512 + ti * P:
                                            sl4 * 512 + (ti + 1) * P,
                                            jc * 256:(jc + 1) * 256],
                               h_scr2[:, off:off + 256], "dsy")
                    ss_free[ti % 2] = v_ss
                dw_free[sld] = ("pe", C["pe"].n)
                l2p_free[sld] = ("dve", n_sel)
                hp_free[sld] = ("dve", n_sel)

        wait("s", "dsy", C["dsy"].n)
        x1_free = [("dve", 0), ("dve", 0)]
        sel_done = 0
        for tq in range(2):
            wait("s", "dve", sel_done)
            v_sel = dma("s", h_self.rearrange("p (t d) -> p t d", t=4),
                        d_selsp[tq * 512:tq * 512 + 512, :]
                        .rearrange("(t p) d -> p t d", p=P), "dsy")
            for ti in range(4):
                tt = tq * 4 + ti
                slx = ti % 2
                kind, val = x1_free[slx]
                wait("s", kind, val)
                v_x1 = dma("s", h_x1f[slx][:], d_x1sp[tt * P:(tt + 1) * P, :], "dsy")
                selt = h_self[:, ti * D:(ti + 1) * D]
                wait("v", "dsy", v_sel)
                n_r = inc("v", V(lambda e, selt=selt: e.tensor_reduce(
                    s_st1[:, 0:1], selt, AX.X, ALU.add)), "red")
                wait("v", "red", n_r)
                n_mu = inc("v", V(lambda e: e.tensor_scalar(
                    s_st1[:, 0:1], s_st1[:, 0:1], 1.0 / D, None, ALU.mult)), "red")
                wait("v", "red", n_mu)
                V(lambda e, selt=selt: e.tensor_scalar(h_scr[:], selt,
                                                       s_st1[:, 0:1], None,
                                                       ALU.subtract))
                n_r2 = inc("v", V(lambda e: e.scalar_tensor_tensor(
                    h_scr[:], h_scr[:], 1.0, h_scr[:], ALU.bypass, ALU.mult,
                    accum_out=s_st1[:, 1:2])), "red")
                wait("c", "red", n_r2)
                A(lambda e: e.activation(s_st1[:, 2:3], s_st1[:, 1:2], AF.Ln,
                                         bias=s_eln[:], scale=1.0 / D))
                n_rs = inc("c", A(lambda e: e.activation(
                    s_st1[:, 3:4], s_st1[:, 2:3], AF.Exp, scale=-0.5)), "act")
                wait("v", "act", n_rs)
                V(lambda e, selt=selt: e.tensor_scalar(
                    h_scr[:], selt, s_st1[:, 0:1], s_st1[:, 3:4], ALU.subtract,
                    ALU.mult))
                wait("v", "dsy", v_x1)
                n_out = inc("v", V(lambda e, slx=slx: e.tensor_add(
                    h_x1f[slx][:], h_x1f[slx][:], h_scr[:])), "dve")
                x1_free[slx] = ("dve", n_out)
                sel_done = n_out
                wait("s", "dve", n_out)
                dma("s", d_y[tt * P:(tt + 1) * P, :], h_x1f[slx][:], "out")
        wait("s", "out", C["out"].n)

    with nc.Block() as block:
        block.gpsimd(lambda e: emit("g", e))
        block.sync(lambda e: emit("s", e))
        block.scalar(lambda e: emit("c", e))
        block.tensor(lambda e: emit("t", e))
        block.vector(lambda e: emit("v", e))
    stack.close()
    return nc


_NC_CACHE = None


def kernel(**inputs):
    global _NC_CACHE
    shared, per_core = _prep(inputs)
    if _NC_CACHE is None:
        _NC_CACHE = build()
    in_maps = []
    for c in range(NCORE):
        m = dict(shared)
        m.update(per_core[c])
        in_maps.append(m)
    res = run_bass_kernel_spmd(_NC_CACHE, in_maps, core_ids=list(range(NCORE)))
    y = np.concatenate([res.results[c]["y"] for c in range(NCORE)], axis=0)
    return y.reshape(NB, SEQ, D)
